# version 02
# baseline (speedup 1.0000x reference)
import math
import contextlib
import numpy as np
import ml_dtypes
import concourse.bass as bass
import concourse.mybir as mybir
from concourse.bass_utils import run_bass_kernel_spmd

F32 = mybir.dt.float32
BF16 = mybir.dt.bfloat16
AF = mybir.ActivationFunctionType
ALU = mybir.AluOpType
AX = mybir.AxisListType

NCORES = 8
D = 1024
T = 2048
NC = D // 128
NT = T // 512

ENGINES = ("t", "v", "s", "p")
QUEUES = ("qs", "qs1", "qs2", "qs3", "qo", "qw0", "qw1", "qw2", "qw3",
          "qm", "qa")
Q_ISSUER = {"qs": "sync", "qs1": "sync", "qs2": "sync", "qs3": "sync",
            "qo": "sync", "qw0": "p", "qw1": "p", "qw2": "p", "qw3": "p",
            "qm": "sync", "qa": "s"}


def ceil_div(a, b):
    return -(-a // b)


class TB:

    def __init__(self, h, key):
        self.h = h
        self.key = key

    def __getitem__(self, idx):
        return self.h[idx]

    @property
    def ap(self):
        return self.h[:]


def R(*bufs):
    return [b.key for b in bufs]


class Ctx:
    def __init__(self, nc):
        self.nc = nc
        self.ops = {"t": [], "v": [], "s": [], "p": [], "sync": []}
        self.cnt = {e: 0 for e in ENGINES}
        self.cnt.update({q: 0 for q in QUEUES})
        self.sems = {}
        self.last_write = {}
        self.readers = {}
        self.pending = {e: [] for e in ENGINES}
        self.waited = {}
        self._names = set()
        self.prefix = ""

    def make_sems(self, stack):
        for name in ENGINES + QUEUES:
            self.sems[name] = stack.enter_context(self.nc.semaphore("sem_" + name))

    def sbuf(self, stack, name, shape, dtype):
        name = self.prefix + name
        assert name not in self._names
        self._names.add(name)
        h = stack.enter_context(self.nc.sbuf_tensor(name, shape, dtype))
        return TB(h, name)

    def sbuf_at(self, name, shape, dtype, at, delta=0):
        name = self.prefix + name
        assert name not in self._names
        self._names.add(name)
        addr = self.nc.lookup_mloc(at.h if isinstance(at, TB) else at).addr + delta
        h = self.nc.alloc_sbuf_tensor_at(name, shape, dtype, offset=addr)
        return TB(h, name)

    def psum(self, stack, name, shape, dtype=F32):
        name = self.prefix + name
        assert name not in self._names
        self._names.add(name)
        h = stack.enter_context(self.nc.psum_tensor(name, shape, dtype))
        return TB(h, name)

    _fence = None

    def _need_waits(self, stream, reads, writes):
        waits = {}

        def _tgt(src, c):
            return c

        def _add(src, c):
            if src == stream == "t":
                return
            if c is None:
                assert src == stream == "t", \
                    f"pending-count dependency from {src} seen by {stream}"
                return
            waits[src] = max(waits.get(src, 0), _tgt(src, c))

        for k in reads:
            lw = self.last_write.get(k)
            if lw is not None:
                _add(*lw)
        for k in writes:
            lw = self.last_write.get(k)
            if lw is not None:
                _add(*lw)
            for src, c in self.readers.get(k, {}).items():
                _add(src, c)
        if self._fence is not None:
            done = self._fence_done.setdefault(stream, False)
            if not done:
                for src, val in self._fence.items():
                    if src == stream == "t":
                        pass
                    waits[src] = max(waits.get(src, 0), val)
                self._fence_done[stream] = True
        out = []
        for src, val in waits.items():
            seen = self.waited.get((stream, src), 0)
            if val > seen:
                self.waited[(stream, src)] = val
                out.append((self.sems[src], val))
        return out

    def op(self, e, build, reads=(), writes=(), inc=True):
        assert e in ENGINES
        reads = [b.key if isinstance(b, TB) else b for b in reads]
        writes = [b.key if isinstance(b, TB) else b for b in writes]
        waits = self._need_waits(e, reads, writes)
        if inc:
            self.cnt[e] += 1
            cnt = self.cnt[e]
        else:
            cnt = None
        sem = self.sems[e]

        def run(eng, waits=waits, build=build, inc=inc, sem=sem):
            for s, v in waits:
                eng.wait_ge(s, v)
            ins = build(eng)
            if inc:
                ins.then_inc(sem, 1)

        self.ops[e].append(run)
        if cnt is None:
            for k in reads:
                self.pending[e].append(("r", k))
            for k in writes:
                self.pending[e].append(("w", k))
                self.last_write[k] = (e, None)
                self.readers[k] = {}
        else:
            for kind, k in self.pending[e]:
                if kind == "r":
                    self.readers.setdefault(k, {})[e] = cnt
                else:
                    if self.last_write.get(k, (None, 0))[0] == e and \
                       self.last_write[k][1] is None:
                        self.last_write[k] = (e, cnt)
            self.pending[e] = []
            for k in reads:
                self.readers.setdefault(k, {})[e] = cnt
            for k in writes:
                self.last_write[k] = (e, cnt)
                self.readers[k] = {}
        return cnt

    def dma(self, q, out, in_, reads=(), writes=(), **kw):
        assert q in QUEUES
        reads = [b.key if isinstance(b, TB) else b for b in reads]
        writes = [b.key if isinstance(b, TB) else b for b in writes]
        issuer = Q_ISSUER[q]
        waits = self._need_waits(issuer, reads, writes)
        prev = self.cnt[q]
        self.cnt[q] += 16
        cnt = self.cnt[q]
        sem = self.sems[q]

        def run(eng, waits=waits, out=out, in_=in_, kw=kw, sem=sem, prev=prev):
            if prev > 0:
                eng.wait_ge(sem, prev)
            for s, v in waits:
                eng.wait_ge(s, v)
            eng.dma_start(out=out, in_=in_, **kw).then_inc(sem, 16)

        self.ops[issuer].append(run)
        for k in reads:
            self.readers.setdefault(k, {})[q] = cnt
        for k in writes:
            self.last_write[k] = (q, cnt)
            self.readers[k] = {}
        return cnt

    def fence(self):
        self._fence = {s: self.cnt[s] for s in ENGINES + QUEUES if self.cnt[s] > 0}
        self._fence_done = {}

    def finish(self):
        targets = []
        for src in ENGINES + QUEUES:
            v = self.cnt[src]
            if v > 0:
                targets.append((self.sems[src], v))

        def run(eng, targets=targets):
            for s, v in targets:
                eng.wait_ge(s, v)

        self.ops["sync"].append(run)

    def build(self, block):
        for name, attr in (("t", "tensor"), ("v", "vector"), ("s", "scalar"),
                           ("p", "gpsimd"), ("sync", "sync")):
            ops = self.ops[name]
            if not ops:
                continue
            getattr(block, attr)(lambda eng, ops=ops: [f(eng) for f in ops])

    def mm(self, psum_ap, psum_buf, parts, accumulate=False, close=True):
        n = len(parts)
        for i, (lhsT, rhs, rks) in enumerate(parts):
            start = (i == 0) and not accumulate
            stop = (i == n - 1) and close
            last = (i == n - 1)
            self.op(
                "t",
                lambda eng, lhsT=lhsT, rhs=rhs, start=start, stop=stop:
                    eng.matmul(psum_ap, lhsT, rhs, start=start, stop=stop),
                reads=rks,
                writes=[psum_buf] if (stop or i == 0) else [],
                inc=last,
            )



class StdBuf:
    def __init__(self, ctx, stack, Tl=T, n_tmp=3, n_wslab=2, wslab_cols=1024,
                 psum_f32=6, psum_bf16=1):
        self.T = Tl
        self.x = [ctx.sbuf(stack, f"x{c}", [128, Tl], F32) for c in range(NC)]
        self.xb = [ctx.sbuf(stack, f"xb{c}", [128, Tl], BF16) for c in range(NC)]
        self.tmp = [ctx.sbuf(stack, f"tmp{i}", [128, Tl], F32) for i in range(n_tmp)]
        self.mu = ctx.sbuf(stack, "mu", [128, Tl], F32)
        self.rstd = ctx.sbuf(stack, "rstd", [128, Tl], F32)
        self.rows = ctx.sbuf(stack, "rows", [1, Tl], F32)
        self.rows2 = ctx.sbuf(stack, "rows2", [1, Tl], F32)
        self.ones_b = ctx.sbuf(stack, "ones_b", [128, 1], BF16)
        self.ones_f = ctx.sbuf(stack, "ones_f", [128, 1], F32)
        self.onesrow = ctx.sbuf(stack, "onesrow", [1, 128], F32)
        self.vecs = ctx.sbuf(stack, "vecs", [128, 128], F32)
        self.nvec = 0
        self.consts = ctx.sbuf(stack, "consts", [128, 8], F32)
        self._cmap = {}
        self._ctx = ctx
        self.wslab = [ctx.sbuf(stack, f"wslab{i}", [128, wslab_cols], BF16)
                      for i in range(n_wslab)]
        self.ps = [ctx.psum(stack, f"ps{i}", [128, 512], F32) for i in range(psum_f32)]
        self.pb = [ctx.psum(stack, f"pb{i}", [128, 1024], BF16) for i in range(psum_bf16)]

    def vcol(self, col):
        return self.vecs[:, col:col + 1]

    def const_ap(self, value):
        value = float(value)
        if value not in self._cmap:
            col = len(self._cmap)
            assert col < 8
            self._cmap[value] = col
            self._ctx.op("v", lambda eng, col=col, value=value: eng.memset(
                self.consts[:, col:col + 1], value), writes=[self.consts])
        return self.consts[:, self._cmap[value]:self._cmap[value] + 1]

    def valloc(self, n):
        c0 = self.nvec
        self.nvec += n
        assert self.nvec <= 160
        return c0

    def vkey(self, col):
        return self.vecs.key


def emit_const_init(ctx, sb):
    ctx.op("v", lambda eng: eng.memset(sb.ones_b.ap, 1.0), writes=[sb.ones_b])
    ctx.op("v", lambda eng: eng.memset(sb.ones_f.ap, 1.0), writes=[sb.ones_f])
    ctx.op("v", lambda eng: eng.memset(sb.onesrow.ap, 1.0), writes=[sb.onesrow])


def load_vec(ctx, sb, dram_ap, length=D):
    nch = length // 128
    col0 = sb.valloc(nch)
    for j in range(nch):
        ctx.dma("qm", sb.vecs[:, col0 + j:col0 + j + 1],
                dram_ap[j * 128:(j + 1) * 128, None], writes=[sb.vecs])
    return col0


QS_RR = ("qs", "qs1", "qs2", "qs3")


def load_stream(ctx, sb, x_dram, dst=None):
    dst = dst or sb.x
    for c in range(NC):
        ctx.dma(QS_RR[c % 4], dst[c].ap, x_dram[c * 128:(c + 1) * 128, :],
                writes=[dst[c]])


def store_stream(ctx, sb, out_dram, src=None):
    src = src or sb.x
    for c in range(NC):
        ctx.dma("qo", out_dram[c * 128:(c + 1) * 128, :], src[c].ap, reads=[src[c]])


def plain_ld(xin, rd_key=None):
    rks = [rd_key] if rd_key else []

    def f(ctx, dst, c, col0=0, ncols=T, dcol0=None, q=None):
        d0 = col0 if dcol0 is None else dcol0
        ctx.dma(q or QS_RR[c % 4], dst.h[:, d0:d0 + ncols],
                xin[c * 128:(c + 1) * 128, col0:col0 + ncols],
                reads=rks, writes=[dst])
    return f


def plain_st(xout, wr_key=None):
    wks = [wr_key] if wr_key else []

    def f(ctx, src, c, col0=0, ncols=T, scol0=None, q="qo"):
        s0 = col0 if scol0 is None else scol0
        ctx.dma(q, xout[c * 128:(c + 1) * 128, col0:col0 + ncols],
                src.h[:, s0:s0 + ncols], reads=[src], writes=wks)
    return f


def a2a_ld(buf, rd_key):
    def f(ctx, dst, c, col0=0, ncols=T, dcol0=None, q=None):
        d0 = col0 if dcol0 is None else dcol0
        assert col0 % 256 == 0 and ncols % 256 == 0
        for k in range(ncols // 256):
            s = col0 // 256 + k
            ctx.dma(q or QS_RR[c % 4], dst.h[:, d0 + k * 256:d0 + (k + 1) * 256],
                    buf[s * 1024 + c * 128:s * 1024 + (c + 1) * 128, :],
                    reads=[rd_key], writes=[dst])
    return f


def a2a_st(buf, wr_key):
    def f(ctx, src, c, col0=0, ncols=T, scol0=None, q="qo"):
        s0 = col0 if scol0 is None else scol0
        assert col0 % 256 == 0 and ncols % 256 == 0
        for k in range(ncols // 256):
            j = col0 // 256 + k
            ctx.dma(q, buf[j * 1024 + c * 128:j * 1024 + (c + 1) * 128, :],
                    src.h[:, s0 + k * 256:s0 + (k + 1) * 256],
                    reads=[src], writes=[wr_key])
    return f


def emit_a2a(ctx, cc_sem, in_ap, out_ap, in_key, out_key, tiny):
    from concourse.aot_env import allow_ici_all_to_all

    def coll(eng):
        with allow_ici_all_to_all():
            return eng.collective_compute(
                "AllToAll", ALU.bypass, replica_groups=[list(range(NCORES))],
                ins=[in_ap.opt()], outs=[out_ap.opt()]).then_inc(cc_sem)
    ctx.op("p", coll, reads=[in_key], inc=False)

    def marker(eng):
        eng.wait_ge(cc_sem, 1)
        return eng.memset(tiny.ap, 0.0)
    ctx.op("p", marker, writes=[out_key, tiny], inc=True)


def cast_stream(ctx, sb, src_list, dst_list, shift_chunks=()):
    for c in range(NC):
        src, dst = src_list[c], dst_list[c]
        if c in shift_chunks:
            w = dst.h.shape[1]
            ctx.op("v", lambda eng, s=src, d=dst, w=w: eng.tensor_copy(
                d[:, 1:w], s[:, 0:w - 1]),
                reads=[src], writes=[dst])
            ctx.op("v", lambda eng, d=dst: eng.memset(d[:, 0:1], 0.0), writes=[dst])
        else:
            ctx.op("v", lambda eng, s=src, d=dst: eng.tensor_copy(d.ap, s.ap),
                   reads=[src], writes=[dst])


def stream_w_slab(ctx, sb, w_dram, buf_idx, col0, ncols, kchunks=NC, krow0=0):
    slab = sb.wslab[buf_idx]
    for k in range(kchunks):
        ctx.dma("qw%d" % ((buf_idx % 2) * 2 + k % 2), slab[:, k * ncols:(k + 1) * ncols],
                w_dram[krow0 + k * 128:krow0 + (k + 1) * 128, col0:col0 + ncols],
                writes=[slab])
    return slab


def proj_fm(ctx, sb, w_dram, dout, xbufs, out_cb, kchunks=NC, pslots=(0, 1, 2, 3),
            Tl=None, wbufs=(0, 1)):
    Tl = Tl or sb.T
    ntt = Tl // 512
    noc = dout // 128
    for oc in range(noc):
        slab = stream_w_slab(ctx, sb, w_dram, wbufs[oc % len(wbufs)], oc * 128, 128,
                             kchunks)
        for tt in range(ntt):
            ps = sb.ps[pslots[(oc * ntt + tt) % len(pslots)]]
            parts = [(slab[:, k * 128:(k + 1) * 128],
                      xbufs[k][:, tt * 512:(tt + 1) * 512],
                      [slab, xbufs[k]]) for k in range(kchunks)]
            ctx.mm(ps.ap, ps, parts)
            out_cb(oc, tt, ps)


def emit_ln_fm(ctx, sb, src_list, dst_list, eps, gamma_col=None, beta_col=None,
               src_bf16=False, Tl=None, nch=NC, dim=D, src_off=0, dst_off=0):
    Tl = Tl or sb.T
    ntt = Tl // 512
    sq = sb.tmp[-1]
    ones = sb.ones_b if src_bf16 else sb.ones_f
    stat = sb.ps[4]
    bc = sb.ps[5]
    for tt in range(ntt):
        sl = slice(src_off + tt * 512, src_off + (tt + 1) * 512)
        lo = slice(tt * 512, (tt + 1) * 512)
        parts = [(ones.ap, src_list[c][:, sl], [ones, src_list[c]])
                 for c in range(nch)]
        ctx.mm(stat[0:1, :], stat, parts)
        for c in range(nch):
            ctx.op("s", lambda eng, c=c, sl=sl: eng.activation(
                sq[:, 0:512], src_list[c][:, sl], AF.Square),
                reads=[src_list[c]], writes=[sq])
            ctx.mm(stat[32:33, :], stat,
                   [(sb.ones_f.ap, sq[:, 0:512], [sb.ones_f, sq])],
                   accumulate=(c > 0), close=(c == nch - 1))
        ctx.op("v", lambda eng, lo=lo: eng.tensor_copy(sb.rows[0:1, lo], stat[0:1, :]),
               reads=[stat], writes=[sb.rows])
        ctx.op("v", lambda eng, lo=lo: eng.tensor_copy(sb.rows2[0:1, lo], stat[32:33, :]),
               reads=[stat], writes=[sb.rows2])
        ctx.mm(bc.ap, bc, [(sb.onesrow.ap, sb.rows[0:1, lo], [sb.onesrow, sb.rows])])
        ctx.op("v", lambda eng, lo=lo: eng.tensor_scalar(
            sb.mu[:, lo], bc.ap, 1.0 / dim, None, ALU.mult),
            reads=[bc], writes=[sb.mu])
        ctx.mm(bc.ap, bc, [(sb.onesrow.ap, sb.rows2[0:1, lo], [sb.onesrow, sb.rows2])])
        ctx.op("v", lambda eng, lo=lo: eng.tensor_scalar(
            sb.rstd[:, lo], bc.ap, 1.0 / dim, None, ALU.mult),
            reads=[bc], writes=[sb.rstd])
    ctx.op("v", lambda eng: eng.tensor_mul(sq[:, 0:Tl], sb.mu[:, 0:Tl], sb.mu[:, 0:Tl]),
           reads=[sb.mu], writes=[sq])
    ctx.op("v", lambda eng: eng.tensor_sub(sb.rstd[:, 0:Tl], sb.rstd[:, 0:Tl], sq[:, 0:Tl]),
           reads=[sb.rstd, sq], writes=[sb.rstd])
    epsap = sb.const_ap(eps)
    ctx.op("s", lambda eng: eng.activation(sb.rstd[:, 0:Tl], sb.rstd[:, 0:Tl], AF.Ln,
                                           bias=epsap),
           reads=[sb.rstd, sb.consts], writes=[sb.rstd])
    ctx.op("s", lambda eng: eng.activation(sb.rstd[:, 0:Tl], sb.rstd[:, 0:Tl], AF.Exp,
                                           scale=-0.5),
           reads=[sb.rstd], writes=[sb.rstd])
    for c in range(nch):
        src, dst = src_list[c], dst_list[c]
        ssl = slice(src_off, src_off + Tl)
        dsl = slice(dst_off, dst_off + Tl)
        ctx.op("v", lambda eng, src=src, ssl=ssl: eng.tensor_sub(
            sq[:, 0:Tl], src[:, ssl], sb.mu[:, 0:Tl]),
            reads=[src, sb.mu], writes=[sq])
        if gamma_col is not None:
            ctx.op("v", lambda eng: eng.tensor_mul(sq[:, 0:Tl], sq[:, 0:Tl],
                                                   sb.rstd[:, 0:Tl]),
                   reads=[sq, sb.rstd], writes=[sq])
            ctx.op("v", lambda eng, dst=dst, c=c, dsl=dsl: eng.tensor_scalar(
                dst[:, dsl], sq[:, 0:Tl], sb.vcol(gamma_col + c),
                sb.vcol(beta_col + c), ALU.mult, ALU.add),
                reads=[sq, sb.vecs], writes=[dst])
        else:
            ctx.op("v", lambda eng, dst=dst, dsl=dsl: eng.tensor_mul(
                dst[:, dsl], sq[:, 0:Tl], sb.rstd[:, 0:Tl]),
                reads=[sq, sb.rstd], writes=[dst])



L0A_VECS = ["bg", "bi", "A", "Bp", "Cp", "Dp", "gamma", "beta",
            "bfg", "bfu", "bfd", "bo"]


def declare_l0_attn(nc, p=""):
    aps = {}
    for name in ["Wg", "Wi", "Wfg", "Wfu", "Wfd", "Wo"]:
        aps[name] = nc.dram_tensor(p + name, [D, D], F32, kind="ExternalInput").ap()
    for name in L0A_VECS:
        aps[name] = nc.dram_tensor(p + name, [D], F32, kind="ExternalInput").ap()
    return aps


def emit_l0_attn(nc, ctx, stack, aps, xld, xst):
    wd = aps
    vin = {n: aps[n] for n in L0A_VECS}
    sb = StdBuf(ctx, stack)
    y = [ctx.sbuf(stack, f"y{c}", [128, T], BF16) for c in range(NC)]
    if True:
            emit_const_init(ctx, sb)
            cols = {n: load_vec(ctx, sb, ap) for n, ap in vin.items()}
            for c in range(NC):
                xld(ctx, sb.x[c], c, q=QS_RR[c % 4])
            cast_stream(ctx, sb, sb.x, sb.xb)
            cb0 = sb.valloc(NC)
            for c in range(NC):
                ctx.op("v", lambda eng, c=c: eng.tensor_mul(
                    sb.vcol(cb0 + c), sb.vcol(cols["Cp"] + c), sb.vcol(cols["Bp"] + c)),
                    reads=[sb.vecs], writes=[sb.vecs])

            gate, xi, tscr = sb.tmp[0], sb.tmp[1], sb.tmp[2]

            def dual_proj(srcbufs, wa, ba, wb, bb, fa=AF.Silu, fb=AF.Identity):
                for oc in range(NC):
                    for (wkey, dst, bcol, func, wbuf, psl) in (
                            (wa, gate, ba, fa, 0, (0, 1)),
                            (wb, xi, bb, fb, 1, (2, 3))):
                        slab = stream_w_slab(ctx, sb, wd[wkey], wbuf, oc * 128, 128)
                        for tt in range(NT):
                            ps = sb.ps[psl[tt % 2]]
                            parts = [(slab[:, k * 128:(k + 1) * 128],
                                      srcbufs[k][:, tt * 512:(tt + 1) * 512],
                                      [slab, srcbufs[k]]) for k in range(NC)]
                            ctx.mm(ps.ap, ps, parts)
                            ctx.op("s", lambda eng, tt=tt, ps=ps, dst=dst, bcol=bcol,
                                   func=func, oc=oc: eng.activation(
                                       dst[:, tt * 512:(tt + 1) * 512], ps.ap, func,
                                       bias=sb.vcol(bcol + oc)),
                                   reads=[ps, sb.vecs], writes=[dst])
                    yield oc

            for oc in dual_proj(sb.xb, "Wg", cols["bg"], "Wi", cols["bi"]):
                ctx.op("v", lambda eng: eng.tensor_mul(xi.ap, gate.ap, xi.ap),
                       reads=[gate, xi], writes=[xi])
                ctx.op("v", lambda eng, oc=oc: eng.tensor_tensor_scan(
                    gate.ap, sb.vcol(cols["A"] + oc).broadcast_to((128, T)), xi.ap,
                    0.0, ALU.mult, ALU.add),
                    reads=[xi, sb.vecs], writes=[gate])
                ctx.op("v", lambda eng, oc=oc: eng.tensor_scalar(
                    tscr.ap, xi.ap, sb.vcol(cols["Dp"] + oc), None, ALU.mult),
                    reads=[xi, sb.vecs], writes=[tscr])
                ctx.op("v", lambda eng, oc=oc: eng.scalar_tensor_tensor(
                    y[oc].ap, gate.ap, sb.vcol(cb0 + oc), tscr.ap, ALU.mult, ALU.add),
                    reads=[gate, sb.vecs, tscr], writes=[y[oc]])

            emit_ln_fm(ctx, sb, y, sb.xb, 1e-3, gamma_col=cols["gamma"],
                       beta_col=cols["beta"], src_bf16=True)
            for oc in dual_proj(sb.xb, "Wfg", cols["bfg"], "Wfu", cols["bfu"]):
                ctx.op("v", lambda eng, oc=oc: eng.tensor_mul(y[oc].ap, gate.ap, xi.ap),
                       reads=[gate, xi], writes=[y[oc]])
            def evac_to_xb(oc, tt, ps):
                ctx.op("s", lambda eng, oc=oc, tt=tt, ps=ps: eng.activation(
                    sb.xb[oc][:, tt * 512:(tt + 1) * 512], ps.ap, AF.Identity,
                    bias=sb.vcol(cols["bfd"] + oc)),
                    reads=[ps, sb.vecs], writes=[sb.xb[oc]])
            proj_fm(ctx, sb, wd["Wfd"], D, y, evac_to_xb)
            def evac_res(oc, tt, ps):
                sl = slice(tt * 512, (tt + 1) * 512)
                ctx.op("v", lambda eng, oc=oc, sl=sl, ps=ps: eng.tensor_scalar(
                    tscr[:, sl], ps.ap, sb.vcol(cols["bo"] + oc), None, ALU.add),
                    reads=[ps, sb.vecs], writes=[tscr])
                ctx.op("v", lambda eng, oc=oc, sl=sl: eng.tensor_add(
                    sb.x[oc][:, sl], sb.x[oc][:, sl], tscr[:, sl]),
                    reads=[tscr, sb.x[oc]], writes=[sb.x[oc]])
            proj_fm(ctx, sb, wd["Wo"], D, sb.xb, evac_res)
            emit_ln_fm(ctx, sb, sb.x, sb.x, 1e-5)
            for c in range(NC):
                xst(ctx, sb.x[c], c)


def build_l0_attn():
    nc = bass.Bass("TRN2", target_bir_lowering=False)
    aps = declare_l0_attn(nc)
    xin = nc.dram_tensor("xin", [D, T], F32, kind="ExternalInput").ap()
    xout = nc.dram_tensor("xout", [D, T], F32, kind="ExternalOutput").ap()
    with contextlib.ExitStack() as stack:
        ctx = Ctx(nc)
        ctx.make_sems(stack)
        with nc.Block() as block:
            emit_l0_attn(nc, ctx, stack, aps, plain_ld(xin), plain_st(xout))
            ctx.finish()
            ctx.build(block)
    return nc




RND = float(1.5 * 2 ** 23)


def emit_quantize_weight(ctx, sb, w_dram, wq_dram, rows, cols, wf, racc, rtmp, qs_col):
    nch = rows // 128
    cw = min(cols, 2048)
    ncs = cols // cw
    assert cols % cw == 0
    ctx.op("v", lambda eng: eng.memset(racc.ap, 0.0), writes=[racc])
    for r in range(nch):
        for s in range(ncs):
            buf = wf[(r * ncs + s) % 2]
            ctx.dma("qw%d" % ((r * ncs + s) % 2), buf[:, 0:cw],
                    w_dram[r * 128:(r + 1) * 128, s * cw:(s + 1) * cw],
                    writes=[buf])
            ctx.op("v", lambda eng, buf=buf: eng.reduce_sum(
                rtmp.ap, buf[:, 0:cw], AX.X, apply_absolute_value=True),
                reads=[buf], writes=[rtmp])
            ctx.op("v", lambda eng: eng.tensor_add(racc.ap, racc.ap, rtmp.ap),
                   reads=[rtmp, racc], writes=[racc])
    stat = sb.ps[4]
    ctx.mm(stat[0:1, 0:1], stat, [(sb.ones_f.ap, racc.ap, [sb.ones_f, racc])])
    ctx.op("v", lambda eng: eng.tensor_scalar(
        sb.rows[0:1, 0:1], stat[0:1, 0:1], 1.0 / (rows * cols), 1e-5, ALU.mult, ALU.add),
        reads=[stat], writes=[sb.rows])
    ctx.op("v", lambda eng: eng.reciprocal(sb.rows[0:1, 1:2], sb.rows[0:1, 0:1]),
           reads=[sb.rows], writes=[sb.rows])
    bc = sb.ps[5]
    ctx.mm(bc[:, 0:1], bc, [(sb.onesrow.ap, sb.rows[0:1, 1:2], [sb.onesrow, sb.rows])])
    ctx.op("v", lambda eng: eng.tensor_copy(sb.vcol(qs_col), bc[:, 0:1]),
           reads=[bc], writes=[sb.vecs])
    qb = sb.wslab[0]
    for r in range(nch):
        for s in range(ncs):
            buf = wf[(r * ncs + s) % 2]
            ctx.dma("qw%d" % ((r * ncs + s) % 2), buf[:, 0:cw],
                    w_dram[r * 128:(r + 1) * 128, s * cw:(s + 1) * cw],
                    writes=[buf])
            ctx.op("v", lambda eng, buf=buf: eng.tensor_scalar(
                buf[:, 0:cw], buf[:, 0:cw], sb.vcol(qs_col), RND, ALU.mult, ALU.add),
                reads=[buf, sb.vecs], writes=[buf])
            ctx.op("v", lambda eng, buf=buf: eng.tensor_scalar(
                buf[:, 0:cw], buf[:, 0:cw], RND, -1.0, ALU.subtract, ALU.max),
                reads=[buf], writes=[buf])
            ctx.op("v", lambda eng, buf=buf: eng.tensor_scalar(
                qb[:, 0:cw], buf[:, 0:cw], 1.0, None, ALU.min),
                reads=[buf], writes=[qb])
            ctx.dma("qo", wq_dram[r * 128:(r + 1) * 128, s * cw:(s + 1) * cw],
                    qb[:, 0:cw], reads=[qb], writes=["wq_dram"])


def declare_l0_ffn(nc, p=""):
    FF = 4096
    return {
        "W1T": nc.dram_tensor(p + "W1T", [D, FF], F32, kind="ExternalInput").ap(),
        "W2T": nc.dram_tensor(p + "W2T", [FF, D], F32, kind="ExternalInput").ap(),
        "b1": nc.dram_tensor(p + "b1", [FF], F32, kind="ExternalInput").ap(),
        "b2": nc.dram_tensor(p + "b2", [D], F32, kind="ExternalInput").ap(),
        "w1q": nc.dram_tensor(p + "w1q", [D, FF], BF16, kind="Internal").ap(),
        "w2q": nc.dram_tensor(p + "w2q", [FF, D], BF16, kind="Internal").ap(),
    }


def emit_l0_ffn(nc, ctx, stack, aps, xld, xst):
    FF = 4096
    W1T, W2T, b1, b2 = aps["W1T"], aps["W2T"], aps["b1"], aps["b2"]
    w1q, w2q = aps["w1q"], aps["w2q"]
    if True:
        sb = StdBuf(ctx, stack, n_tmp=1, n_wslab=2, wslab_cols=2048, psum_bf16=0)
        wf = [ctx.sbuf(stack, f"wf{i}", [128, 2048], F32) for i in range(2)]
        racc = ctx.sbuf(stack, "racc", [128, 1], F32)
        rtmp = ctx.sbuf(stack, "rtmp", [128, 1], F32)
        h = [ctx.sbuf(stack, f"h{i}", [128, 512], BF16) for i in range(32)]
        xbt = sb.xb
        ps8 = [ctx.psum(stack, f"ps8_{i}", [128, 512], F32) for i in range(2)]
        if True:
            emit_const_init(ctx, sb)
            b1c = load_vec(ctx, sb, b1, FF)
            b2c = load_vec(ctx, sb, b2, D)
            qs1 = sb.valloc(1)
            qs2 = sb.valloc(1)
            for c in range(NC):
                xld(ctx, sb.x[c], c, q=QS_RR[c % 4])
            cast_stream(ctx, sb, sb.x, xbt)
            emit_quantize_weight(ctx, sb, W1T, w1q, D, FF, wf, racc, rtmp, qs1)
            emit_quantize_weight(ctx, sb, W2T, w2q, FF, D, wf, racc, rtmp, qs2)
            for tt in range(NT):
                sl = slice(tt * 512, (tt + 1) * 512)
                for fc in range(32):
                    slab = sb.wslab[fc % 2]
                    for k in range(NC):
                        ctx.dma("qw%d" % (fc % 2), slab[:, k * 128:(k + 1) * 128],
                                w1q[k * 128:(k + 1) * 128, fc * 128:(fc + 1) * 128],
                                reads=["wq_dram"], writes=[slab])
                    ps = ps8[fc % 2]
                    parts = [(slab[:, k * 128:(k + 1) * 128], xbt[k][:, sl],
                              [slab, xbt[k]]) for k in range(NC)]
                    ctx.mm(ps.ap, ps, parts)
                    ctx.op("s", lambda eng, ps=ps, fc=fc: eng.activation(
                        h[fc].ap, ps.ap, AF.Relu, bias=sb.vcol(b1c + fc)),
                        reads=[ps, sb.vecs], writes=[h[fc]])
                pslots = sb.ps[0:4] + ps8 + sb.ps[4:6]
                for fc in range(32):
                    slab = sb.wslab[fc % 2]
                    ctx.dma("qw%d" % (fc % 2), slab[:, 0:1024],
                            w2q[fc * 128:(fc + 1) * 128, 0:1024],
                            reads=["wq_dram"], writes=[slab])
                    for dc in range(NC):
                        ctx.mm(pslots[dc].ap, pslots[dc],
                               [(slab[:, dc * 128:(dc + 1) * 128], h[fc].ap,
                                 [slab, h[fc]])], accumulate=(fc > 0),
                               close=(fc == 31))
                for dc in range(NC):
                    ctx.op("v", lambda eng, dc=dc, sl=sl: eng.scalar_tensor_tensor(
                        sb.x[dc][:, sl], pslots[dc].ap, sb.vcol(b2c + dc),
                        sb.x[dc][:, sl], ALU.add, ALU.add),
                        reads=[pslots[dc], sb.vecs, sb.x[dc]], writes=[sb.x[dc]])
            emit_ln_fm(ctx, sb, sb.x, sb.x, 1e-5)
            for c in range(NC):
                xst(ctx, sb.x[c], c)


def build_l0_ffn():
    nc = bass.Bass("TRN2", target_bir_lowering=False)
    aps = declare_l0_ffn(nc)
    xin = nc.dram_tensor("xin", [D, T], F32, kind="ExternalInput").ap()
    xout = nc.dram_tensor("xout", [D, T], F32, kind="ExternalOutput").ap()
    with contextlib.ExitStack() as stack:
        ctx = Ctx(nc)
        ctx.make_sems(stack)
        with nc.Block() as block:
            emit_l0_ffn(nc, ctx, stack, aps, plain_ld(xin), plain_st(xout))
            ctx.finish()
            ctx.build(block)
    return nc


def run_l0_ffn(x_fm_cores, ins):
    return _run_block("l0_ffn", build_l0_ffn, x_fm_cores, ins)



class NS:
    pass


def _mk_lnbuf(ctx, stack, Tl=T, n_tmp=2):
    sb = NS()
    sb.T = Tl
    sb.tmp = [ctx.sbuf(stack, f"tmp{i}", [128, Tl], F32) for i in range(n_tmp)]
    sb.mu = ctx.sbuf(stack, "mu", [128, Tl], F32)
    sb.rstd = ctx.sbuf(stack, "rstd", [128, Tl], F32)
    sb.rows = TB(sb.mu.h, "mu")
    sb.rows2 = TB(sb.rstd.h, "rstd")
    sb.ones_b = ctx.sbuf(stack, "ones_b", [128, 1], BF16)
    sb.ones_f = ctx.sbuf(stack, "ones_f", [128, 1], F32)
    sb.onesrow = ctx.sbuf(stack, "onesrow", [1, 128], F32)
    sb.vecs = ctx.sbuf(stack, "vecs", [128, 160], F32)
    sb.nvec = 0
    sb.consts = ctx.sbuf(stack, "consts", [128, 8], F32)
    sb._cmap = {}
    sb._ctx = ctx
    sb.vcol = lambda col: sb.vecs[:, col:col + 1]
    import types
    sb.valloc = types.MethodType(StdBuf.valloc, sb)
    sb.const_ap = types.MethodType(StdBuf.const_ap, sb)
    return sb


SHUF_PAIRSWAP = [i ^ 1 for i in range(32)]


def declare_l1_attn(nc, p=""):
    aps = {}
    for name in ["WQ", "WK", "WV"]:
        aps[name] = nc.dram_tensor(p + name, [D, D], F32, kind="ExternalInput").ap()
    for name in ["csq", "ssq", "csk", "ssk"]:
        aps[name] = nc.dram_tensor(p + name, [D, T], F32, kind="ExternalInput").ap()
    aps["idin"] = nc.dram_tensor(p + "idin", [128, 128], F32,
                                 kind="ExternalInput").ap()
    aps["qx_d"] = nc.dram_tensor(p + "qx_d", [D, T], BF16, kind="Internal").ap()
    aps["kx_d"] = nc.dram_tensor(p + "kx_d", [D, T], BF16, kind="Internal").ap()
    aps["v_d"] = nc.dram_tensor(p + "v_d", [T, D], BF16, kind="Internal").ap()
    return aps


def emit_l1_attn(nc, ctx, stack, aps, xld, xst):
    WQ, WK, WV = aps["WQ"], aps["WK"], aps["WV"]
    csq, ssq, csk, ssk = aps["csq"], aps["ssq"], aps["csk"], aps["ssk"]
    idin = aps["idin"]
    qx_d, kx_d, v_d = aps["qx_d"], aps["kx_d"], aps["v_d"]
    if True:
        sb = _mk_lnbuf(ctx, stack)
        sb.ps = [ctx.psum(stack, f"ps{i}", [128, 512], F32) for i in range(8)]
        wslab = [ctx.sbuf(stack, f"wslab{i}", [128, 1024], BF16) for i in range(2)]
        sb.wslab = wslab
        ident = ctx.sbuf(stack, "ident", [128, 128], F32)
        xb = [ctx.sbuf(stack, f"xb{c}", [128, T], BF16) for c in range(NC)]
        xo = [ctx.sbuf(stack, f"xo{c}", [128, T], BF16) for c in range(NC)]

        if True:
            emit_const_init(ctx, sb)
            ctx.dma("qm", ident.ap, idin, writes=[ident])
            with contextlib.ExitStack() as sA:
                x = [ctx.sbuf(sA, f"x{c}", [128, T], F32) for c in range(NC)]
                for c in range(NC):
                    xld(ctx, x[c], c, q="qs")
                    ctx.op("v", lambda eng, c=c: eng.tensor_copy(xb[c].ap, x[c].ap),
                           reads=[x[c]], writes=[xb[c]])
            ctx.fence()
            with contextlib.ExitStack() as sB:
                qraw = ctx.sbuf(sB, "qraw", [128, T], BF16)
                shuf = ctx.sbuf(sB, "shuf", [128, T], BF16)
                qxb = ctx.sbuf(sB, "qxb", [128, T], BF16)
                cs_sl = ctx.sbuf(sB, "cs_sl", [128, T], F32)
                ss_sl = ctx.sbuf(sB, "ss_sl", [128, T], F32)
                vbuf = ctx.sbuf(sB, "vbuf", [128, D], BF16)
                for (W, tcs, tss, out_d) in ((WQ, csq, ssq, qx_d), (WK, csk, ssk, kx_d)):
                    for oc in range(NC):
                        slab = stream_w_slab(ctx, sb, W, oc % 2, oc * 128, 128)
                        for tt in range(NT):
                            ps = sb.ps[tt % 2]
                            parts = [(slab[:, k * 128:(k + 1) * 128],
                                      xb[k][:, tt * 512:(tt + 1) * 512],
                                      [slab, xb[k]]) for k in range(NC)]
                            ctx.mm(ps.ap, ps, parts)
                            ctx.op("s", lambda eng, tt=tt, ps=ps: eng.activation(
                                qraw[:, tt * 512:(tt + 1) * 512], ps.ap, AF.Identity),
                                reads=[ps], writes=[qraw])
                        ctx.dma("qs", cs_sl.ap, tcs[oc * 128:(oc + 1) * 128, :], writes=[cs_sl])
                        ctx.dma("qs", ss_sl.ap, tss[oc * 128:(oc + 1) * 128, :], writes=[ss_sl])
                        ctx.op("v", lambda eng: eng.stream_shuffle(shuf.ap, qraw.ap, SHUF_PAIRSWAP),
                               reads=[qraw], writes=[shuf])
                        ctx.op("v", lambda eng: eng.tensor_mul(sb.tmp[0].ap, qraw.ap, cs_sl.ap),
                               reads=[qraw, cs_sl], writes=[sb.tmp[0]])
                        ctx.op("v", lambda eng: eng.tensor_mul(sb.tmp[1].ap, shuf.ap, ss_sl.ap),
                               reads=[shuf, ss_sl], writes=[sb.tmp[1]])
                        ctx.op("v", lambda eng: eng.tensor_add(qxb.ap, sb.tmp[0].ap, sb.tmp[1].ap),
                               reads=[sb.tmp[0], sb.tmp[1]], writes=[qxb])
                        ctx.dma("qo", out_d[oc * 128:(oc + 1) * 128, :], qxb.ap, reads=[qxb])
                for tt in range(T // 128):
                    for c in range(NC):
                        slab = wslab[c % 2]
                        ctx.dma("qw%d" % (c % 2), slab.ap,
                                WV[c * 128:(c + 1) * 128, :], writes=[slab])
                        for nh in range(2):
                            ctx.mm(sb.ps[2 + nh].ap, sb.ps[2 + nh],
                                   [(xb[c][:, tt * 128:(tt + 1) * 128],
                                     slab[:, nh * 512:(nh + 1) * 512], [xb[c], slab])],
                                   accumulate=(c > 0), close=(c == NC - 1))
                    for nh in range(2):
                        ctx.op("s", lambda eng, nh=nh: eng.activation(
                            vbuf[:, nh * 512:(nh + 1) * 512], sb.ps[2 + nh].ap, AF.Identity),
                            reads=[sb.ps[2 + nh]], writes=[vbuf])
                    ctx.dma("qo", v_d[tt * 128:(tt + 1) * 128, :], vbuf.ap, reads=[vbuf])
            ctx.fence()
            with contextlib.ExitStack() as sC:
                qxs = [ctx.sbuf(sC, f"qxs{c}", [128, 512], BF16) for c in range(NC)]
                kxs = [ctx.sbuf(sC, f"kxs{c}", [128, T], BF16) for c in range(NC)]
                p2 = ctx.sbuf(sC, "p2", [128, T], F32)
                p2t = ctx.sbuf(sC, "p2t", [128, 16 * 512], BF16)
                dtile = ctx.sbuf(sC, "dtile", [128, T], F32)
                nmax4 = ctx.sbuf(sC, "nmax4", [128, 4], F32)
                nmax = ctx.sbuf(sC, "nmax", [128, 1], F32)
                zpart = ctx.sbuf(sC, "zpart", [128, 4], F32)
                zr = ctx.sbuf(sC, "zr", [128, 1], F32)
                for c in range(NC):
                    ctx.dma("qs", kxs[c].ap, kx_d[c * 128:(c + 1) * 128, :], writes=[kxs[c]])

                for qb in range(4):
                    for c in range(NC):
                        ctx.dma("qs", qxs[c].ap,
                                qx_d[c * 128:(c + 1) * 128, qb * 512:(qb + 1) * 512],
                                writes=[qxs[c]])
                    for qt in range(4):
                        q0 = qb * 512 + qt * 128
                        for nt in range(4):
                            ps = sb.ps[nt]
                            parts = [(qxs[c][:, qt * 128:(qt + 1) * 128],
                                      kxs[c][:, nt * 512:(nt + 1) * 512],
                                      [qxs[c], kxs[c]]) for c in range(NC)]
                            ctx.mm(ps.ap, ps, parts)
                            ctx.op("v", lambda eng, nt=nt, ps=ps: eng.reduce_max(
                                nmax4[:, nt:nt + 1], ps.ap, AX.X, negate=True),
                                reads=[ps], writes=[nmax4])
                        ctx.op("v", lambda eng: eng.tensor_reduce(
                            nmax.ap, nmax4.ap, AX.X, ALU.min),
                            reads=[nmax4], writes=[nmax])
                        for nt in range(4):
                            ps = sb.ps[nt]
                            ctx.op("s", lambda eng, nt=nt, ps=ps: eng.activation(
                                p2[:, nt * 512:(nt + 1) * 512], ps.ap, AF.Exp,
                                bias=nmax[:, 0:1], accum_out=zpart[:, nt:nt + 1]),
                                reads=[ps, nmax], writes=[p2, zpart])
                        ctx.op("v", lambda eng: eng.reduce_sum(zr.ap, zpart.ap, AX.X),
                               reads=[zpart], writes=[zr])
                        ctx.op("v", lambda eng: eng.reciprocal(zr.ap, zr.ap),
                               reads=[zr], writes=[zr])
                        ctx.op("p", lambda eng, q0=q0: eng.iota(
                            dtile.ap, [[-1, T]], base=q0, channel_multiplier=1,
                            allow_small_or_imprecise_dtypes=True),
                            writes=[dtile])
                        ctx.op("s", lambda eng: eng.activation(
                            dtile.ap, dtile.ap, AF.Abs),
                            reads=[dtile], writes=[dtile])
                        ctx.op("s", lambda eng: eng.activation(
                            dtile.ap, dtile.ap, AF.Exp, scale=math.log(0.9)),
                            reads=[dtile], writes=[dtile])
                        ctx.op("v", lambda eng: eng.scalar_tensor_tensor(
                            p2.ap, p2.ap, zr[:, 0:1], dtile.ap, ALU.mult, ALU.mult),
                            reads=[p2, zr, dtile], writes=[p2])
                        for kc in range(16):
                            pt = sb.ps[7]
                            ctx.op("t", lambda eng, kc=kc: eng.transpose(
                                pt[0:128, 0:128], p2[:, kc * 128:(kc + 1) * 128], ident.ap),
                                reads=[p2, ident], writes=[pt])
                            ctx.op("v", lambda eng, kc=kc, qt=qt: eng.tensor_copy(
                                p2t[:, kc * 512 + qt * 128: kc * 512 + (qt + 1) * 128],
                                pt[0:128, 0:128]),
                                reads=[pt], writes=[p2t])
                    for kc in range(16):
                        slab = wslab[kc % 2]
                        ctx.dma("qw%d" % (kc % 2), slab.ap,
                                v_d[kc * 128:(kc + 1) * 128, :], writes=[slab])
                        for dc in range(NC):
                            ctx.mm(sb.ps[dc].ap, sb.ps[dc],
                                   [(slab[:, dc * 128:(dc + 1) * 128],
                                     p2t[:, kc * 512:(kc + 1) * 512], [slab, p2t])],
                                   accumulate=(kc > 0), close=(kc == 15))
                    for dc in range(NC):
                        ctx.op("s", lambda eng, dc=dc, qb=qb: eng.activation(
                            xo[dc][:, qb * 512:(qb + 1) * 512], sb.ps[dc].ap, AF.Identity),
                            reads=[sb.ps[dc]], writes=[xo[dc]])
            ctx.fence()
            with contextlib.ExitStack() as sD:
                xd = [ctx.sbuf(sD, f"x2{c}", [128, T], F32) for c in range(NC)]
                sb.x = xd
                for c in range(NC):
                    xld(ctx, xd[c], c, q="qs")
                    ctx.op("v", lambda eng, a=xd[c], b=xo[c]: eng.tensor_add(a.ap, a.ap, b.ap),
                           reads=[xd[c], xo[c]], writes=[xd[c]])
                emit_ln_fm(ctx, sb, xd, xd, 1e-5)
                for c in range(NC):
                    xst(ctx, xd[c], c)


def build_l1_attn():
    nc = bass.Bass("TRN2", target_bir_lowering=False)
    aps = declare_l1_attn(nc)
    xin = nc.dram_tensor("xin", [D, T], F32, kind="ExternalInput").ap()
    xout = nc.dram_tensor("xout", [D, T], F32, kind="ExternalOutput").ap()
    with contextlib.ExitStack() as stack:
        ctx = Ctx(nc)
        ctx.make_sems(stack)
        with nc.Block() as block:
            emit_l1_attn(nc, ctx, stack, aps, plain_ld(xin), plain_st(xout))
            ctx.finish()
            ctx.build(block)
    return nc


def _xpos_tables():
    half = D // 2
    scale_vec = (np.arange(0, D, 2, dtype=np.float64) + 0.4 * D) / (1.4 * D)
    power = np.arange(T, dtype=np.float64)[:, None] / 512.0
    scale = scale_vec[None, :] ** power
    inv_freq = 1.0 / 10000.0 ** (np.arange(half, dtype=np.float64) / half)
    sinu = np.arange(T, dtype=np.float64)[:, None] * inv_freq[None, :]
    sin = np.repeat(np.sin(sinu), 2, axis=-1)
    cos = np.repeat(np.cos(sinu), 2, axis=-1)
    scale2 = np.repeat(scale, 2, axis=-1)
    sgn = np.where(np.arange(D) % 2 == 0, -1.0, 1.0)[None, :]
    csq = (cos * scale2).T
    ssq = (sin * scale2 * sgn).T
    csk = (cos / scale2).T
    ssk = (sin / scale2 * sgn).T
    return [np.ascontiguousarray(a, dtype=np.float32) for a in (csq, ssq, csk, ssk)]


def _weights_l1_attn(ins):
    csq, ssq, csk, ssk = _xpos_tables()
    return {
        "WQ": _f32c(ins["L1_attn_W_Q"]), "WK": _f32c(ins["L1_attn_W_K"]),
        "WV": _f32c(ins["L1_attn_W_V"]),
        "csq": csq, "ssq": ssq, "csk": csk, "ssk": ssk,
        "idin": np.eye(128, dtype=np.float32),
    }


def run_l1_attn(x_fm_cores, ins):
    return _run_block("l1_attn", build_l1_attn, x_fm_cores, ins)



def declare_l1_ffn(nc, p=""):
    H = 2048
    E = 8
    return {
        "rwT": nc.dram_tensor(p + "rwT", [D, E], F32, kind="ExternalInput").ap(),
        "ebias": nc.dram_tensor(p + "ebias", [E], F32, kind="ExternalInput").ap(),
        "w1": nc.dram_tensor(p + "w1", [E, D, H], F32, kind="ExternalInput").ap(),
        "w3": nc.dram_tensor(p + "w3", [E, D, H], F32, kind="ExternalInput").ap(),
        "w2": nc.dram_tensor(p + "w2", [E, H, D], F32, kind="ExternalInput").ap(),
        "sw1": nc.dram_tensor(p + "sw1", [D, H], F32, kind="ExternalInput").ap(),
        "sw3": nc.dram_tensor(p + "sw3", [D, H], F32, kind="ExternalInput").ap(),
        "sw2": nc.dram_tensor(p + "sw2", [H, D], F32, kind="ExternalInput").ap(),
        "idin": nc.dram_tensor(p + "idin", [128, 128], F32,
                               kind="ExternalInput").ap(),
    }


def emit_l1_ffn(nc, ctx, stack, aps, xld, xst):
    H = 2048
    E = 8
    rwT, ebias = aps["rwT"], aps["ebias"]
    w1, w3, w2 = aps["w1"], aps["w3"], aps["w2"]
    sw1, sw3, sw2 = aps["sw1"], aps["sw3"], aps["sw2"]
    idin = aps["idin"]
    if True:
        sb = _mk_lnbuf(ctx, stack, n_tmp=1)
        sb.ps = [ctx.psum(stack, f"ps{i}", [128, 512], F32) for i in range(8)]
        ident = ctx.sbuf(stack, "ident", [128, 128], F32)
        xb = [ctx.sbuf(stack, f"xb{c}", [128, T], BF16) for c in range(NC)]
        xo = [ctx.sbuf(stack, f"xo{c}", [128, T], F32) for c in range(NC)]
        w2res = [ctx.sbuf(stack, f"w2res{i}", [128, D], BF16) for i in range(16)]
        eo = [ctx.sbuf(stack, f"eo{i}", [128, 512], BF16) for i in range(16)]
        wsl = [ctx.sbuf(stack, f"wsl{i}", [128, 8 * 128], BF16) for i in range(4)]
        sb.wslab = wsl
        rwt = ctx.sbuf(stack, "rwt", [128, 64], BF16)
        ebrow = ctx.sbuf(stack, "ebrow", [1, 8], BF16)
        onesrow_b = ctx.sbuf(stack, "onesrow_b", [1, 128], BF16)
        lg = ctx.sbuf(stack, "lg", [128, 8], F32)
        l2 = ctx.sbuf(stack, "l2", [128, 8], F32)
        m1 = ctx.sbuf(stack, "m1", [128, 1], F32)
        m2 = ctx.sbuf(stack, "m2", [128, 1], F32)
        p1 = ctx.sbuf(stack, "p1", [128, 1], F32)
        p2s = ctx.sbuf(stack, "p2s", [128, 1], F32)
        mk1 = ctx.sbuf(stack, "mk1", [128, 8], F32)
        mk2 = ctx.sbuf(stack, "mk2", [128, 8], F32)
        cw = ctx.sbuf(stack, "cw", [128, 8], F32)
        cwT = ctx.sbuf(stack, "cwT", [8, T], F32)
        cwb = ctx.sbuf(stack, "cwb", [128, 512], F32)
        ga = ctx.sbuf(stack, "ga", [128, 512], BF16)
        ua = ctx.sbuf(stack, "ua", [128, 512], BF16)
        ohot = ctx.sbuf(stack, "ohot", [8, 8 * 128], F32)

        if True:
            emit_const_init(ctx, sb)
            ctx.op("v", lambda eng: eng.memset(onesrow_b.ap, 1.0), writes=[onesrow_b])
            ctx.op("p", lambda eng: eng.iota(ohot.ap, [[1, 8], [0, 128]],
                                             base=0, channel_multiplier=-1,
                                             allow_small_or_imprecise_dtypes=True),
                   writes=[ohot])
            ctx.op("v", lambda eng: eng.tensor_scalar(ohot.ap, ohot.ap, 0.0, None,
                                                      ALU.is_equal),
                   reads=[ohot], writes=[ohot])
            ctx.dma("qm", ident.ap, idin, writes=[ident])
            ctx.dma("qw0", ebrow.ap, ebias[None, :], writes=[ebrow])
            for c in range(NC):
                ctx.dma("qw1", rwt[:, c * 8:(c + 1) * 8],
                        rwT[c * 128:(c + 1) * 128, :], writes=[rwt])
            with contextlib.ExitStack() as sA:
                xtmp = [ctx.sbuf(sA, "xtmp0", [128, T], F32)]
                for c in range(NC):
                    b = xtmp[0]
                    xld(ctx, b, c, q=QS_RR[c % 2])
                    ctx.op("v", lambda eng, a=xb[c], b=b: eng.tensor_copy(a.ap, b.ap),
                           reads=[b], writes=[xb[c]])
            ctx.fence()
            for c in range(NC):
                ctx.op("v", lambda eng, a=xo[c]: eng.memset(a.ap, 0.0), writes=[xo[c]])
            for p in range(T // 128):
                ps = sb.ps[0]
                parts = [(xb[c][:, p * 128:(p + 1) * 128], rwt[:, c * 8:(c + 1) * 8],
                          [xb[c], rwt]) for c in range(NC)]
                parts.append((onesrow_b.ap, ebrow.ap, [onesrow_b, ebrow]))
                ctx.mm(ps[:, 0:8], ps, parts)
                ctx.op("v", lambda eng, ps=ps: eng.tensor_copy(lg.ap, ps[:, 0:8]),
                       reads=[ps], writes=[lg])
                ctx.op("v", lambda eng: eng.reduce_max(m1.ap, lg.ap, AX.X),
                       reads=[lg], writes=[m1])
                ctx.op("v", lambda eng: eng.tensor_scalar(mk1.ap, lg.ap, m1[:, 0:1],
                                                          None, ALU.is_ge),
                       reads=[lg, m1], writes=[mk1])
                ctx.op("v", lambda eng: eng.scalar_tensor_tensor(
                    l2.ap, mk1.ap, -1e30, lg.ap, ALU.mult, ALU.add),
                    reads=[mk1, lg], writes=[l2])
                ctx.op("v", lambda eng: eng.reduce_max(m2.ap, l2.ap, AX.X),
                       reads=[l2], writes=[m2])
                ctx.op("v", lambda eng: eng.tensor_scalar(mk2.ap, l2.ap, m2[:, 0:1],
                                                          None, ALU.is_ge),
                       reads=[l2, m2], writes=[mk2])
                ctx.op("v", lambda eng: eng.tensor_sub(p1.ap, m2.ap, m1.ap),
                       reads=[m1, m2], writes=[p1])
                ctx.op("s", lambda eng: eng.activation(p1.ap, p1.ap, AF.Sigmoid,
                                                       scale=-1.0),
                       reads=[p1], writes=[p1])
                ctx.op("v", lambda eng: eng.tensor_scalar(p2s.ap, p1.ap, -1.0, 1.0,
                                                          ALU.mult, ALU.add),
                       reads=[p1], writes=[p2s])
                ctx.op("v", lambda eng: eng.tensor_scalar(mk1.ap, mk1.ap, p1[:, 0:1],
                                                          None, ALU.mult),
                       reads=[mk1, p1], writes=[mk1])
                ctx.op("v", lambda eng: eng.scalar_tensor_tensor(
                    cw.ap, mk2.ap, p2s[:, 0:1], mk1.ap, ALU.mult, ALU.add),
                    reads=[mk2, p2s, mk1], writes=[cw])
                pt = sb.ps[7]
                ctx.op("t", lambda eng, pt=pt: eng.transpose(
                    pt[0:8, 0:128], cw.ap, ident.ap),
                    reads=[cw, ident], writes=[pt])
                ctx.op("v", lambda eng, p=p, pt=pt: eng.tensor_copy(
                    cwT[0:8, p * 128:(p + 1) * 128], pt[0:8, 0:128]),
                    reads=[pt], writes=[cwT])
            for e in range(E + 1):
                for hc in range(16):
                    wdram = sw2 if e == 0 else w2[e - 1]
                    ctx.dma("qw%d" % (hc % 4), w2res[hc].ap,
                            wdram[hc * 128:(hc + 1) * 128, :], writes=[w2res[hc]])
                for tt in range(NT):
                    sl = slice(tt * 512, (tt + 1) * 512)
                    if e == 0:
                        ctx.op("v", lambda eng: eng.memset(cwb.ap, 1.0), writes=[cwb])
                    else:
                        bc = sb.ps[6]
                        ctx.mm(bc.ap, bc, [(ohot[:, (e - 1) * 128:e * 128],
                                            cwT[0:8, sl],
                                            [ohot, cwT])])
                        ctx.op("v", lambda eng, bc=bc: eng.tensor_copy(cwb.ap, bc.ap),
                               reads=[bc], writes=[cwb])
                    for hc in range(16):
                        b0 = (hc % 2) * 2
                        slg = wsl[b0]
                        slu = wsl[b0 + 1]
                        wg = sw1 if e == 0 else w1[e - 1]
                        wu = sw3 if e == 0 else w3[e - 1]
                        for k in range(NC):
                            ctx.dma("qw%d" % (k % 2), slg[:, k * 128:(k + 1) * 128],
                                    wg[k * 128:(k + 1) * 128, hc * 128:(hc + 1) * 128],
                                    writes=[slg])
                            ctx.dma("qw%d" % (2 + k % 2), slu[:, k * 128:(k + 1) * 128],
                                    wu[k * 128:(k + 1) * 128, hc * 128:(hc + 1) * 128],
                                    writes=[slu])
                        psg = sb.ps[0]
                        psu = sb.ps[1]
                        ctx.mm(psg[:, 0:128 * 4], psg,
                               [(slg[:, k * 128:(k + 1) * 128], xb[k][:, sl], [slg, xb[k]])
                                for k in range(NC)])
                        ctx.mm(psu[:, 0:128 * 4], psu,
                               [(slu[:, k * 128:(k + 1) * 128], xb[k][:, sl], [slu, xb[k]])
                                for k in range(NC)])
                        ctx.op("s", lambda eng, psg=psg: eng.activation(
                            ga.ap, psg.ap, AF.Silu), reads=[psg], writes=[ga])
                        ctx.op("v", lambda eng, psu=psu: eng.tensor_mul(
                            ua.ap, psu.ap, cwb.ap),
                            reads=[psu, cwb], writes=[ua])
                        ctx.op("v", lambda eng, hc=hc: eng.tensor_mul(
                            eo[hc].ap, ga.ap, ua.ap),
                            reads=[ga, ua], writes=[eo[hc]])
                    for hc in range(16):
                        for dc in range(NC):
                            ctx.mm(sb.ps[dc].ap, sb.ps[dc],
                                   [(w2res[hc][:, dc * 128:(dc + 1) * 128], eo[hc].ap,
                                     [w2res[hc], eo[hc]])],
                                   accumulate=(hc > 0), close=(hc == 15))
                    for dc in range(NC):
                        ctx.op("v", lambda eng, dc=dc, sl=sl: eng.tensor_add(
                            xo[dc][:, sl], xo[dc][:, sl], sb.ps[dc].ap),
                            reads=[sb.ps[dc], xo[dc]], writes=[xo[dc]])
            ctx.fence()
            with contextlib.ExitStack() as sD:
                xtmp = [ctx.sbuf(sD, "xr0", [128, T], F32)]
                for c in range(NC):
                    b = xtmp[0]
                    xld(ctx, b, c, q=QS_RR[c % 2])
                    ctx.op("v", lambda eng, a=xo[c], b=b: eng.tensor_add(
                        a.ap, a.ap, b.ap),
                        reads=[xo[c], b], writes=[xo[c]])
                emit_ln_fm(ctx, sb, xo, xo, 1e-5)
                for c in range(NC):
                    xst(ctx, xo[c], c)


def build_l1_ffn():
    nc = bass.Bass("TRN2", target_bir_lowering=False)
    aps = declare_l1_ffn(nc)
    xin = nc.dram_tensor("xin", [D, T], F32, kind="ExternalInput").ap()
    xout = nc.dram_tensor("xout", [D, T], F32, kind="ExternalOutput").ap()
    with contextlib.ExitStack() as stack:
        ctx = Ctx(nc)
        ctx.make_sems(stack)
        with nc.Block() as block:
            emit_l1_ffn(nc, ctx, stack, aps, plain_ld(xin), plain_st(xout))
            ctx.finish()
            ctx.build(block)
    return nc


def _weights_l1_ffn(ins):
    return {
        "rwT": _f32c(np.asarray(ins["L1_ffn_router_w"]).T),
        "ebias": _f32c(ins["L1_ffn_experts_bias"]),
        "w1": _f32c(ins["L1_ffn_w1"]), "w3": _f32c(ins["L1_ffn_w3"]),
        "w2": _f32c(ins["L1_ffn_w2"]),
        "sw1": _f32c(np.asarray(ins["L1_ffn_sw1"])[0]),
        "sw3": _f32c(np.asarray(ins["L1_ffn_sw3"])[0]),
        "sw2": _f32c(np.asarray(ins["L1_ffn_sw2"])[0]),
        "idin": np.eye(128, dtype=np.float32),
    }


def run_l1_ffn(x_fm_cores, ins):
    return _run_block("l1_ffn", build_l1_ffn, x_fm_cores, ins)



def declare_l2_ffn(nc, p=""):
    FF = 4096
    aps = {
        "w1S": nc.dram_tensor(p + "w1S", [FF // 128, 128, D], F32,
                              kind="ExternalInput").ap(),
        "w2S": nc.dram_tensor(p + "w2S", [FF // 128, 128, D], F32,
                              kind="ExternalInput").ap(),
        "h1S": nc.dram_tensor(p + "h1S", [FF // 128, 128, FF], F32,
                              kind="ExternalInput").ap(),
        "h2S": nc.dram_tensor(p + "h2S", [FF // 128, 128, FF], F32,
                              kind="ExternalInput").ap(),
        "w3T": nc.dram_tensor(p + "w3T", [FF, D], F32, kind="ExternalInput").ap(),
    }
    for name, ln in [("s1", FF), ("s2", FF), ("sh1", FF), ("sh2", FF), ("s3", D)]:
        aps[name] = nc.dram_tensor(p + name, [ln], F32, kind="ExternalInput").ap()
    return aps


def emit_l2_ffn(nc, ctx, stack, aps, xld, xst):
    FF = 4096
    w1S, w2S, h1S, h2S, w3T = (aps["w1S"], aps["w2S"], aps["h1S"], aps["h2S"],
                               aps["w3T"])
    vin = {n: aps[n] for n in ["s1", "s2", "sh1", "sh2", "s3"]}
    if True:
        sb = _mk_lnbuf(ctx, stack, Tl=512, n_tmp=1)
        sb.ps = [ctx.psum(stack, f"ps{i}", [128, 512], F32) for i in range(8)]
        xb = [ctx.sbuf(stack, f"xb{c}", [128, T], BF16) for c in range(NC)]
        hA = [ctx.sbuf(stack, f"hA{i}", [128, 512], BF16) for i in range(32)]
        hB = [ctx.sbuf(stack, f"hB{i}", [128, 512], BF16) for i in range(32)]
        wsl = [ctx.sbuf(stack, f"wsl{i}", [128, 4096], BF16) for i in range(2)]
        xr = ctx.sbuf(stack, "xr", [128, 512], F32)
        xnew = [ctx.sbuf(stack, f"xnew{i}", [128, 512], F32) for i in range(NC)]

        if True:
            emit_const_init(ctx, sb)
            cols = {n: load_vec(ctx, sb, ap, ln)
                    for (n, ln), ap in zip([("s1", FF), ("s2", FF), ("sh1", FF),
                                            ("sh2", FF), ("s3", D)],
                                           [vin["s1"], vin["s2"], vin["sh1"],
                                            vin["sh2"], vin["s3"]])}
            with contextlib.ExitStack() as sA:
                xtmp = ctx.sbuf(sA, "xtmp0", [128, T], F32)
                for c in range(NC):
                    xld(ctx, xtmp, c, q=QS_RR[c % 2])
                    ctx.op("v", lambda eng, a=xb[c], b=xtmp: eng.tensor_copy(a.ap, b.ap),
                           reads=[xtmp], writes=[xb[c]])
            ctx.fence()
            for tt in range(NT):
                sl = slice(tt * 512, (tt + 1) * 512)
                for fc in range(32):
                    for (wd, psl, buf) in ((w1S, 0, 0), (w2S, 1, 1)):
                        slab = wsl[buf]
                        ctx.dma("qw%d" % (buf * 2 + fc % 2),
                                slab[:, 0:D], wd[fc], writes=[slab])
                        ps = sb.ps[psl]
                        ctx.mm(ps.ap, ps,
                               [(slab[:, k * 128:(k + 1) * 128], xb[k][:, sl],
                                 [slab, xb[k]]) for k in range(NC)])
                    ctx.op("s", lambda eng, fc=fc: eng.activation(
                        sb.tmp[0][:, 0:512], sb.ps[0].ap, AF.Silu,
                        scale=sb.vcol(cols["s1"] + fc)),
                        reads=[sb.ps[0], sb.vecs], writes=[sb.tmp[0]])
                    ctx.op("v", lambda eng, fc=fc: eng.tensor_scalar(
                        hA[fc].ap, sb.ps[1].ap, sb.vcol(cols["s2"] + fc), None,
                        ALU.mult),
                        reads=[sb.ps[1], sb.vecs], writes=[hA[fc]])
                    ctx.op("v", lambda eng, fc=fc: eng.tensor_mul(
                        hA[fc].ap, hA[fc].ap, sb.tmp[0][:, 0:512]),
                        reads=[hA[fc], sb.tmp[0]], writes=[hA[fc]])
                for (wd, scol, src_h, dst_h) in ((h1S, cols["sh1"], hA, hB),
                                                 (h2S, cols["sh2"], hB, hA)):
                    for fo in range(32):
                        slab = wsl[fo % 2]
                        ctx.dma("qw%d" % ((fo % 2) * 2), slab[:, 0:2048],
                                wd[fo, :, 0:2048], writes=[slab])
                        ctx.dma("qw%d" % ((fo % 2) * 2 + 1), slab[:, 2048:4096],
                                wd[fo, :, 2048:4096], writes=[slab])
                        ps = sb.ps[2 + fo % 2]
                        ctx.mm(ps.ap, ps,
                               [(slab[:, fi * 128:(fi + 1) * 128], src_h[fi].ap,
                                 [slab, src_h[fi]]) for fi in range(32)])
                        ctx.op("s", lambda eng, fo=fo, ps=ps, scol=scol, dst=dst_h[fo]:
                               eng.activation(dst.ap, ps.ap, AF.Gelu,
                                              scale=sb.vcol(scol + fo)),
                               reads=[ps, sb.vecs], writes=[dst_h[fo]])
                for fc in range(32):
                    slab = wsl[fc % 2]
                    ctx.dma("qw%d" % (fc % 2), slab[:, 0:1024],
                            w3T[fc * 128:(fc + 1) * 128, 0:1024],
                            writes=[slab])
                    for dc in range(NC):
                        ctx.mm(sb.ps[dc].ap, sb.ps[dc],
                               [(slab[:, dc * 128:(dc + 1) * 128], hA[fc].ap,
                                 [slab, hA[fc]])],
                               accumulate=(fc > 0), close=(fc == 31))
                for dc in range(NC):
                    xld(ctx, xr, dc, col0=tt * 512, ncols=512, dcol0=0,
                        q=QS_RR[dc % 2])
                    ctx.op("v", lambda eng, dc=dc: eng.scalar_tensor_tensor(
                        xnew[dc].ap, sb.ps[dc].ap, sb.vcol(cols["s3"] + dc),
                        xr.ap, ALU.mult, ALU.add),
                        reads=[sb.ps[dc], sb.vecs, xr], writes=[xnew[dc]])
                emit_ln_fm(ctx, sb, xnew, xnew, 1e-5, Tl=512)
                for dc in range(NC):
                    xst(ctx, xnew[dc], dc, col0=tt * 512, ncols=512, scol0=0)


def build_l2_ffn():
    nc = bass.Bass("TRN2", target_bir_lowering=False)
    aps = declare_l2_ffn(nc)
    xin = nc.dram_tensor("xin", [D, T], F32, kind="ExternalInput").ap()
    xout = nc.dram_tensor("xout", [D, T], F32, kind="ExternalOutput").ap()
    with contextlib.ExitStack() as stack:
        ctx = Ctx(nc)
        ctx.make_sems(stack)
        with nc.Block() as block:
            emit_l2_ffn(nc, ctx, stack, aps, plain_ld(xin), plain_st(xout))
            ctx.finish()
            ctx.build(block)
    return nc


def _weights_l2_ffn(ins):
    return {
        "w1S": pack_lhsT(np.asarray(ins["L2_ffn_w1_w"]).T),
        "w2S": pack_lhsT(np.asarray(ins["L2_ffn_w2_w"]).T),
        "h1S": pack_lhsT(np.asarray(ins["L2_ffn_h1_w"]).T),
        "h2S": pack_lhsT(np.asarray(ins["L2_ffn_h2_w"]).T),
        "w3T": _f32c(np.asarray(ins["L2_ffn_w3_w"]).T),
        "s1": _f32c(ins["L2_ffn_w1_s"]), "s2": _f32c(ins["L2_ffn_w2_s"]),
        "sh1": _f32c(ins["L2_ffn_h1_s"]), "sh2": _f32c(ins["L2_ffn_h2_s"]),
        "s3": _f32c(ins["L2_ffn_w3_s"]),
    }


def run_l2_ffn(x_fm_cores, ins):
    return _run_block("l2_ffn", build_l2_ffn, x_fm_cores, ins)



_SM_RTS = np.roots([1.0, -3.0, 6.0, -6.0])
_SM_R = float(_SM_RTS[np.isreal(_SM_RTS)].real[0])
_SM_U = (_SM_R - 3.0) / 2.0
_SM_V = float((_SM_R * _SM_R - 3 * _SM_R + 6) - _SM_U * _SM_U)


def declare_l3_attn(nc, p=""):
    return {
        "wqkvT": nc.dram_tensor(p + "wqkvT", [D, 3 * D], F32,
                                kind="ExternalInput").ap(),
        "woT": nc.dram_tensor(p + "woT", [D, D], F32, kind="ExternalInput").ap(),
        "b_out": nc.dram_tensor(p + "b_out", [D], F32, kind="ExternalInput").ap(),
        "idin": nc.dram_tensor(p + "idin", [128, 128], F32,
                               kind="ExternalInput").ap(),
        "qk_d": nc.dram_tensor(p + "qk_d", [2 * D, T], BF16, kind="Internal").ap(),
        "v_d": nc.dram_tensor(p + "v_d", [T, D], BF16, kind="Internal").ap(),
    }


def emit_l3_attn(nc, ctx, stack, aps, xld, xst, nheads=16, mode='full'):
    H = 16
    HD = 64
    SC = float(HD) ** -0.5
    wqkvT, woT, b_out, idin = aps["wqkvT"], aps["woT"], aps["b_out"], aps["idin"]
    qk_d, v_d = aps["qk_d"], aps["v_d"]
    if True:
        sb = _mk_lnbuf(ctx, stack, Tl=512, n_tmp=1)
        sb.ps = [ctx.psum(stack, f"ps{i}", [128, 512], F32) for i in range(8)]
        ident = ctx.sbuf(stack, "ident", [128, 128], F32)
        wsl = [ctx.sbuf(stack, f"wsl{i}", [128, 1024], BF16) for i in range(2)]
        sb.wslab = wsl
        xb = [ctx.sbuf(stack, f"xb{c}", [128, T], BF16) for c in range(NC)]
        xo = [ctx.sbuf(stack, f"xo{c}", [128, T], BF16) for c in range(NC)]
        qh = ctx.sbuf(stack, "qh", [64, T], BF16)
        kh = ctx.sbuf(stack, "kh", [64, T], BF16)
        vh = [ctx.sbuf(stack, f"vh{i}", [128, 64], BF16) for i in range(16)]
        p2 = ctx.sbuf(stack, "p2", [128, T], F32)
        p2t = ctx.sbuf(stack, "p2t", [128, 16 * 512], BF16)
        sqb = ctx.sbuf(stack, "sqb", [128, T], F32)
        nmax4 = ctx.sbuf(stack, "nmax4", [128, 4], F32)
        nmax = ctx.sbuf(stack, "nmax", [128, 1], F32)
        bias1 = ctx.sbuf(stack, "bias1", [128, 1], F32)
        bias2 = ctx.sbuf(stack, "bias2", [128, 1], F32)
        zpart = ctx.sbuf(stack, "zpart", [128, 4], F32)
        zr = ctx.sbuf(stack, "zr", [128, 1], F32)
        xnew = [ctx.sbuf(stack, f"xnew{i}", [128, 512], F32) for i in range(NC)]
        xr = ctx.sbuf(stack, "xr", [128, 512], F32)
        qkvb = ctx.sbuf(stack, "qkvb", [128, T], BF16)

        if True:
            emit_const_init(ctx, sb)
            ctx.dma("qm", ident.ap, idin, writes=[ident])
            boc = load_vec(ctx, sb, b_out, D)
            with contextlib.ExitStack() as sA:
                xtmp = ctx.sbuf(sA, "xtmp0", [128, T], F32)
                for c in range(NC):
                    xld(ctx, xtmp, c, q=QS_RR[c % 2])
                    ctx.op("v", lambda eng, a=xb[c], b=xtmp: eng.tensor_copy(a.ap, b.ap),
                           reads=[xtmp], writes=[xb[c]])
            ctx.fence()
            for oc in range(16):
                slab = stream_w_slab(ctx, sb, wqkvT, oc % 2, oc * 128, 128)
                for tt in range(NT):
                    ps = sb.ps[tt % 2]
                    parts = [(slab[:, k * 128:(k + 1) * 128],
                              xb[k][:, tt * 512:(tt + 1) * 512],
                              [slab, xb[k]]) for k in range(NC)]
                    ctx.mm(ps.ap, ps, parts)
                    ctx.op("s", lambda eng, tt=tt, ps=ps: eng.activation(
                        qkvb[:, tt * 512:(tt + 1) * 512], ps.ap, AF.Identity),
                        reads=[ps], writes=[qkvb])
                ctx.dma("qo", qk_d[oc * 128:(oc + 1) * 128, :], qkvb.ap, reads=[qkvb])
            for tt in range(T // 128):
                for c in range(NC):
                    slab = wsl[c % 2]
                    ctx.dma("qw%d" % (c % 2), slab.ap,
                            wqkvT[c * 128:(c + 1) * 128, 2 * D:3 * D], writes=[slab])
                    for nh in range(2):
                        ctx.mm(sb.ps[2 + nh].ap, sb.ps[2 + nh],
                               [(xb[c][:, tt * 128:(tt + 1) * 128],
                                 slab[:, nh * 512:(nh + 1) * 512], [xb[c], slab])],
                               accumulate=(c > 0), close=(c == NC - 1))
                for nh in range(2):
                    ctx.op("s", lambda eng, nh=nh: eng.activation(
                        qkvb[:, nh * 512:(nh + 1) * 512], sb.ps[2 + nh].ap, AF.Identity),
                        reads=[sb.ps[2 + nh]], writes=[qkvb])
                ctx.dma("qo", v_d[tt * 128:(tt + 1) * 128, :], qkvb[:, 0:D],
                        reads=[qkvb])
            ctx.fence()
            for h in range(nheads):
                ctx.dma("qs", qh.ap, qk_d[h * 64:(h + 1) * 64, :], writes=[qh])
                ctx.dma("qs1", kh.ap, qk_d[D + h * 64:D + (h + 1) * 64, :], writes=[kh])
                for kc in range(16):
                    ctx.dma("qs2", vh[kc].ap,
                            v_d[kc * 128:(kc + 1) * 128, h * 64:(h + 1) * 64],
                            writes=[vh[kc]])
                for qb in range(4):
                    for qt in range(4):
                        for nt in range(4):
                            ps = sb.ps[nt]
                            ctx.mm(ps.ap, ps,
                                   [(qh[:, (qb * 4 + qt) * 128:(qb * 4 + qt + 1) * 128],
                                     kh[:, nt * 512:(nt + 1) * 512], [qh, kh])])
                            ctx.op("v", lambda eng, nt=nt, ps=ps: eng.reduce_max(
                                nmax4[:, nt:nt + 1], ps.ap, AX.X, negate=True),
                                reads=[ps], writes=[nmax4])
                        if mode == "s":
                            ctx.op("v", lambda eng: eng.tensor_copy(
                                p2.ap, sqb.ap), reads=[sqb], writes=[p2])
                            continue
                        ctx.op("v", lambda eng: eng.tensor_reduce(
                            nmax.ap, nmax4.ap, AX.X, ALU.min),
                            reads=[nmax4], writes=[nmax])
                        ctx.op("v", lambda eng: eng.tensor_scalar(
                            bias1.ap, nmax.ap, SC, _SM_U, ALU.mult, ALU.add),
                            reads=[nmax], writes=[bias1])
                        ctx.op("v", lambda eng: eng.tensor_scalar(
                            bias2.ap, nmax.ap, SC, -_SM_R, ALU.mult, ALU.add),
                            reads=[nmax], writes=[bias2])
                        for nt in range(4):
                            ps = sb.ps[nt]
                            sl5 = slice(nt * 512, (nt + 1) * 512)
                            ctx.op("s", lambda eng, ps=ps, sl5=sl5: eng.activation(
                                sqb[:, sl5], ps.ap, AF.Square, bias=bias1[:, 0:1],
                                scale=SC),
                                reads=[ps, bias1], writes=[sqb])
                            if mode == "sm2":
                                ctx.op("v", lambda eng, sl5=sl5: eng.tensor_copy(
                                    p2[:, sl5], sqb[:, sl5]), reads=[sqb], writes=[p2])
                                continue
                            ctx.op("s", lambda eng, ps=ps, sl5=sl5: eng.activation(
                                p2[:, sl5], ps.ap, AF.Identity, bias=bias2[:, 0:1],
                                scale=SC),
                                reads=[ps, bias2], writes=[p2])
                            if mode == "sm3":
                                continue
                            ctx.op("v", lambda eng, sl5=sl5: eng.scalar_tensor_tensor(
                                p2[:, sl5], sqb[:, sl5], _SM_V, p2[:, sl5],
                                ALU.add, ALU.mult),
                                reads=[sqb, p2], writes=[p2])
                            if mode == "sm4":
                                continue
                            ctx.op("s", lambda eng, sl5=sl5: eng.activation(
                                p2[:, sl5], p2[:, sl5], AF.Ln, scale=-1.0 / 6.0),
                                reads=[p2], writes=[p2])
                            ctx.op("s", lambda eng, sl5=sl5, nt=nt: eng.activation(
                                p2[:, sl5], p2[:, sl5], AF.Exp, scale=-1.0,
                                accum_out=zpart[:, nt:nt + 1]),
                                reads=[p2], writes=[p2, zpart])
                        ctx.op("v", lambda eng: eng.reduce_sum(zr.ap, zpart.ap, AX.X),
                               reads=[zpart], writes=[zr])
                        ctx.op("v", lambda eng: eng.reciprocal(zr.ap, zr.ap),
                               reads=[zr], writes=[zr])
                        ctx.op("v", lambda eng: eng.tensor_scalar(
                            p2.ap, p2.ap, zr[:, 0:1], None, ALU.mult),
                            reads=[p2, zr], writes=[p2])
                        if mode.startswith("sm"):
                            continue
                        for kc in range(16):
                            pt = sb.ps[7]
                            ctx.op("t", lambda eng, kc=kc: eng.transpose(
                                pt[0:128, 0:128], p2[:, kc * 128:(kc + 1) * 128],
                                ident.ap),
                                reads=[p2, ident], writes=[pt])
                            ctx.op("v", lambda eng, kc=kc, qt=qt: eng.tensor_copy(
                                p2t[:, kc * 512 + qt * 128: kc * 512 + (qt + 1) * 128],
                                pt[0:128, 0:128]),
                                reads=[pt], writes=[p2t])
                    if mode != "full":
                        continue
                    for kc in range(16):
                        ctx.mm(sb.ps[4][0:64, :], sb.ps[4],
                               [(vh[kc].ap, p2t[:, kc * 512:(kc + 1) * 512],
                                 [vh[kc], p2t])],
                               accumulate=(kc > 0), close=(kc == 15))
                    ctx.op("s", lambda eng, h=h, qb=qb: eng.activation(
                        xo[h // 2][(h % 2) * 64:(h % 2) * 64 + 64,
                                   qb * 512:(qb + 1) * 512],
                        sb.ps[4][0:64, :], AF.Identity),
                        reads=[sb.ps[4]], writes=[xo[h // 2]])
            ctx.fence()
            for tt in range(NT):
                sl = slice(tt * 512, (tt + 1) * 512)
                for oc in range(NC):
                    slab = stream_w_slab(ctx, sb, woT, oc % 2, oc * 128, 128)
                    ps = sb.ps[oc % 4]
                    parts = [(slab[:, k * 128:(k + 1) * 128], xo[k][:, sl],
                              [slab, xo[k]]) for k in range(NC)]
                    ctx.mm(ps.ap, ps, parts)
                    xld(ctx, xr, oc, col0=tt * 512, ncols=512, dcol0=0,
                        q=QS_RR[2 + oc % 2])
                    ctx.op("v", lambda eng, oc=oc, ps=ps: eng.tensor_scalar(
                        xnew[oc].ap, ps.ap, 1.0, sb.vcol(boc + oc), ALU.mult, ALU.add),
                        reads=[ps, sb.vecs], writes=[xnew[oc]])
                    ctx.op("v", lambda eng, oc=oc: eng.tensor_add(
                        xnew[oc].ap, xnew[oc].ap, xr.ap),
                        reads=[xnew[oc], xr], writes=[xnew[oc]])
                emit_ln_fm(ctx, sb, xnew, xnew, 1e-5, Tl=512)
                for oc in range(NC):
                    xst(ctx, xnew[oc], oc, col0=tt * 512, ncols=512, scol0=0)


def build_l3_attn(nheads=16, mode='full'):
    nc = bass.Bass("TRN2", target_bir_lowering=False)
    aps = declare_l3_attn(nc)
    xin = nc.dram_tensor("xin", [D, T], F32, kind="ExternalInput").ap()
    xout = nc.dram_tensor("xout", [D, T], F32, kind="ExternalOutput").ap()
    with contextlib.ExitStack() as stack:
        ctx = Ctx(nc)
        ctx.make_sems(stack)
        with nc.Block() as block:
            emit_l3_attn(nc, ctx, stack, aps, plain_ld(xin), plain_st(xout),
                         nheads, mode)
            ctx.finish()
            ctx.build(block)
    return nc


def _weights_l3_attn(ins):
    return {
        "wqkvT": _f32c(np.asarray(ins["L3_attn_w_qkv"]).T),
        "woT": _f32c(np.asarray(ins["L3_attn_w_out"]).T),
        "b_out": _f32c(ins["L3_attn_b_out"]),
        "idin": np.eye(128, dtype=np.float32),
    }


def run_l3_attn(x_fm_cores, ins):
    return _run_block("l3_attn", build_l3_attn, x_fm_cores, ins,
                      wkey="l3_attn")



L3F_DEPTH = 6
L3F_W = 256
L3F_DI = 2730
L3F_NA = 22
L3F_WIDTHS = [128] * 21 + [42]


def declare_l3_ffn(nc, p=""):
    aps = {
        "Wqkv": nc.dram_tensor(p + "Wqkv", [L3F_DEPTH, D, 3 * D], F32,
                               kind="ExternalInput").ap(),
        "Wo": nc.dram_tensor(p + "Wo", [L3F_DEPTH, D, D], F32,
                             kind="ExternalInput").ap(),
        "W1P": nc.dram_tensor(p + "W1P", [L3F_DEPTH, 2 * L3F_NA, 128, D], F32,
                              kind="ExternalInput").ap(),
        "W2P": nc.dram_tensor(p + "W2P", [L3F_DEPTH, L3F_NA * 128, D], F32,
                              kind="ExternalInput").ap(),
        "qscale": nc.dram_tensor(p + "qscale", [L3F_DEPTH, 64], F32,
                                 kind="ExternalInput").ap(),
        "kscale": nc.dram_tensor(p + "kscale", [L3F_DEPTH, 64], F32,
                                 kind="ExternalInput").ap(),
        "ln_g": nc.dram_tensor(p + "ln_g", [L3F_DEPTH, D], F32,
                               kind="ExternalInput").ap(),
        "ln_b": nc.dram_tensor(p + "ln_b", [L3F_DEPTH, D], F32,
                               kind="ExternalInput").ap(),
        "b1": nc.dram_tensor(p + "b1", [L3F_DEPTH, 5460], F32,
                             kind="ExternalInput").ap(),
        "b2": nc.dram_tensor(p + "b2", [L3F_DEPTH, D], F32,
                             kind="ExternalInput").ap(),
        "csq": nc.dram_tensor(p + "csq", [128, T], F32, kind="ExternalInput").ap(),
        "snq": nc.dram_tensor(p + "snq", [128, T], F32, kind="ExternalInput").ap(),
        "csk": nc.dram_tensor(p + "csk", [128, T], F32, kind="ExternalInput").ap(),
        "snk": nc.dram_tensor(p + "snk", [128, T], F32, kind="ExternalInput").ap(),
        "bho": nc.dram_tensor(p + "bho", [128, 8 * 16], F32,
                              kind="ExternalInput").ap(),
        "sel": nc.dram_tensor(p + "sel", [16, 8 * 128], F32,
                              kind="ExternalInput").ap(),
        "mw": nc.dram_tensor(p + "mw", [256, 512], F32, kind="ExternalInput").ap(),
        "idin": nc.dram_tensor(p + "idin", [128, 128], F32,
                               kind="ExternalInput").ap(),
        "qk_d": nc.dram_tensor(p + "qk_d", [2 * D, T], BF16, kind="Internal").ap(),
        "v_d": nc.dram_tensor(p + "v_d", [T, D], BF16, kind="Internal").ap(),
    }
    return aps


def emit_l3_ffn(nc, ctx, stack, aps, xld, xst):
    Wqkv, Wo, W1P, W2P = aps["Wqkv"], aps["Wo"], aps["W1P"], aps["W2P"]
    qscale, kscale = aps["qscale"], aps["kscale"]
    ln_g, ln_b, b1, b2 = aps["ln_g"], aps["ln_b"], aps["b1"], aps["b2"]
    csq, snq, csk, snk = aps["csq"], aps["snq"], aps["csk"], aps["snk"]
    bho, sel, mw, idin = aps["bho"], aps["sel"], aps["mw"], aps["idin"]
    qk_d, v_d = aps["qk_d"], aps["v_d"]
    if True:
        sb = _mk_lnbuf(ctx, stack, Tl=512, n_tmp=1)
        sb.ps = [ctx.psum(stack, f"ps{i}", [128, 512], F32) for i in range(8)]
        ident = ctx.sbuf(stack, "ident", [128, 128], F32)
        wsl = [ctx.sbuf(stack, f"wsl{i}", [128, 1024], BF16) for i in range(2)]
        sb.wslab = wsl
        x3 = [ctx.sbuf(stack, f"x3_{c}", [128, T], F32) for c in range(NC)]
        sxb = [ctx.sbuf(stack, f"sxb{c}", [128, T], BF16) for c in range(NC)]
        qtmp = [ctx.sbuf(stack, f"qtmp{c}", [128, T], BF16) for c in range(NC)]
        ao = qtmp
        sqs = ctx.sbuf(stack, "sqs", [128, 512], BF16)
        ss16 = ctx.sbuf(stack, "ss16", [16, T], F32)
        bhot = ctx.sbuf(stack, "bhot", [128, 8 * 16], BF16)
        selt = ctx.sbuf(stack, "selt", [16, 8 * 128], F32)
        rq = ctx.sbuf(stack, "rq", [128, 512], BF16)
        mwt = [ctx.sbuf(stack, f"mwt{i}", [128, 512], F32) for i in range(2)]
        nmax = ctx.sbuf(stack, "nmax", [128, 1], F32)
        bias1 = ctx.sbuf(stack, "bias1", [128, 1], F32)
        zr = ctx.sbuf(stack, "zr", [128, 1], F32)
        hb = [ctx.sbuf(stack, f"hb{i}", [128, 512], BF16) for i in range(L3F_NA)]
        xnb = [ctx.sbuf(stack, f"xnb{i}", [128, 512], BF16) for i in range(NC)]
        gat = ctx.sbuf(stack, "gat", [128, 512], F32)
        cs_t = ctx.sbuf_at("cs_t", [128, T], BF16, hb[0])
        sn_t = ctx.sbuf_at("sn_t", [128, T], BF16, hb[4])
        qn = ctx.sbuf_at("qn", [128, T], BF16, hb[8])
        rot = ctx.sbuf_at("rot", [128, T], BF16, hb[12])
        t1b = ctx.sbuf_at("t1b", [128, T], BF16, hb[16])
        qh = ctx.sbuf_at("qh", [64, T], BF16, hb[0])
        kh = ctx.sbuf_at("kh", [64, T], BF16, hb[4])
        pt2 = ctx.sbuf_at("pt2", [128, 1024], BF16, hb[8])
        smb = ctx.sbuf_at("smb", [128, 512], F32, hb[10])
        vh = [ctx.sbuf_at(f"vh{i}", [128, 64], BF16, hb[12], delta=i * 128)
              for i in range(16)]

        if True:
            emit_const_init(ctx, sb)
            ctx.dma("qm", ident.ap, idin, writes=[ident])
            ctx.dma("qw0", bhot.ap, bho, writes=[bhot])
            ctx.dma("qm", selt.ap, sel, writes=[selt])
            for i in range(2):
                ctx.dma("qm", mwt[i].ap, mw[i * 128:(i + 1) * 128, :], writes=[mwt[i]])
            for c in range(NC):
                xld(ctx, x3[c], c, q=QS_RR[c % 4])
            vc_b1a = sb.valloc(22)
            vc_b1g = sb.valloc(22)
            vc_b2 = sb.valloc(8)
            vc_g = sb.valloc(8)
            vc_bb = sb.valloc(8)
            vc_qs = sb.valloc(1)
            vc_ks = sb.valloc(1)

            def load_lvec(dram_row, off, col0, length):
                nchv = ceil_div(length, 128)
                for j in range(nchv):
                    w = min(128, length - j * 128)
                    ctx.dma("qm", sb.vecs[0:w, col0 + j:col0 + j + 1],
                            dram_row[off + j * 128:off + j * 128 + w, None],
                            writes=[sb.vecs])

            for li in range(L3F_DEPTH):
                load_lvec(b1[li], 0, vc_b1a, L3F_DI)
                load_lvec(b1[li], L3F_DI, vc_b1g, L3F_DI)
                load_lvec(b2[li], 0, vc_b2, D)
                load_lvec(ln_g[li], 0, vc_g, D)
                load_lvec(ln_b[li], 0, vc_bb, D)
                for (vc, sc_d) in ((vc_qs, qscale), (vc_ks, kscale)):
                    ctx.dma("qm", sb.vecs[0:64, vc:vc + 1], sc_d[li][:, None],
                            writes=[sb.vecs])
                    ctx.dma("qm", sb.vecs[64:128, vc:vc + 1], sc_d[li][:, None],
                            writes=[sb.vecs])
                cast_stream(ctx, sb, x3, sxb, shift_chunks=(4, 5, 6, 7))
                for (part, cs_d, sn_d, vcs) in ((0, csq, snq, vc_qs),
                                                (1, csk, snk, vc_ks)):
                    ctx.dma("qw2", cs_t.ap, cs_d, writes=[cs_t])
                    ctx.dma("qw3", sn_t.ap, sn_d, writes=[sn_t])
                    for c in range(NC):
                        oc = part * 8 + c
                        slab = stream_w_slab(ctx, sb, Wqkv[li], c % 2, oc * 128, 128)
                        for tt in range(NT):
                            sl = slice(tt * 512, (tt + 1) * 512)
                            ps = sb.ps[tt % 2]
                            ctx.mm(ps.ap, ps,
                                   [(slab[:, k * 128:(k + 1) * 128], sxb[k][:, sl],
                                     [slab, sxb[k]]) for k in range(NC)])
                            ctx.op("s", lambda eng, ps=ps, c=c, sl=sl: eng.activation(
                                qtmp[c][:, sl], ps.ap, AF.Identity),
                                reads=[ps], writes=[qtmp[c]])
                            ctx.op("s", lambda eng, ps=ps: eng.activation(
                                sqs.ap, ps.ap, AF.Square),
                                reads=[ps], writes=[sqs])
                            ctx.mm(sb.ps[2 + tt][0:16, :], sb.ps[2 + tt],
                                   [(bhot[:, c * 16:(c + 1) * 16], sqs.ap,
                                     [bhot, sqs])],
                                   accumulate=(c > 0), close=(c == NC - 1))
                    for tt in range(NT):
                        ctx.op("v", lambda eng, tt=tt: eng.tensor_copy(
                            ss16[0:16, tt * 512:(tt + 1) * 512],
                            sb.ps[2 + tt][0:16, :]),
                            reads=[sb.ps[2 + tt]], writes=[ss16])
                    ctx.op("s", lambda eng: eng.activation(ss16.ap, ss16.ap, AF.Ln),
                           reads=[ss16], writes=[ss16])
                    ctx.op("s", lambda eng: eng.activation(ss16.ap, ss16.ap, AF.Exp,
                                                           scale=-0.5),
                           reads=[ss16], writes=[ss16])
                    ctx.op("v", lambda eng: eng.tensor_scalar(
                        ss16.ap, ss16.ap, 1e12, None, ALU.min),
                        reads=[ss16], writes=[ss16])
                    for c in range(NC):
                        oc = part * 8 + c
                        for tt in range(NT):
                            sl = slice(tt * 512, (tt + 1) * 512)
                            bc = sb.ps[6]
                            ctx.mm(bc.ap, bc,
                                   [(selt[:, c * 128:(c + 1) * 128],
                                     ss16[0:16, sl],
                                     [selt, ss16])])
                            ctx.op("v", lambda eng, bc=bc, vcs=vcs:
                                   eng.tensor_scalar(
                                       rq.ap, bc.ap,
                                       sb.vcol(vcs), None, ALU.mult),
                                   reads=[bc, sb.vecs], writes=[rq])
                            ctx.op("v", lambda eng, c=c, sl=sl: eng.tensor_mul(
                                qn[:, sl], qtmp[c][:, sl], rq.ap),
                                reads=[qtmp[c], rq], writes=[qn])
                        for (da, sa) in ((0, 32), (32, 0), (64, 96), (96, 64)):
                            ctx.dma("qs2" if da < 64 else "qs3",
                                    rot.h[da:da + 32, :], qn.h[sa:sa + 32, :],
                                    reads=[qn], writes=[rot])
                        ctx.op("v", lambda eng: eng.tensor_mul(t1b.ap, qn.ap, cs_t.ap),
                               reads=[qn, cs_t], writes=[t1b])
                        ctx.op("v", lambda eng: eng.tensor_mul(rot.ap, rot.ap, sn_t.ap),
                               reads=[rot, sn_t], writes=[rot])
                        ctx.op("v", lambda eng: eng.tensor_add(t1b.ap, t1b.ap, rot.ap),
                               reads=[t1b, rot], writes=[t1b])
                        ctx.dma("qo", qk_d[oc * 128:(oc + 1) * 128, :], t1b.ap,
                                reads=[t1b])
                for tt in range(T // 128):
                    for c in range(NC):
                        slab = wsl[c % 2]
                        ctx.dma("qw%d" % (2 + c % 2), slab.ap,
                                Wqkv[li][c * 128:(c + 1) * 128, 2 * D:3 * D],
                                writes=[slab])
                        for nh in range(2):
                            ctx.mm(sb.ps[nh].ap, sb.ps[nh],
                                   [(sxb[c][:, tt * 128:(tt + 1) * 128],
                                     slab[:, nh * 512:(nh + 1) * 512],
                                     [sxb[c], slab])],
                                   accumulate=(c > 0), close=(c == NC - 1))
                    for nh in range(2):
                        ctx.op("s", lambda eng, nh=nh: eng.activation(
                            t1b[:, nh * 512:(nh + 1) * 512], sb.ps[nh].ap,
                            AF.Identity),
                            reads=[sb.ps[nh]], writes=[t1b])
                    ctx.dma("qo", v_d[tt * 128:(tt + 1) * 128, :], t1b[:, 0:D],
                            reads=[t1b])
                ctx.fence()
                for h in range(16):
                    ctx.dma("qs", qh.ap, qk_d[h * 64:(h + 1) * 64, :], writes=[qh])
                    ctx.dma("qs1", kh.ap, qk_d[D + h * 64:D + (h + 1) * 64, :],
                            writes=[kh])
                    for kc in range(16):
                        ctx.dma("qs2", vh[kc].ap,
                                v_d[kc * 128:(kc + 1) * 128, h * 64:(h + 1) * 64],
                                writes=[vh[kc]])
                    for w in range(8):
                        kw = 256 if w == 0 else 512
                        k0 = 0 if w == 0 else (w - 1) * 256
                        for qt2 in range(2):
                            qlo = w * 256 + qt2 * 128
                            ps = sb.ps[qt2]
                            ctx.mm(ps[:, 0:kw], ps,
                                   [(qh[:, qlo:qlo + 128], kh[:, k0:k0 + kw],
                                     [qh, kh])])
                            mtb = mwt[qt2]
                            mtap = mtb[:, 256:512] if w == 0 else mtb[:, 0:512]
                            ctx.op("v", lambda eng, ps=ps, mtap=mtap, kw=kw:
                                   eng.tensor_add(smb[:, 0:kw], ps[:, 0:kw],
                                                  mtap),
                                   reads=[ps, mtb], writes=[smb])
                            ctx.op("v", lambda eng, kw=kw: eng.reduce_max(
                                nmax[:, 0:1], smb[:, 0:kw], AX.X, negate=True),
                                reads=[smb], writes=[nmax])
                            ctx.op("v", lambda eng: eng.tensor_scalar(
                                bias1.ap, nmax.ap, 8.0, None, ALU.mult),
                                reads=[nmax], writes=[bias1])
                            ctx.op("s", lambda eng, kw=kw: eng.activation(
                                smb[:, 0:kw], smb[:, 0:kw], AF.Exp,
                                bias=bias1[:, 0:1], scale=8.0,
                                accum_out=zr[:, 0:1]),
                                reads=[smb, bias1], writes=[smb, zr])
                            ctx.op("v", lambda eng: eng.reciprocal(zr.ap, zr.ap),
                                   reads=[zr], writes=[zr])
                            ctx.op("v", lambda eng, kw=kw: eng.tensor_scalar(
                                smb[:, 0:kw], smb[:, 0:kw], zr[:, 0:1], None,
                                ALU.mult),
                                reads=[smb, zr], writes=[smb])
                            for kc2 in range(kw // 128):
                                pt = sb.ps[7]
                                ctx.op("t", lambda eng, kc2=kc2: eng.transpose(
                                    pt[0:128, 0:128],
                                    smb[:, kc2 * 128:(kc2 + 1) * 128], ident.ap),
                                    reads=[smb, ident], writes=[pt])
                                ctx.op("v", lambda eng, kc2=kc2, qt2=qt2:
                                       eng.tensor_copy(
                                           pt2[:, kc2 * 256 + qt2 * 128:
                                               kc2 * 256 + (qt2 + 1) * 128],
                                           pt[0:128, 0:128]),
                                       reads=[pt], writes=[pt2])
                        nkc = kw // 128
                        for kc2 in range(nkc):
                            kcg = (k0 // 128) + kc2
                            ctx.mm(sb.ps[4][0:64, 0:256], sb.ps[4],
                                   [(vh[kcg].ap, pt2[:, kc2 * 256:(kc2 + 1) * 256],
                                     [vh[kcg], pt2])],
                                   accumulate=(kc2 > 0), close=(kc2 == nkc - 1))
                        ctx.op("s", lambda eng, h=h, w=w: eng.activation(
                            ao[h // 2][(h % 2) * 64:(h % 2) * 64 + 64,
                                       w * 256:(w + 1) * 256],
                            sb.ps[4][0:64, 0:256], AF.Identity),
                            reads=[sb.ps[4]], writes=[ao[h // 2]])
                ctx.fence()
                for tt in range(NT):
                    sl = slice(tt * 512, (tt + 1) * 512)
                    for oc in range(NC):
                        slab = stream_w_slab(ctx, sb, Wo[li], oc % 2, oc * 128, 128)
                        ps = sb.ps[oc % 4]
                        ctx.mm(ps.ap, ps,
                               [(slab[:, k * 128:(k + 1) * 128], ao[k][:, sl],
                                 [slab, ao[k]]) for k in range(NC)])
                        ctx.op("v", lambda eng, oc=oc, ps=ps, sl=sl: eng.tensor_add(
                            x3[oc][:, sl], x3[oc][:, sl], ps.ap),
                            reads=[ps, x3[oc]], writes=[x3[oc]])
                ctx.fence()
                cast_stream(ctx, sb, x3, sxb, shift_chunks=(4, 5, 6, 7))
                for tt in range(NT):
                    sl = slice(tt * 512, (tt + 1) * 512)
                    emit_ln_fm(ctx, sb, sxb, xnb, 1e-5, gamma_col=vc_g,
                               beta_col=vc_bb, src_bf16=True, Tl=512,
                               src_off=tt * 512, dst_off=0)
                    for j in range(L3F_NA):
                        wj = L3F_WIDTHS[j]
                        slab_a = wsl[0]
                        slab_g = wsl[1]
                        ctx.dma("qw0", slab_a.ap, W1P[li, j], writes=[slab_a])
                        ctx.dma("qw1", slab_g.ap, W1P[li, L3F_NA + j],
                                writes=[slab_g])
                        psa = sb.ps[0]
                        psg = sb.ps[1]
                        ctx.mm(psa.ap, psa,
                               [(slab_a[:, k * 128:(k + 1) * 128], xnb[k].ap,
                                 [slab_a, xnb[k]]) for k in range(NC)])
                        ctx.mm(psg.ap, psg,
                               [(slab_g[:, k * 128:(k + 1) * 128], xnb[k].ap,
                                 [slab_g, xnb[k]]) for k in range(NC)])
                        ctx.op("v", lambda eng, psa=psa, j=j, wj=wj: eng.tensor_scalar(
                            gat[0:wj, :], psa[0:wj, :], sb.vcol(vc_b1a + j)[0:wj, :],
                            None, ALU.add),
                            reads=[psa, sb.vecs], writes=[gat])
                        ctx.op("s", lambda eng, psg=psg, j=j, wj=wj: eng.activation(
                            sqs[0:wj, :], psg[0:wj, :], AF.Gelu,
                            bias=sb.vcol(vc_b1g + j)[0:wj, :]),
                            reads=[psg, sb.vecs], writes=[sqs])
                        ctx.op("v", lambda eng, j=j, wj=wj: eng.tensor_mul(
                            hb[j][0:wj, :], gat[0:wj, :], sqs[0:wj, :]),
                            reads=[gat, sqs], writes=[hb[j]])
                    for j in range(L3F_NA):
                        wj = L3F_WIDTHS[j]
                        slab = wsl[j % 2]
                        ctx.dma("qw%d" % (2 + j % 2), slab.ap,
                                W2P[li, j * 128:(j + 1) * 128, :], writes=[slab])
                        for dc in range(NC):
                            ctx.mm(sb.ps[dc].ap, sb.ps[dc],
                                   [(slab[0:wj, dc * 128:(dc + 1) * 128],
                                     hb[j][0:wj, :], [slab, hb[j]])],
                                   accumulate=(j > 0), close=(j == L3F_NA - 1))
                    for dc in range(NC):
                        ctx.op("v", lambda eng, dc=dc, sl=sl: eng.scalar_tensor_tensor(
                            x3[dc][:, sl], sb.ps[dc].ap, sb.vcol(vc_b2 + dc),
                            x3[dc][:, sl], ALU.add, ALU.add),
                            reads=[sb.ps[dc], sb.vecs, x3[dc]], writes=[x3[dc]])
                ctx.fence()
            xnew = [ctx.sbuf_at(f"xnew{i}", [128, 512], F32, hb[0], delta=i * 2048)
                    for i in range(NC)]
            xr = ctx.sbuf_at("xr", [128, 512], F32, hb[16])
            ctx.fence()
            for tt in range(NT):
                sl = slice(tt * 512, (tt + 1) * 512)
                for oc in range(NC):
                    xld(ctx, xr, oc, col0=tt * 512, ncols=512, dcol0=0,
                        q=QS_RR[oc % 2])
                    ctx.op("v", lambda eng, oc=oc, sl=sl: eng.tensor_add(
                        xnew[oc].ap, x3[oc][:, sl], xr.ap),
                        reads=[x3[oc], xr], writes=[xnew[oc]])
                emit_ln_fm(ctx, sb, xnew, xnew, 1e-5, Tl=512)
                for oc in range(NC):
                    xst(ctx, xnew[oc], oc, col0=tt * 512, ncols=512, scol0=0)


def build_l3_ffn():
    nc = bass.Bass("TRN2", target_bir_lowering=False)
    aps = declare_l3_ffn(nc)
    xin = nc.dram_tensor("xin", [D, T], F32, kind="ExternalInput").ap()
    xout = nc.dram_tensor("xout", [D, T], F32, kind="ExternalOutput").ap()
    with contextlib.ExitStack() as stack:
        ctx = Ctx(nc)
        ctx.make_sems(stack)
        with nc.Block() as block:
            emit_l3_ffn(nc, ctx, stack, aps, plain_ld(xin), plain_st(xout))
            ctx.finish()
            ctx.build(block)
    return nc


def _l3f_rotary_tables():
    DIM_HEAD = 64
    WINDOW = 256
    n = T
    half = np.arange(0, DIM_HEAD, 2, dtype=np.float64)
    inv_freq = 1.0 / 10000.0 ** (half / DIM_HEAD)
    t = np.arange(n, dtype=np.float64)
    freqs = t[:, None] * inv_freq[None, :]
    freqs = np.concatenate([freqs, freqs], -1)
    base_scale = (half + 0.4 * DIM_HEAD) / (1.4 * DIM_HEAD)
    power = (t - n // 2) / (WINDOW // 2)
    scale = base_scale[None, :] ** power[:, None]
    scale = np.concatenate([scale, scale], -1)
    cos = np.cos(freqs)
    sin = np.sin(freqs)
    sgn = np.where(np.arange(DIM_HEAD) < 32, -1.0, 1.0)[None, :]
    csq_ = (cos * scale).T
    snq_ = (sin * scale * sgn).T
    csk_ = (cos / scale).T
    snk_ = (sin / scale * sgn).T
    stack2 = lambda a: np.ascontiguousarray(
        np.concatenate([a, a], axis=0), dtype=np.float32)
    return stack2(csq_), stack2(snq_), stack2(csk_), stack2(snk_)


def _weights_l3_ffn(ins):
    csq_, snq_, csk_, snk_ = _l3f_rotary_tables()
    bho = np.zeros((128, 8 * 16), dtype=np.float32)
    sel = np.zeros((16, 8 * 128), dtype=np.float32)
    for c in range(8):
        bho[0:64, c * 16 + 2 * c] = 1.0
        bho[64:128, c * 16 + 2 * c + 1] = 1.0
        sel[2 * c, c * 128:c * 128 + 64] = 1.0
        sel[2 * c + 1, c * 128 + 64:(c + 1) * 128] = 1.0
    i_ = np.arange(256)[:, None]
    jj = np.arange(256)[None, :]
    m0 = np.where(jj <= i_, 0.0, -1e30).astype(np.float32)
    j_ = np.arange(512)[None, :]
    mwv = np.where((j_ >= i_) & (j_ <= i_ + 256), 0.0, -1e30).astype(np.float32)
    W1 = np.asarray(ins["L3_ffn_W1"], dtype=np.float32)
    W1X = np.zeros((L3F_DEPTH, 1024, 2 * L3F_NA * 128), dtype=np.float32)
    for b in range(L3F_NA):
        wj = L3F_WIDTHS[b]
        W1X[:, :, b * 128:b * 128 + wj] = W1[:, :, b * 128:b * 128 + wj]
        W1X[:, :, (L3F_NA + b) * 128:(L3F_NA + b) * 128 + wj] = \
            W1[:, :, L3F_DI + b * 128:L3F_DI + b * 128 + wj]
    W1Pv = np.stack([pack_lhsT(W1X[l]) for l in range(L3F_DEPTH)], axis=0)
    W2 = np.asarray(ins["L3_ffn_W2"], dtype=np.float32)
    W2Pv = np.zeros((L3F_DEPTH, L3F_NA * 128, D), dtype=np.float32)
    W2Pv[:, 0:L3F_DI, :] = W2
    return {
        "Wqkv": _f32c(ins["L3_ffn_Wqkv"]), "Wo": _f32c(ins["L3_ffn_Wo"]),
        "W1P": W1Pv, "W2P": W2Pv,
        "qscale": _f32c(ins["L3_ffn_q_scale"]), "kscale": _f32c(ins["L3_ffn_k_scale"]),
        "ln_g": _f32c(ins["L3_ffn_ln_g"]), "ln_b": _f32c(ins["L3_ffn_ln_b"]),
        "b1": _f32c(ins["L3_ffn_b1"]), "b2": _f32c(ins["L3_ffn_b2"]),
        "csq": csq_, "snq": snq_, "csk": csk_, "snk": snk_,
        "bho": bho, "sel": sel, "mw": mwv,
        "idin": np.eye(128, dtype=np.float32),
    }


def run_l3_ffn(x_fm_cores, ins):
    return _run_block("l3_ffn", build_l3_ffn, x_fm_cores, ins)




def declare_l2_attn(nc, p=""):
    return {
        "winT": nc.dram_tensor(p + "winT", [D, 3 * D], F32,
                               kind="ExternalInput").ap(),
        "b_in": nc.dram_tensor(p + "b_in", [3 * D], F32, kind="ExternalInput").ap(),
        "kde": nc.dram_tensor(p + "kde", [11, 64], F32, kind="ExternalInput").ap(),
        "vde": nc.dram_tensor(p + "vde", [11, 64], F32, kind="ExternalInput").ap(),
        "woT": nc.dram_tensor(p + "woT", [D, D], F32, kind="ExternalInput").ap(),
        "b_out": nc.dram_tensor(p + "b_out", [D], F32, kind="ExternalInput").ap(),
        "bho": nc.dram_tensor(p + "bho", [128, 8 * 16], F32,
                              kind="ExternalInput").ap(),
        "sel": nc.dram_tensor(p + "sel", [16, 8 * 128], F32,
                              kind="ExternalInput").ap(),
        "idin": nc.dram_tensor(p + "idin", [128, 128], F32,
                               kind="ExternalInput").ap(),
    }


def emit_l2_attn(nc, ctx, stack, aps, xld, xst):
    S8 = 8
    B256 = 256
    winT, b_in, kde, vde = aps["winT"], aps["b_in"], aps["kde"], aps["vde"]
    woT, b_out, bho, sel, idin = (aps["woT"], aps["b_out"], aps["bho"],
                                  aps["sel"], aps["idin"])
    if True:
        sb = _mk_lnbuf(ctx, stack, Tl=512, n_tmp=1)
        sb.ps = [ctx.psum(stack, f"ps{i}", [128, 512], F32) for i in range(8)]
        ident = ctx.sbuf(stack, "ident", [128, 128], F32)
        wsl = [ctx.sbuf(stack, f"wsl{i}", [128, 1024], BF16) for i in range(2)]
        sb.wslab = wsl
        xb = [ctx.sbuf(stack, f"xb{c}", [128, T], BF16) for c in range(NC)]
        qc = [ctx.sbuf(stack, f"qc{c}", [128, T], BF16) for c in range(NC)]
        kc = [ctx.sbuf(stack, f"kc{c}", [128, T], BF16) for c in range(NC)]
        vc = [ctx.sbuf(stack, f"vc{c}", [128, T], BF16) for c in range(NC)]
        kdt = ctx.sbuf(stack, "kdt", [128, 11], F32)
        vdt = ctx.sbuf(stack, "vdt", [128, 11], F32)
        bhot = ctx.sbuf(stack, "bhot", [128, 8 * 16], BF16)
        selt = ctx.sbuf(stack, "selt", [16, 8 * 128], F32)
        Lb = [ctx.sbuf(stack, f"Lb{i}", [128, 1024], F32) for i in range(2)]
        Pb = [ctx.sbuf(stack, f"Pb{i}", [128, 1024], F32) for i in range(2)]
        gm = ctx.sbuf(stack, "gm", [128, 512], F32)
        gz = ctx.sbuf(stack, "gz", [128, 128], F32)
        pt16 = ctx.sbuf(stack, "pt16", [16, 256], F32)
        vpl = ctx.sbuf(stack, "vpl", [128, 256], BF16)
        tmp1 = ctx.sbuf(stack, "tmp1", [128, 256], F32)
        tmp1b = ctx.sbuf(stack, "tmp1b", [128, 128], BF16)
        scr16 = ctx.sbuf(stack, "scr16", [128, 16], F32)
        out1 = [ctx.sbuf_at(f"out1_{c}", [128, T], F32,
                            qc[2 * c] if c < 4 else kc[2 * (c - 4)])
                for c in range(NC)]
        ob = [ctx.sbuf_at(f"ob{c}", [128, T], BF16, vc[c]) for c in range(NC)]
        xnew = [ctx.sbuf_at(f"xnew{i}", [128, 512], F32, Lb[0], delta=i * 2048)
                for i in range(NC)]
        xr = ctx.sbuf_at("xr", [128, 512], F32, gm)

        if True:
            emit_const_init(ctx, sb)
            ctx.dma("qm", ident.ap, idin, writes=[ident])
            ctx.dma("qw0", bhot.ap, bho, writes=[bhot])
            ctx.dma("qm", selt.ap, sel, writes=[selt])
            for j in range(11):
                for half in range(2):
                    ctx.dma("qm", kdt[half * 64:(half + 1) * 64, j:j + 1],
                            kde[j][:, None], writes=[kdt])
                    ctx.dma("qm", vdt[half * 64:(half + 1) * 64, j:j + 1],
                            vde[j][:, None], writes=[vdt])
            bic = load_vec(ctx, sb, b_in, 3 * D)
            boc = load_vec(ctx, sb, b_out, D)
            ctx.op("v", lambda eng: eng.tensor_scalar(
                sb.vecs[:, bic:bic + 8], sb.vecs[:, bic:bic + 8], 0.125, None,
                ALU.mult),
                reads=[sb.vecs], writes=[sb.vecs])
            with contextlib.ExitStack() as sA:
                xtmp = ctx.sbuf(sA, "xtmp0", [128, T], F32)
                for c in range(NC):
                    xld(ctx, xtmp, c, q=QS_RR[c % 2])
                    ctx.op("v", lambda eng, a=xb[c], b=xtmp: eng.tensor_copy(a.ap, b.ap),
                           reads=[xtmp], writes=[xb[c]])
            ctx.fence()
            for grp, dst in ((0, qc), (1, kc), (2, vc)):
                for c in range(NC):
                    oc = grp * 8 + c
                    slab = stream_w_slab(ctx, sb, winT, c % 2, oc * 128, 128)
                    for tt in range(NT):
                        sl = slice(tt * 512, (tt + 1) * 512)
                        ps = sb.ps[tt % 2]
                        ctx.mm(ps.ap, ps,
                               [(slab[:, k * 128:(k + 1) * 128], xb[k][:, sl],
                                 [slab, xb[k]]) for k in range(NC)])
                        sc = 0.125 if grp == 0 else 1.0
                        ctx.op("s", lambda eng, ps=ps, dst=dst, c=c, sl=sl, sc=sc,
                               oc=oc: eng.activation(
                                   dst[c][:, sl], ps.ap, AF.Identity,
                                   bias=sb.vcol(bic + oc), scale=sc),
                               reads=[ps, sb.vecs], writes=[dst[c]])
            for bh in range(2):
                for s in range(S8):
                    for t in range(S8):
                        dist = abs(s - t)
                        p16 = sb.ps[4]
                        for c in range(NC):
                            ksl = kc[c][:, t * B256 + bh * 128:
                                        t * B256 + (bh + 1) * 128]
                            qsl = qc[c][:, s * B256 + bh * 128:
                                        s * B256 + (bh + 1) * 128]
                            ctx.op("v", lambda eng, ksl=ksl, c=c, dist=dist:
                                   eng.tensor_scalar(
                                       vpl[:, 0:128], ksl,
                                       kdt[:, dist:dist + 1], None, ALU.add),
                                   reads=[kc[c], kdt], writes=[vpl])
                            ctx.op("v", lambda eng, qsl=qsl, c=c: eng.tensor_mul(
                                tmp1b.ap, qsl, vpl[:, 0:128]),
                                reads=[qc[c], vpl], writes=[tmp1b])
                            ctx.mm(p16[0:16, 0:128], p16,
                                   [(bhot[:, c * 16:(c + 1) * 16],
                                     tmp1b.ap, [bhot, tmp1b])],
                                   accumulate=(c > 0), close=(c == NC - 1))
                        ctx.op("v", lambda eng, p16=p16: eng.tensor_copy(
                            pt16[0:16, 0:128], p16[0:16, 0:128]),
                            reads=[p16], writes=[pt16])
                        pt = sb.ps[5]
                        ctx.op("t", lambda eng: eng.transpose(
                            pt[0:128, 0:16], pt16[0:16, 0:128], ident[0:16, 0:16]),
                            reads=[pt16, ident], writes=[pt])
                        ctx.op("v", lambda eng, bh=bh, s=s, t=t, pt=pt: eng.tensor_copy(
                            bass.AP(Lb[bh].h, s * 128 + t, [[1024, 128], [8, 16]]),
                            pt[0:128, 0:16]),
                            reads=[pt], writes=[Lb[bh]])
                L = Lb[bh]
                P = Pb[bh]
                ctx.op("v", lambda eng, L=L: eng.tensor_max(
                    gm[:, 0:512], bass.AP(L.h, 0, [[1024, 128], [2, 512]]),
                    bass.AP(L.h, 1, [[1024, 128], [2, 512]])),
                    reads=[L], writes=[gm])
                ctx.op("v", lambda eng: eng.tensor_max(
                    gm[:, 0:256], bass.AP(gm.h, 0, [[512, 128], [2, 256]]),
                    bass.AP(gm.h, 1, [[512, 128], [2, 256]])),
                    reads=[gm], writes=[gm])
                ctx.op("v", lambda eng: eng.tensor_max(
                    gz[:, 0:128], bass.AP(gm.h, 0, [[512, 128], [2, 128]]),
                    bass.AP(gm.h, 1, [[512, 128], [2, 128]])),
                    reads=[gm], writes=[gz])
                ctx.op("v", lambda eng, L=L, P=P: eng.tensor_sub(
                    P.ap, L.ap,
                    bass.AP(gz.h, 0, [[128, 128], [1, 128], [0, 8]])),
                    reads=[L, gz], writes=[P])
                ctx.op("s", lambda eng, P=P: eng.activation(P.ap, P.ap, AF.Exp),
                       reads=[P], writes=[P])
                ctx.op("v", lambda eng, P=P: eng.tensor_add(
                    gm[:, 0:512], bass.AP(P.h, 0, [[1024, 128], [2, 512]]),
                    bass.AP(P.h, 1, [[1024, 128], [2, 512]])),
                    reads=[P], writes=[gm])
                ctx.op("v", lambda eng: eng.tensor_add(
                    gm[:, 0:256], bass.AP(gm.h, 0, [[512, 128], [2, 256]]),
                    bass.AP(gm.h, 1, [[512, 128], [2, 256]])),
                    reads=[gm], writes=[gm])
                ctx.op("v", lambda eng: eng.tensor_add(
                    gz[:, 0:128], bass.AP(gm.h, 0, [[512, 128], [2, 128]]),
                    bass.AP(gm.h, 1, [[512, 128], [2, 128]])),
                    reads=[gm], writes=[gz])
                ctx.op("v", lambda eng: eng.reciprocal(gz[:, 0:128], gz[:, 0:128]),
                       reads=[gz], writes=[gz])
                ctx.op("v", lambda eng, P=P: eng.tensor_mul(
                    P.ap, P.ap,
                    bass.AP(gz.h, 0, [[128, 128], [1, 128], [0, 8]])),
                    reads=[P, gz], writes=[P])
            ctx.fence()
            for c in range(NC):
                ctx.op("v", lambda eng, c=c: eng.memset(out1[c].ap, 0.0),
                       writes=[out1[c]])
            for bh in range(2):
                for s in range(S8):
                    for t in range(S8):
                        dist = abs(s - t)
                        ctx.op("v", lambda eng, bh=bh, s=s, t=t: eng.tensor_copy(
                            scr16.ap,
                            bass.AP(Pb[bh].h, s * 128 + t, [[1024, 128], [8, 16]])),
                            reads=[Pb[bh]], writes=[scr16])
                        pt = sb.ps[5]
                        ctx.op("t", lambda eng: eng.transpose(
                            pt[0:16, 0:128], scr16.ap, ident.ap),
                            reads=[scr16, ident], writes=[pt])
                        ctx.op("v", lambda eng, pt=pt: eng.tensor_copy(
                            pt16[0:16, 0:128], pt[0:16, 0:128]),
                            reads=[pt], writes=[pt16])
                        for c in range(NC):
                            bc = sb.ps[6]
                            ctx.mm(bc[:, 0:128], bc,
                                   [(selt[:, c * 128:(c + 1) * 128],
                                     pt16[0:16, 0:128], [selt, pt16])])
                            vsl = vc[c][:, t * B256 + bh * 128:
                                        t * B256 + (bh + 1) * 128]
                            ctx.op("v", lambda eng, vsl=vsl, c=c, dist=dist:
                                   eng.tensor_scalar(
                                       vpl[:, 0:128], vsl,
                                       vdt[:, dist:dist + 1], None, ALU.add),
                                   reads=[vc[c], vdt], writes=[vpl])
                            ctx.op("v", lambda eng, bc=bc: eng.tensor_mul(
                                tmp1[:, 0:128], vpl[:, 0:128], bc[:, 0:128]),
                                reads=[vpl, bc], writes=[tmp1])
                            osl = slice(s * B256 + bh * 128,
                                        s * B256 + (bh + 1) * 128)
                            ctx.op("v", lambda eng, c=c, osl=osl: eng.tensor_add(
                                out1[c][:, osl], out1[c][:, osl], tmp1[:, 0:128]),
                                reads=[out1[c], tmp1], writes=[out1[c]])
            ctx.fence()
            for c in range(NC):
                ctx.op("v", lambda eng, c=c: eng.tensor_copy(ob[c].ap, out1[c].ap),
                       reads=[out1[c]], writes=[ob[c]])
            ctx.fence()
            for tt in range(NT):
                sl = slice(tt * 512, (tt + 1) * 512)
                for oc in range(NC):
                    slab = stream_w_slab(ctx, sb, woT, oc % 2, oc * 128, 128)
                    ps = sb.ps[oc % 4]
                    ctx.mm(ps.ap, ps,
                           [(slab[:, k * 128:(k + 1) * 128], ob[k][:, sl],
                             [slab, ob[k]]) for k in range(NC)])
                    xld(ctx, xr, oc, col0=tt * 512, ncols=512, dcol0=0,
                        q=QS_RR[2 + oc % 2])
                    ctx.op("v", lambda eng, oc=oc, ps=ps: eng.tensor_scalar(
                        xnew[oc].ap, ps.ap, 1.0, sb.vcol(boc + oc), ALU.mult,
                        ALU.add),
                        reads=[ps, sb.vecs], writes=[xnew[oc]])
                    ctx.op("v", lambda eng, oc=oc: eng.tensor_add(
                        xnew[oc].ap, xnew[oc].ap, xr.ap),
                        reads=[xnew[oc], xr], writes=[xnew[oc]])
                emit_ln_fm(ctx, sb, xnew, xnew, 1e-5, Tl=512)
                for oc in range(NC):
                    xst(ctx, xnew[oc], oc, col0=tt * 512, ncols=512, scol0=0)


def build_l2_attn():
    nc = bass.Bass("TRN2", target_bir_lowering=False)
    aps = declare_l2_attn(nc)
    xin = nc.dram_tensor("xin", [D, T], F32, kind="ExternalInput").ap()
    xout = nc.dram_tensor("xout", [D, T], F32, kind="ExternalOutput").ap()
    with contextlib.ExitStack() as stack:
        ctx = Ctx(nc)
        ctx.make_sems(stack)
        with nc.Block() as block:
            emit_l2_attn(nc, ctx, stack, aps, plain_ld(xin), plain_st(xout))
            ctx.finish()
            ctx.build(block)
    return nc


def _weights_l2_attn(ins):
    bho = np.zeros((128, 8 * 16), dtype=np.float32)
    sel = np.zeros((16, 8 * 128), dtype=np.float32)
    for c in range(8):
        bho[0:64, c * 16 + 2 * c] = 1.0
        bho[64:128, c * 16 + 2 * c + 1] = 1.0
        sel[2 * c, c * 128:c * 128 + 64] = 1.0
        sel[2 * c + 1, c * 128 + 64:(c + 1) * 128] = 1.0
    return {
        "winT": _f32c(np.asarray(ins["L2_attn_W_in"]).T),
        "b_in": _f32c(ins["L2_attn_b_in"]),
        "kde": _f32c(ins["L2_attn_k_dist_emb"]),
        "vde": _f32c(ins["L2_attn_v_dist_emb"]),
        "woT": _f32c(np.asarray(ins["L2_attn_W_out"]).T),
        "b_out": _f32c(ins["L2_attn_b_out"]),
        "bho": bho, "sel": sel, "idin": np.eye(128, dtype=np.float32),
    }


def run_l2_attn(x_fm_cores, ins):
    return _run_block("l2_attn", build_l2_attn, x_fm_cores, ins)


_BUILT = {}


def _get(name, builder):
    if name not in _BUILT:
        _BUILT[name] = builder()
    return _BUILT[name]


def _f32c(a):
    return np.ascontiguousarray(a, dtype=np.float32)


def pack_lhsT(Wt):
    K, M = Wt.shape
    nk, nm = K // 128, M // 128
    out = Wt.reshape(nk, 128, nm, 128).transpose(2, 1, 0, 3).reshape(nm, 128, nk * 128)
    return np.ascontiguousarray(out, dtype=np.float32)


def _weights_l0_attn(ins):
    base = {}
    for short, full in [("Wg", "Wg"), ("Wi", "Wi"), ("Wfg", "Wfg"), ("Wfu", "Wfu"),
                        ("Wfd", "Wfd"), ("Wo", "Wo"), ("bg", "bg"), ("bi", "bi"),
                        ("A", "A"), ("Bp", "Bp"), ("Cp", "Cp"), ("Dp", "Dp"),
                        ("gamma", "gamma"), ("beta", "beta"), ("bfg", "bfg"),
                        ("bfu", "bfu"), ("bfd", "bfd"), ("bo", "bo")]:
        base[short] = _f32c(ins["L0_attn_" + full])
    return base


def _weights_l0_ffn(ins):
    return {
        "W1T": _f32c(np.asarray(ins["L0_ffn_W1"]).T),
        "W2T": _f32c(np.asarray(ins["L0_ffn_W2"]).T),
        "b1": _f32c(ins["L0_ffn_b1"]), "b2": _f32c(ins["L0_ffn_b2"]),
    }


IN_MAPS = {"l0_attn": _weights_l0_attn, "l0_ffn": _weights_l0_ffn,
           "l1_attn": _weights_l1_attn, "l1_ffn": _weights_l1_ffn,
           "l2_attn": _weights_l2_attn,
           "l2_ffn": _weights_l2_ffn, "l3_attn": _weights_l3_attn,
           "l3_ffn": _weights_l3_ffn}


def _run_block(name, builder, x_fm_cores, ins, wkey=None):
    nc = _get(name, builder)
    base = IN_MAPS[wkey or name](ins)
    in_maps = [dict(base, xin=_f32c(x_fm_cores[c])) for c in range(NCORES)]
    res = run_bass_kernel_spmd(nc, in_maps, core_ids=list(range(NCORES)))
    return [r["xout"] for r in res.results]


def run_l0_attn(x_fm_cores, ins):
    return _run_block("l0_attn", build_l0_attn, x_fm_cores, ins)


def _to_l2_layout(x_fm):
    F = np.stack([np.asarray(f).T for f in x_fm], axis=0)
    out = []
    for c in range(NCORES):
        sl = F[:, c * 256:(c + 1) * 256, :]
        out.append(np.ascontiguousarray(sl.reshape(8 * 256, D).T))
    return out


def _from_l2_layout(cores):
    F = np.zeros((NCORES, T, D), dtype=np.float32)
    for c in range(NCORES):
        sl = np.asarray(cores[c]).T.reshape(8, 256, D)
        F[:, c * 256:(c + 1) * 256, :] = sl
    return [np.ascontiguousarray(F[b].T) for b in range(NCORES)]



FUSE_SPECS = [
    ("l0a_", declare_l0_attn, emit_l0_attn, "_weights_l0_attn"),
    ("l0f_", declare_l0_ffn, emit_l0_ffn, "_weights_l0_ffn"),
    ("l1a_", declare_l1_attn, emit_l1_attn, "_weights_l1_attn"),
    ("l1f_", declare_l1_ffn, emit_l1_ffn, "_weights_l1_ffn"),
    ("l2a_", declare_l2_attn, emit_l2_attn, "_weights_l2_attn"),
    ("l2f_", declare_l2_ffn, emit_l2_ffn, "_weights_l2_ffn"),
    ("l3a_", declare_l3_attn, emit_l3_attn, "_weights_l3_attn"),
    ("l3f_", declare_l3_ffn, emit_l3_ffn, "_weights_l3_ffn"),
]


def build_fused(nb=8):
    nc = bass.Bass("TRN2", target_bir_lowering=False)
    xin0 = nc.dram_tensor("xin", [D, T], F32, kind="ExternalInput").ap()
    bnd = []
    for i in range(nb):
        last = (i == nb - 1)
        kind = "ExternalOutput" if last else "Internal"
        name = "xout" if last else f"hb{i}"
        shape = [8 * D, 256] if i in (3, 5) else [D, T]
        bnd.append(nc.dram_tensor(name, shape, F32, kind=kind).ap())
    a2a1_out = (nc.dram_tensor("a2a1_out", [8 * D, 256], F32, kind="Internal").ap()
                if nb > 4 else None)
    a2a2_out = (nc.dram_tensor("a2a2_out", [8 * D, 256], F32, kind="Internal").ap()
                if nb > 6 else None)
    with contextlib.ExitStack() as stack:
        ctx = Ctx(nc)
        ctx.make_sems(stack)
        cc1 = stack.enter_context(nc.semaphore("cc1"))
        cc2 = stack.enter_context(nc.semaphore("cc2"))
        tiny = ctx.sbuf(stack, "cc_tiny", [1, 8], F32)
        with nc.Block() as block:
            for i in range(nb):
                p, dec, emit, _w = FUSE_SPECS[i]
                aps = dec(nc, p)
                if i == 0:
                    ld = plain_ld(xin0)
                elif i == 4:
                    ld = a2a_ld(a2a1_out, "a2a1_out")
                elif i == 6:
                    ld = a2a_ld(a2a2_out, "a2a2_out")
                else:
                    ld = plain_ld(bnd[i - 1])
                mk_st = a2a_st if i in (3, 5) else plain_st
                st = mk_st(bnd[i], f"st{i}")
                ctx.prefix = p
                with contextlib.ExitStack() as bs:
                    emit(nc, ctx, bs, aps, ld, st)
                ctx.prefix = ""
                ctx.fence()
                if i == 3 and nb > 4:
                    emit_a2a(ctx, cc1, bnd[3], a2a1_out, "st3", "a2a1_out", tiny)
                if i == 5 and nb > 6:
                    emit_a2a(ctx, cc2, bnd[5], a2a2_out, "st5", "a2a2_out", tiny)
            ctx.finish()
            ctx.build(block)
    return nc


def fused_in_maps(inputs, x):
    base = {}
    for p, _dec, _emit, wname in FUSE_SPECS:
        for k, v in globals()[wname](inputs).items():
            base[p + k] = v
    return [dict(base, xin=np.ascontiguousarray(x[c].T)) for c in range(NCORES)]


def kernel(**inputs):
    x = np.asarray(inputs["L0_attn_x"], dtype=np.float32)
    nc = _get("fused", build_fused)
    in_maps = fused_in_maps(inputs, x)
    res = run_bass_kernel_spmd(nc, in_maps, core_ids=list(range(NCORES)))
    return np.stack([np.ascontiguousarray(np.asarray(r["xout"]).T)
                     for r in res.results], axis=0)


# version 03
# speedup vs baseline: 1.2606x; 1.2606x over previous
import math
import contextlib
import numpy as np
import ml_dtypes
import concourse.bass as bass
import concourse.mybir as mybir
from concourse.bass_utils import run_bass_kernel_spmd

F32 = mybir.dt.float32
BF16 = mybir.dt.bfloat16
AF = mybir.ActivationFunctionType
ALU = mybir.AluOpType
AX = mybir.AxisListType

NCORES = 8
D = 1024
T = 2048
NC = D // 128
NT = T // 512

ENGINES = ("t", "v", "s", "p")
QUEUES = ("qs", "qs1", "qs2", "qs3", "qo", "qw0", "qw1", "qw2", "qw3",
          "qm", "qa")
Q_ISSUER = {"qs": "sync", "qs1": "sync", "qs2": "sync", "qs3": "sync",
            "qo": "sync", "qw0": "p", "qw1": "p", "qw2": "p", "qw3": "p",
            "qm": "sync", "qa": "s"}


def ceil_div(a, b):
    return -(-a // b)


class TB:

    def __init__(self, h, key):
        self.h = h
        self.key = key

    def __getitem__(self, idx):
        return self.h[idx]

    @property
    def ap(self):
        return self.h[:]


def R(*bufs):
    return [b.key for b in bufs]


class Ctx:
    def __init__(self, nc):
        self.nc = nc
        self.ops = {"t": [], "v": [], "s": [], "p": [], "sync": []}
        self.cnt = {e: 0 for e in ENGINES}
        self.cnt.update({q: 0 for q in QUEUES})
        self.sems = {}
        self.last_write = {}
        self.readers = {}
        self.pending = {e: [] for e in ENGINES}
        self.waited = {}
        self._names = set()
        self.prefix = ""

    def make_sems(self, stack):
        for name in ENGINES + QUEUES:
            self.sems[name] = stack.enter_context(self.nc.semaphore("sem_" + name))

    def sbuf(self, stack, name, shape, dtype):
        name = self.prefix + name
        assert name not in self._names
        self._names.add(name)
        h = stack.enter_context(self.nc.sbuf_tensor(name, shape, dtype))
        return TB(h, name)

    def sbuf_at(self, name, shape, dtype, at, delta=0):
        name = self.prefix + name
        assert name not in self._names
        self._names.add(name)
        addr = self.nc.lookup_mloc(at.h if isinstance(at, TB) else at).addr + delta
        h = self.nc.alloc_sbuf_tensor_at(name, shape, dtype, offset=addr)
        return TB(h, name)

    def psum(self, stack, name, shape, dtype=F32):
        name = self.prefix + name
        assert name not in self._names
        self._names.add(name)
        h = stack.enter_context(self.nc.psum_tensor(name, shape, dtype))
        return TB(h, name)

    _fence = None

    def _need_waits(self, stream, reads, writes):
        waits = {}

        def _tgt(src, c):
            return c

        def _add(src, c):
            if src == stream == "t":
                return
            if c is None:
                assert src == stream == "t", \
                    f"pending-count dependency from {src} seen by {stream}"
                return
            waits[src] = max(waits.get(src, 0), _tgt(src, c))

        for k in reads:
            lw = self.last_write.get(k)
            if lw is not None:
                _add(*lw)
        for k in writes:
            lw = self.last_write.get(k)
            if lw is not None:
                _add(*lw)
            for src, c in self.readers.get(k, {}).items():
                _add(src, c)
        if self._fence is not None:
            done = self._fence_done.setdefault(stream, False)
            if not done:
                for src, val in self._fence.items():
                    if src == stream == "t":
                        pass
                    waits[src] = max(waits.get(src, 0), val)
                self._fence_done[stream] = True
        out = []
        for src, val in waits.items():
            seen = self.waited.get((stream, src), 0)
            if val > seen:
                self.waited[(stream, src)] = val
                out.append((self.sems[src], val))
        return out

    def op(self, e, build, reads=(), writes=(), inc=True):
        assert e in ENGINES
        reads = [b.key if isinstance(b, TB) else b for b in reads]
        writes = [b.key if isinstance(b, TB) else b for b in writes]
        waits = self._need_waits(e, reads, writes)
        if inc:
            self.cnt[e] += 1
            cnt = self.cnt[e]
        else:
            cnt = None
        sem = self.sems[e]

        def run(eng, waits=waits, build=build, inc=inc, sem=sem):
            for s, v in waits:
                eng.wait_ge(s, v)
            ins = build(eng)
            if inc:
                ins.then_inc(sem, 1)

        self.ops[e].append(run)
        if cnt is None:
            for k in reads:
                self.pending[e].append(("r", k))
            for k in writes:
                self.pending[e].append(("w", k))
                self.last_write[k] = (e, None)
                self.readers[k] = {}
        else:
            for kind, k in self.pending[e]:
                if kind == "r":
                    self.readers.setdefault(k, {})[e] = cnt
                else:
                    if self.last_write.get(k, (None, 0))[0] == e and \
                       self.last_write[k][1] is None:
                        self.last_write[k] = (e, cnt)
            self.pending[e] = []
            for k in reads:
                self.readers.setdefault(k, {})[e] = cnt
            for k in writes:
                self.last_write[k] = (e, cnt)
                self.readers[k] = {}
        return cnt

    def dma(self, q, out, in_, reads=(), writes=(), **kw):
        assert q in QUEUES
        reads = [b.key if isinstance(b, TB) else b for b in reads]
        writes = [b.key if isinstance(b, TB) else b for b in writes]
        issuer = Q_ISSUER[q]
        waits = self._need_waits(issuer, reads, writes)
        prev = self.cnt[q]
        self.cnt[q] += 16
        cnt = self.cnt[q]
        sem = self.sems[q]

        def run(eng, waits=waits, out=out, in_=in_, kw=kw, sem=sem, prev=prev):
            if prev > 0:
                eng.wait_ge(sem, prev)
            for s, v in waits:
                eng.wait_ge(s, v)
            eng.dma_start(out=out, in_=in_, **kw).then_inc(sem, 16)

        self.ops[issuer].append(run)
        for k in reads:
            self.readers.setdefault(k, {})[q] = cnt
        for k in writes:
            self.last_write[k] = (q, cnt)
            self.readers[k] = {}
        return cnt

    def fence(self):
        self._fence = {s: self.cnt[s] for s in ENGINES + QUEUES if self.cnt[s] > 0}
        self._fence_done = {}

    def finish(self):
        targets = []
        for src in ENGINES + QUEUES:
            v = self.cnt[src]
            if v > 0:
                targets.append((self.sems[src], v))

        def run(eng, targets=targets):
            for s, v in targets:
                eng.wait_ge(s, v)

        self.ops["sync"].append(run)

    def build(self, block):
        for name, attr in (("t", "tensor"), ("v", "vector"), ("s", "scalar"),
                           ("p", "gpsimd"), ("sync", "sync")):
            ops = self.ops[name]
            if not ops:
                continue
            getattr(block, attr)(lambda eng, ops=ops: [f(eng) for f in ops])

    def mm(self, psum_ap, psum_buf, parts, accumulate=False, close=True):
        n = len(parts)
        for i, (lhsT, rhs, rks) in enumerate(parts):
            start = (i == 0) and not accumulate
            stop = (i == n - 1) and close
            last = (i == n - 1)
            self.op(
                "t",
                lambda eng, lhsT=lhsT, rhs=rhs, start=start, stop=stop:
                    eng.matmul(psum_ap, lhsT, rhs, start=start, stop=stop),
                reads=rks,
                writes=[psum_buf] if (stop or i == 0) else [],
                inc=last,
            )



class StdBuf:
    def __init__(self, ctx, stack, Tl=T, n_tmp=3, n_wslab=2, wslab_cols=1024,
                 psum_f32=6, psum_bf16=1):
        self.T = Tl
        self.x = [ctx.sbuf(stack, f"x{c}", [128, Tl], F32) for c in range(NC)]
        self.xb = [ctx.sbuf(stack, f"xb{c}", [128, Tl], BF16) for c in range(NC)]
        self.tmp = [ctx.sbuf(stack, f"tmp{i}", [128, Tl], F32) for i in range(n_tmp)]
        self.mu = ctx.sbuf(stack, "mu", [128, Tl], F32)
        self.rstd = ctx.sbuf(stack, "rstd", [128, Tl], F32)
        self.rows = ctx.sbuf(stack, "rows", [1, Tl], F32)
        self.rows2 = ctx.sbuf(stack, "rows2", [1, Tl], F32)
        self.ones_b = ctx.sbuf(stack, "ones_b", [128, 1], BF16)
        self.ones_f = ctx.sbuf(stack, "ones_f", [128, 1], F32)
        self.onesrow = ctx.sbuf(stack, "onesrow", [1, 128], F32)
        self.vecs = ctx.sbuf(stack, "vecs", [128, 128], F32)
        self.nvec = 0
        self.consts = ctx.sbuf(stack, "consts", [128, 8], F32)
        self._cmap = {}
        self._ctx = ctx
        self.wslab = [ctx.sbuf(stack, f"wslab{i}", [128, wslab_cols], BF16)
                      for i in range(n_wslab)]
        self.ps = [ctx.psum(stack, f"ps{i}", [128, 512], F32) for i in range(psum_f32)]
        self.pb = [ctx.psum(stack, f"pb{i}", [128, 1024], BF16) for i in range(psum_bf16)]

    def vcol(self, col):
        return self.vecs[:, col:col + 1]

    def const_ap(self, value):
        value = float(value)
        if value not in self._cmap:
            col = len(self._cmap)
            assert col < 8
            self._cmap[value] = col
            self._ctx.op("v", lambda eng, col=col, value=value: eng.memset(
                self.consts[:, col:col + 1], value), writes=[self.consts])
        return self.consts[:, self._cmap[value]:self._cmap[value] + 1]

    def valloc(self, n):
        c0 = self.nvec
        self.nvec += n
        assert self.nvec <= 160
        return c0

    def vkey(self, col):
        return self.vecs.key


def emit_const_init(ctx, sb):
    ctx.op("v", lambda eng: eng.memset(sb.ones_b.ap, 1.0), writes=[sb.ones_b])
    ctx.op("v", lambda eng: eng.memset(sb.ones_f.ap, 1.0), writes=[sb.ones_f])
    ctx.op("v", lambda eng: eng.memset(sb.onesrow.ap, 1.0), writes=[sb.onesrow])


def load_vec(ctx, sb, dram_ap, length=D):
    nch = length // 128
    col0 = sb.valloc(nch)
    for j in range(nch):
        ctx.dma("qm", sb.vecs[:, col0 + j:col0 + j + 1],
                dram_ap[j * 128:(j + 1) * 128, None], writes=[sb.vecs])
    return col0


QS_RR = ("qs", "qs1", "qs2", "qs3")


def load_stream(ctx, sb, x_dram, dst=None):
    dst = dst or sb.x
    for c in range(NC):
        ctx.dma(QS_RR[c % 4], dst[c].ap, x_dram[c * 128:(c + 1) * 128, :],
                writes=[dst[c]])


def store_stream(ctx, sb, out_dram, src=None):
    src = src or sb.x
    for c in range(NC):
        ctx.dma("qo", out_dram[c * 128:(c + 1) * 128, :], src[c].ap, reads=[src[c]])


def plain_ld(xin, rd_key=None):
    rks = [rd_key] if rd_key else []

    def f(ctx, dst, c, col0=0, ncols=T, dcol0=None, q=None):
        d0 = col0 if dcol0 is None else dcol0
        ctx.dma(q or QS_RR[c % 4], dst.h[:, d0:d0 + ncols],
                xin[c * 128:(c + 1) * 128, col0:col0 + ncols],
                reads=rks, writes=[dst])
    return f


def plain_st(xout, wr_key=None):
    wks = [wr_key] if wr_key else []

    def f(ctx, src, c, col0=0, ncols=T, scol0=None, q="qo"):
        s0 = col0 if scol0 is None else scol0
        ctx.dma(q, xout[c * 128:(c + 1) * 128, col0:col0 + ncols],
                src.h[:, s0:s0 + ncols], reads=[src], writes=wks)
    return f


def a2a_ld(buf, rd_key):
    def f(ctx, dst, c, col0=0, ncols=T, dcol0=None, q=None):
        d0 = col0 if dcol0 is None else dcol0
        assert col0 % 256 == 0 and ncols % 256 == 0
        for k in range(ncols // 256):
            s = col0 // 256 + k
            ctx.dma(q or QS_RR[c % 4], dst.h[:, d0 + k * 256:d0 + (k + 1) * 256],
                    buf[s * 1024 + c * 128:s * 1024 + (c + 1) * 128, :],
                    reads=[rd_key], writes=[dst])
    return f


def a2a_st(buf, wr_key):
    def f(ctx, src, c, col0=0, ncols=T, scol0=None, q="qo"):
        s0 = col0 if scol0 is None else scol0
        assert col0 % 256 == 0 and ncols % 256 == 0
        for k in range(ncols // 256):
            j = col0 // 256 + k
            ctx.dma(q, buf[j * 1024 + c * 128:j * 1024 + (c + 1) * 128, :],
                    src.h[:, s0 + k * 256:s0 + (k + 1) * 256],
                    reads=[src], writes=[wr_key])
    return f


def emit_a2a(ctx, cc_sem, in_ap, out_ap, in_key, out_key, tiny):
    from concourse.aot_env import allow_ici_all_to_all

    def coll(eng):
        with allow_ici_all_to_all():
            return eng.collective_compute(
                "AllToAll", ALU.bypass, replica_groups=[list(range(NCORES))],
                ins=[in_ap.opt()], outs=[out_ap.opt()]).then_inc(cc_sem)
    ctx.op("p", coll, reads=[in_key], inc=False)

    def marker(eng):
        eng.wait_ge(cc_sem, 1)
        return eng.memset(tiny.ap, 0.0)
    ctx.op("p", marker, writes=[out_key, tiny], inc=True)


def cast_stream(ctx, sb, src_list, dst_list, shift_chunks=()):
    for c in range(NC):
        src, dst = src_list[c], dst_list[c]
        if c in shift_chunks:
            w = dst.h.shape[1]
            ctx.op("v", lambda eng, s=src, d=dst, w=w: eng.tensor_copy(
                d[:, 1:w], s[:, 0:w - 1]),
                reads=[src], writes=[dst])
            ctx.op("v", lambda eng, d=dst: eng.memset(d[:, 0:1], 0.0), writes=[dst])
        else:
            ctx.op("v", lambda eng, s=src, d=dst: eng.tensor_copy(d.ap, s.ap),
                   reads=[src], writes=[dst])


def stream_w_slab(ctx, sb, w_dram, buf_idx, col0, ncols, kchunks=NC, krow0=0):
    slab = sb.wslab[buf_idx]
    view = w_dram[krow0:krow0 + kchunks * 128,
                  col0:col0 + ncols].rearrange("(k p) c -> p k c", p=128)
    ctx.dma("qw%d" % (buf_idx % 4), slab[:, 0:kchunks * ncols], view, writes=[slab])
    return slab


def proj_fm(ctx, sb, w_dram, dout, xbufs, out_cb, kchunks=NC, pslots=(0, 1, 2, 3),
            Tl=None, wbufs=(0, 1)):
    Tl = Tl or sb.T
    ntt = Tl // 512
    noc = dout // 128
    for oc in range(noc):
        slab = stream_w_slab(ctx, sb, w_dram, wbufs[oc % len(wbufs)], oc * 128, 128,
                             kchunks)
        for tt in range(ntt):
            ps = sb.ps[pslots[(oc * ntt + tt) % len(pslots)]]
            parts = [(slab[:, k * 128:(k + 1) * 128],
                      xbufs[k][:, tt * 512:(tt + 1) * 512],
                      [slab, xbufs[k]]) for k in range(kchunks)]
            ctx.mm(ps.ap, ps, parts)
            out_cb(oc, tt, ps)


def emit_ln_fm(ctx, sb, src_list, dst_list, eps, gamma_col=None, beta_col=None,
               src_bf16=False, Tl=None, nch=NC, dim=D, src_off=0, dst_off=0):
    Tl = Tl or sb.T
    ntt = Tl // 512
    sq = sb.tmp[-1]
    ones = sb.ones_b if src_bf16 else sb.ones_f
    stat = sb.ps[4]
    bc = sb.ps[5]
    for tt in range(ntt):
        sl = slice(src_off + tt * 512, src_off + (tt + 1) * 512)
        lo = slice(tt * 512, (tt + 1) * 512)
        parts = [(ones.ap, src_list[c][:, sl], [ones, src_list[c]])
                 for c in range(nch)]
        ctx.mm(stat[0:1, :], stat, parts)
        for c in range(nch):
            ctx.op("s", lambda eng, c=c, sl=sl: eng.activation(
                sq[:, 0:512], src_list[c][:, sl], AF.Square),
                reads=[src_list[c]], writes=[sq])
            ctx.mm(stat[32:33, :], stat,
                   [(sb.ones_f.ap, sq[:, 0:512], [sb.ones_f, sq])],
                   accumulate=(c > 0), close=(c == nch - 1))
        ctx.op("v", lambda eng, lo=lo: eng.tensor_copy(sb.rows[0:1, lo], stat[0:1, :]),
               reads=[stat], writes=[sb.rows])
        ctx.op("v", lambda eng, lo=lo: eng.tensor_copy(sb.rows2[0:1, lo], stat[32:33, :]),
               reads=[stat], writes=[sb.rows2])
        ctx.mm(bc.ap, bc, [(sb.onesrow.ap, sb.rows[0:1, lo], [sb.onesrow, sb.rows])])
        ctx.op("v", lambda eng, lo=lo: eng.tensor_scalar(
            sb.mu[:, lo], bc.ap, 1.0 / dim, None, ALU.mult),
            reads=[bc], writes=[sb.mu])
        ctx.mm(bc.ap, bc, [(sb.onesrow.ap, sb.rows2[0:1, lo], [sb.onesrow, sb.rows2])])
        ctx.op("v", lambda eng, lo=lo: eng.tensor_scalar(
            sb.rstd[:, lo], bc.ap, 1.0 / dim, None, ALU.mult),
            reads=[bc], writes=[sb.rstd])
    ctx.op("v", lambda eng: eng.tensor_mul(sq[:, 0:Tl], sb.mu[:, 0:Tl], sb.mu[:, 0:Tl]),
           reads=[sb.mu], writes=[sq])
    ctx.op("v", lambda eng: eng.tensor_sub(sb.rstd[:, 0:Tl], sb.rstd[:, 0:Tl], sq[:, 0:Tl]),
           reads=[sb.rstd, sq], writes=[sb.rstd])
    epsap = sb.const_ap(eps)
    ctx.op("s", lambda eng: eng.activation(sb.rstd[:, 0:Tl], sb.rstd[:, 0:Tl], AF.Ln,
                                           bias=epsap),
           reads=[sb.rstd, sb.consts], writes=[sb.rstd])
    ctx.op("s", lambda eng: eng.activation(sb.rstd[:, 0:Tl], sb.rstd[:, 0:Tl], AF.Exp,
                                           scale=-0.5),
           reads=[sb.rstd], writes=[sb.rstd])
    for c in range(nch):
        src, dst = src_list[c], dst_list[c]
        ssl = slice(src_off, src_off + Tl)
        dsl = slice(dst_off, dst_off + Tl)
        ctx.op("v", lambda eng, src=src, ssl=ssl: eng.tensor_sub(
            sq[:, 0:Tl], src[:, ssl], sb.mu[:, 0:Tl]),
            reads=[src, sb.mu], writes=[sq])
        if gamma_col is not None:
            ctx.op("v", lambda eng: eng.tensor_mul(sq[:, 0:Tl], sq[:, 0:Tl],
                                                   sb.rstd[:, 0:Tl]),
                   reads=[sq, sb.rstd], writes=[sq])
            ctx.op("v", lambda eng, dst=dst, c=c, dsl=dsl: eng.tensor_scalar(
                dst[:, dsl], sq[:, 0:Tl], sb.vcol(gamma_col + c),
                sb.vcol(beta_col + c), ALU.mult, ALU.add),
                reads=[sq, sb.vecs], writes=[dst])
        else:
            ctx.op("v", lambda eng, dst=dst, dsl=dsl: eng.tensor_mul(
                dst[:, dsl], sq[:, 0:Tl], sb.rstd[:, 0:Tl]),
                reads=[sq, sb.rstd], writes=[dst])



L0A_VECS = ["bg", "bi", "A", "Bp", "Cp", "Dp", "gamma", "beta",
            "bfg", "bfu", "bfd", "bo"]


def declare_l0_attn(nc, p=""):
    aps = {}
    for name in ["Wg", "Wi", "Wfg", "Wfu", "Wfd", "Wo"]:
        aps[name] = nc.dram_tensor(p + name, [D, D], F32, kind="ExternalInput").ap()
    for name in L0A_VECS:
        aps[name] = nc.dram_tensor(p + name, [D], F32, kind="ExternalInput").ap()
    return aps


def emit_l0_attn(nc, ctx, stack, aps, xld, xst):
    wd = aps
    vin = {n: aps[n] for n in L0A_VECS}
    sb = StdBuf(ctx, stack)
    y = [ctx.sbuf(stack, f"y{c}", [128, T], BF16) for c in range(NC)]
    if True:
            emit_const_init(ctx, sb)
            cols = {n: load_vec(ctx, sb, ap) for n, ap in vin.items()}
            for c in range(NC):
                xld(ctx, sb.x[c], c, q=QS_RR[c % 4])
            cast_stream(ctx, sb, sb.x, sb.xb)
            cb0 = sb.valloc(NC)
            for c in range(NC):
                ctx.op("v", lambda eng, c=c: eng.tensor_mul(
                    sb.vcol(cb0 + c), sb.vcol(cols["Cp"] + c), sb.vcol(cols["Bp"] + c)),
                    reads=[sb.vecs], writes=[sb.vecs])

            gate, xi, tscr = sb.tmp[0], sb.tmp[1], sb.tmp[2]

            def dual_proj(srcbufs, wa, ba, wb, bb, fa=AF.Silu, fb=AF.Identity):
                for oc in range(NC):
                    for (wkey, dst, bcol, func, wbuf, psl) in (
                            (wa, gate, ba, fa, 0, (0, 1)),
                            (wb, xi, bb, fb, 1, (2, 3))):
                        slab = stream_w_slab(ctx, sb, wd[wkey], wbuf, oc * 128, 128)
                        for tt in range(NT):
                            ps = sb.ps[psl[tt % 2]]
                            parts = [(slab[:, k * 128:(k + 1) * 128],
                                      srcbufs[k][:, tt * 512:(tt + 1) * 512],
                                      [slab, srcbufs[k]]) for k in range(NC)]
                            ctx.mm(ps.ap, ps, parts)
                            ctx.op("s", lambda eng, tt=tt, ps=ps, dst=dst, bcol=bcol,
                                   func=func, oc=oc: eng.activation(
                                       dst[:, tt * 512:(tt + 1) * 512], ps.ap, func,
                                       bias=sb.vcol(bcol + oc)),
                                   reads=[ps, sb.vecs], writes=[dst])
                    yield oc

            for oc in dual_proj(sb.xb, "Wg", cols["bg"], "Wi", cols["bi"]):
                ctx.op("v", lambda eng: eng.tensor_mul(xi.ap, gate.ap, xi.ap),
                       reads=[gate, xi], writes=[xi])
                ctx.op("v", lambda eng, oc=oc: eng.tensor_tensor_scan(
                    gate.ap, sb.vcol(cols["A"] + oc).broadcast_to((128, T)), xi.ap,
                    0.0, ALU.mult, ALU.add),
                    reads=[xi, sb.vecs], writes=[gate])
                ctx.op("v", lambda eng, oc=oc: eng.tensor_scalar(
                    tscr.ap, xi.ap, sb.vcol(cols["Dp"] + oc), None, ALU.mult),
                    reads=[xi, sb.vecs], writes=[tscr])
                ctx.op("v", lambda eng, oc=oc: eng.scalar_tensor_tensor(
                    y[oc].ap, gate.ap, sb.vcol(cb0 + oc), tscr.ap, ALU.mult, ALU.add),
                    reads=[gate, sb.vecs, tscr], writes=[y[oc]])

            emit_ln_fm(ctx, sb, y, sb.xb, 1e-3, gamma_col=cols["gamma"],
                       beta_col=cols["beta"], src_bf16=True)
            for oc in dual_proj(sb.xb, "Wfg", cols["bfg"], "Wfu", cols["bfu"]):
                ctx.op("v", lambda eng, oc=oc: eng.tensor_mul(y[oc].ap, gate.ap, xi.ap),
                       reads=[gate, xi], writes=[y[oc]])
            def evac_to_xb(oc, tt, ps):
                ctx.op("s", lambda eng, oc=oc, tt=tt, ps=ps: eng.activation(
                    sb.xb[oc][:, tt * 512:(tt + 1) * 512], ps.ap, AF.Identity,
                    bias=sb.vcol(cols["bfd"] + oc)),
                    reads=[ps, sb.vecs], writes=[sb.xb[oc]])
            proj_fm(ctx, sb, wd["Wfd"], D, y, evac_to_xb)
            def evac_res(oc, tt, ps):
                sl = slice(tt * 512, (tt + 1) * 512)
                ctx.op("v", lambda eng, oc=oc, sl=sl, ps=ps: eng.tensor_scalar(
                    tscr[:, sl], ps.ap, sb.vcol(cols["bo"] + oc), None, ALU.add),
                    reads=[ps, sb.vecs], writes=[tscr])
                ctx.op("v", lambda eng, oc=oc, sl=sl: eng.tensor_add(
                    sb.x[oc][:, sl], sb.x[oc][:, sl], tscr[:, sl]),
                    reads=[tscr, sb.x[oc]], writes=[sb.x[oc]])
            proj_fm(ctx, sb, wd["Wo"], D, sb.xb, evac_res)
            emit_ln_fm(ctx, sb, sb.x, sb.x, 1e-5)
            for c in range(NC):
                xst(ctx, sb.x[c], c)


def build_l0_attn():
    nc = bass.Bass("TRN2", target_bir_lowering=False)
    aps = declare_l0_attn(nc)
    xin = nc.dram_tensor("xin", [D, T], F32, kind="ExternalInput").ap()
    xout = nc.dram_tensor("xout", [D, T], F32, kind="ExternalOutput").ap()
    with contextlib.ExitStack() as stack:
        ctx = Ctx(nc)
        ctx.make_sems(stack)
        with nc.Block() as block:
            emit_l0_attn(nc, ctx, stack, aps, plain_ld(xin), plain_st(xout))
            ctx.finish()
            ctx.build(block)
    return nc




RND = float(1.5 * 2 ** 23)


def emit_quantize_weight(ctx, sb, w_dram, wq_dram, rows, cols, wf, racc, rtmp, qs_col):
    nch = rows // 128
    cw = min(cols, 2048)
    ncs = cols // cw
    assert cols % cw == 0
    ctx.op("v", lambda eng: eng.memset(racc.ap, 0.0), writes=[racc])
    for r in range(nch):
        for s in range(ncs):
            buf = wf[(r * ncs + s) % 2]
            ctx.dma("qw%d" % ((r * ncs + s) % 2), buf[:, 0:cw],
                    w_dram[r * 128:(r + 1) * 128, s * cw:(s + 1) * cw],
                    writes=[buf])
            ctx.op("v", lambda eng, buf=buf: eng.reduce_sum(
                rtmp.ap, buf[:, 0:cw], AX.X, apply_absolute_value=True),
                reads=[buf], writes=[rtmp])
            ctx.op("v", lambda eng: eng.tensor_add(racc.ap, racc.ap, rtmp.ap),
                   reads=[rtmp, racc], writes=[racc])
    stat = sb.ps[4]
    ctx.mm(stat[0:1, 0:1], stat, [(sb.ones_f.ap, racc.ap, [sb.ones_f, racc])])
    ctx.op("v", lambda eng: eng.tensor_scalar(
        sb.rows[0:1, 0:1], stat[0:1, 0:1], 1.0 / (rows * cols), 1e-5, ALU.mult, ALU.add),
        reads=[stat], writes=[sb.rows])
    ctx.op("v", lambda eng: eng.reciprocal(sb.rows[0:1, 1:2], sb.rows[0:1, 0:1]),
           reads=[sb.rows], writes=[sb.rows])
    bc = sb.ps[5]
    ctx.mm(bc[:, 0:1], bc, [(sb.onesrow.ap, sb.rows[0:1, 1:2], [sb.onesrow, sb.rows])])
    ctx.op("v", lambda eng: eng.tensor_copy(sb.vcol(qs_col), bc[:, 0:1]),
           reads=[bc], writes=[sb.vecs])
    qb = sb.wslab[0]
    for r in range(nch):
        for s in range(ncs):
            buf = wf[(r * ncs + s) % 2]
            ctx.dma("qw%d" % ((r * ncs + s) % 2), buf[:, 0:cw],
                    w_dram[r * 128:(r + 1) * 128, s * cw:(s + 1) * cw],
                    writes=[buf])
            ctx.op("v", lambda eng, buf=buf: eng.tensor_scalar(
                buf[:, 0:cw], buf[:, 0:cw], sb.vcol(qs_col), RND, ALU.mult, ALU.add),
                reads=[buf, sb.vecs], writes=[buf])
            ctx.op("v", lambda eng, buf=buf: eng.tensor_scalar(
                buf[:, 0:cw], buf[:, 0:cw], RND, -1.0, ALU.subtract, ALU.max),
                reads=[buf], writes=[buf])
            ctx.op("v", lambda eng, buf=buf: eng.tensor_scalar(
                qb[:, 0:cw], buf[:, 0:cw], 1.0, None, ALU.min),
                reads=[buf], writes=[qb])
            ctx.dma("qo", wq_dram[r * 128:(r + 1) * 128, s * cw:(s + 1) * cw],
                    qb[:, 0:cw], reads=[qb], writes=["wq_dram"])


def declare_l0_ffn(nc, p=""):
    FF = 4096
    return {
        "W1T": nc.dram_tensor(p + "W1T", [D, FF], F32, kind="ExternalInput").ap(),
        "W2T": nc.dram_tensor(p + "W2T", [FF, D], F32, kind="ExternalInput").ap(),
        "b1": nc.dram_tensor(p + "b1", [FF], F32, kind="ExternalInput").ap(),
        "b2": nc.dram_tensor(p + "b2", [D], F32, kind="ExternalInput").ap(),
        "w1q": nc.dram_tensor(p + "w1q", [D, FF], BF16, kind="Internal").ap(),
        "w2q": nc.dram_tensor(p + "w2q", [FF, D], BF16, kind="Internal").ap(),
    }


def emit_l0_ffn(nc, ctx, stack, aps, xld, xst):
    FF = 4096
    W1T, W2T, b1, b2 = aps["W1T"], aps["W2T"], aps["b1"], aps["b2"]
    w1q, w2q = aps["w1q"], aps["w2q"]
    if True:
        sb = StdBuf(ctx, stack, n_tmp=1, n_wslab=2, wslab_cols=2048, psum_bf16=0)
        wf = [ctx.sbuf(stack, f"wf{i}", [128, 2048], F32) for i in range(2)]
        racc = ctx.sbuf(stack, "racc", [128, 1], F32)
        rtmp = ctx.sbuf(stack, "rtmp", [128, 1], F32)
        h = [ctx.sbuf(stack, f"h{i}", [128, 512], BF16) for i in range(32)]
        xbt = sb.xb
        ps8 = [ctx.psum(stack, f"ps8_{i}", [128, 512], F32) for i in range(2)]
        if True:
            emit_const_init(ctx, sb)
            b1c = load_vec(ctx, sb, b1, FF)
            b2c = load_vec(ctx, sb, b2, D)
            qs1 = sb.valloc(1)
            qs2 = sb.valloc(1)
            for c in range(NC):
                xld(ctx, sb.x[c], c, q=QS_RR[c % 4])
            cast_stream(ctx, sb, sb.x, xbt)
            emit_quantize_weight(ctx, sb, W1T, w1q, D, FF, wf, racc, rtmp, qs1)
            emit_quantize_weight(ctx, sb, W2T, w2q, FF, D, wf, racc, rtmp, qs2)
            for tt in range(NT):
                sl = slice(tt * 512, (tt + 1) * 512)
                for fc in range(32):
                    slab = sb.wslab[fc % 2]
                    ctx.dma("qw%d" % (fc % 2), slab[:, 0:1024],
                            w1q[0:D, fc * 128:(fc + 1) * 128].rearrange(
                                "(k p) c -> p k c", p=128),
                            reads=["wq_dram"], writes=[slab])
                    ps = ps8[fc % 2]
                    parts = [(slab[:, k * 128:(k + 1) * 128], xbt[k][:, sl],
                              [slab, xbt[k]]) for k in range(NC)]
                    ctx.mm(ps.ap, ps, parts)
                    ctx.op("s", lambda eng, ps=ps, fc=fc: eng.activation(
                        h[fc].ap, ps.ap, AF.Relu, bias=sb.vcol(b1c + fc)),
                        reads=[ps, sb.vecs], writes=[h[fc]])
                pslots = sb.ps[0:4] + ps8 + sb.ps[4:6]
                for fc in range(32):
                    slab = sb.wslab[fc % 2]
                    ctx.dma("qw%d" % (fc % 2), slab[:, 0:1024],
                            w2q[fc * 128:(fc + 1) * 128, 0:1024],
                            reads=["wq_dram"], writes=[slab])
                    for dc in range(NC):
                        ctx.mm(pslots[dc].ap, pslots[dc],
                               [(slab[:, dc * 128:(dc + 1) * 128], h[fc].ap,
                                 [slab, h[fc]])], accumulate=(fc > 0),
                               close=(fc == 31))
                for dc in range(NC):
                    ctx.op("v", lambda eng, dc=dc, sl=sl: eng.scalar_tensor_tensor(
                        sb.x[dc][:, sl], pslots[dc].ap, sb.vcol(b2c + dc),
                        sb.x[dc][:, sl], ALU.add, ALU.add),
                        reads=[pslots[dc], sb.vecs, sb.x[dc]], writes=[sb.x[dc]])
            emit_ln_fm(ctx, sb, sb.x, sb.x, 1e-5)
            for c in range(NC):
                xst(ctx, sb.x[c], c)


def build_l0_ffn():
    nc = bass.Bass("TRN2", target_bir_lowering=False)
    aps = declare_l0_ffn(nc)
    xin = nc.dram_tensor("xin", [D, T], F32, kind="ExternalInput").ap()
    xout = nc.dram_tensor("xout", [D, T], F32, kind="ExternalOutput").ap()
    with contextlib.ExitStack() as stack:
        ctx = Ctx(nc)
        ctx.make_sems(stack)
        with nc.Block() as block:
            emit_l0_ffn(nc, ctx, stack, aps, plain_ld(xin), plain_st(xout))
            ctx.finish()
            ctx.build(block)
    return nc


def run_l0_ffn(x_fm_cores, ins):
    return _run_block("l0_ffn", build_l0_ffn, x_fm_cores, ins)



class NS:
    pass


def _mk_lnbuf(ctx, stack, Tl=T, n_tmp=2):
    sb = NS()
    sb.T = Tl
    sb.tmp = [ctx.sbuf(stack, f"tmp{i}", [128, Tl], F32) for i in range(n_tmp)]
    sb.mu = ctx.sbuf(stack, "mu", [128, Tl], F32)
    sb.rstd = ctx.sbuf(stack, "rstd", [128, Tl], F32)
    sb.rows = TB(sb.mu.h, "mu")
    sb.rows2 = TB(sb.rstd.h, "rstd")
    sb.ones_b = ctx.sbuf(stack, "ones_b", [128, 1], BF16)
    sb.ones_f = ctx.sbuf(stack, "ones_f", [128, 1], F32)
    sb.onesrow = ctx.sbuf(stack, "onesrow", [1, 128], F32)
    sb.vecs = ctx.sbuf(stack, "vecs", [128, 160], F32)
    sb.nvec = 0
    sb.consts = ctx.sbuf(stack, "consts", [128, 8], F32)
    sb._cmap = {}
    sb._ctx = ctx
    sb.vcol = lambda col: sb.vecs[:, col:col + 1]
    import types
    sb.valloc = types.MethodType(StdBuf.valloc, sb)
    sb.const_ap = types.MethodType(StdBuf.const_ap, sb)
    return sb


SHUF_PAIRSWAP = [i ^ 1 for i in range(32)]


def declare_l1_attn(nc, p=""):
    aps = {}
    for name in ["WQ", "WK", "WV"]:
        aps[name] = nc.dram_tensor(p + name, [D, D], F32, kind="ExternalInput").ap()
    for name in ["csq", "ssq", "csk", "ssk"]:
        aps[name] = nc.dram_tensor(p + name, [D, T], F32, kind="ExternalInput").ap()
    aps["idin"] = nc.dram_tensor(p + "idin", [128, 128], F32,
                                 kind="ExternalInput").ap()
    aps["qx_d"] = nc.dram_tensor(p + "qx_d", [D, T], BF16, kind="Internal").ap()
    aps["kx_d"] = nc.dram_tensor(p + "kx_d", [D, T], BF16, kind="Internal").ap()
    aps["v_d"] = nc.dram_tensor(p + "v_d", [T, D], BF16, kind="Internal").ap()
    return aps


def emit_l1_attn(nc, ctx, stack, aps, xld, xst):
    WQ, WK, WV = aps["WQ"], aps["WK"], aps["WV"]
    csq, ssq, csk, ssk = aps["csq"], aps["ssq"], aps["csk"], aps["ssk"]
    idin = aps["idin"]
    qx_d, kx_d, v_d = aps["qx_d"], aps["kx_d"], aps["v_d"]
    if True:
        sb = _mk_lnbuf(ctx, stack)
        sb.ps = [ctx.psum(stack, f"ps{i}", [128, 512], F32) for i in range(8)]
        wslab = [ctx.sbuf(stack, f"wslab{i}", [128, 1024], BF16) for i in range(2)]
        sb.wslab = wslab
        ident = ctx.sbuf(stack, "ident", [128, 128], F32)
        xb = [ctx.sbuf(stack, f"xb{c}", [128, T], BF16) for c in range(NC)]
        xo = [ctx.sbuf(stack, f"xo{c}", [128, T], BF16) for c in range(NC)]

        if True:
            emit_const_init(ctx, sb)
            ctx.dma("qm", ident.ap, idin, writes=[ident])
            with contextlib.ExitStack() as sA:
                x = [ctx.sbuf(sA, f"x{c}", [128, T], F32) for c in range(NC)]
                for c in range(NC):
                    xld(ctx, x[c], c, q="qs")
                    ctx.op("v", lambda eng, c=c: eng.tensor_copy(xb[c].ap, x[c].ap),
                           reads=[x[c]], writes=[xb[c]])
            ctx.fence()
            with contextlib.ExitStack() as sB:
                qraw = ctx.sbuf(sB, "qraw", [128, T], BF16)
                shuf = ctx.sbuf(sB, "shuf", [128, T], BF16)
                qxb = ctx.sbuf(sB, "qxb", [128, T], BF16)
                cs_sl = ctx.sbuf(sB, "cs_sl", [128, T], F32)
                ss_sl = ctx.sbuf(sB, "ss_sl", [128, T], F32)
                vbuf = ctx.sbuf(sB, "vbuf", [128, D], BF16)
                for (W, tcs, tss, out_d) in ((WQ, csq, ssq, qx_d), (WK, csk, ssk, kx_d)):
                    for oc in range(NC):
                        slab = stream_w_slab(ctx, sb, W, oc % 2, oc * 128, 128)
                        for tt in range(NT):
                            ps = sb.ps[tt % 2]
                            parts = [(slab[:, k * 128:(k + 1) * 128],
                                      xb[k][:, tt * 512:(tt + 1) * 512],
                                      [slab, xb[k]]) for k in range(NC)]
                            ctx.mm(ps.ap, ps, parts)
                            ctx.op("s", lambda eng, tt=tt, ps=ps: eng.activation(
                                qraw[:, tt * 512:(tt + 1) * 512], ps.ap, AF.Identity),
                                reads=[ps], writes=[qraw])
                        ctx.dma("qs", cs_sl.ap, tcs[oc * 128:(oc + 1) * 128, :], writes=[cs_sl])
                        ctx.dma("qs", ss_sl.ap, tss[oc * 128:(oc + 1) * 128, :], writes=[ss_sl])
                        ctx.op("v", lambda eng: eng.stream_shuffle(shuf.ap, qraw.ap, SHUF_PAIRSWAP),
                               reads=[qraw], writes=[shuf])
                        ctx.op("v", lambda eng: eng.tensor_mul(sb.tmp[0].ap, qraw.ap, cs_sl.ap),
                               reads=[qraw, cs_sl], writes=[sb.tmp[0]])
                        ctx.op("v", lambda eng: eng.tensor_mul(sb.tmp[1].ap, shuf.ap, ss_sl.ap),
                               reads=[shuf, ss_sl], writes=[sb.tmp[1]])
                        ctx.op("v", lambda eng: eng.tensor_add(qxb.ap, sb.tmp[0].ap, sb.tmp[1].ap),
                               reads=[sb.tmp[0], sb.tmp[1]], writes=[qxb])
                        ctx.dma("qo", out_d[oc * 128:(oc + 1) * 128, :], qxb.ap, reads=[qxb])
                wv = [ctx.sbuf(sB, f"wv{c}", [128, D], BF16) for c in range(NC)]
                for c in range(NC):
                    ctx.dma("qw%d" % (c % 4), wv[c].ap,
                            WV[c * 128:(c + 1) * 128, :], writes=[wv[c]])
                for tt in range(T // 128):
                    for c in range(NC):
                        slab = wv[c]
                        for nh in range(2):
                            ctx.mm(sb.ps[2 + nh].ap, sb.ps[2 + nh],
                                   [(xb[c][:, tt * 128:(tt + 1) * 128],
                                     slab[:, nh * 512:(nh + 1) * 512], [xb[c], slab])],
                                   accumulate=(c > 0), close=(c == NC - 1))
                    for nh in range(2):
                        ctx.op("s", lambda eng, nh=nh: eng.activation(
                            vbuf[:, nh * 512:(nh + 1) * 512], sb.ps[2 + nh].ap, AF.Identity),
                            reads=[sb.ps[2 + nh]], writes=[vbuf])
                    ctx.dma("qo", v_d[tt * 128:(tt + 1) * 128, :], vbuf.ap, reads=[vbuf])
            ctx.fence()
            with contextlib.ExitStack() as sC:
                qxs = [ctx.sbuf(sC, f"qxs{c}", [128, 512], BF16) for c in range(NC)]
                kxs = [ctx.sbuf(sC, f"kxs{c}", [128, T], BF16) for c in range(NC)]
                p2 = ctx.sbuf(sC, "p2", [128, T], F32)
                p2t = ctx.sbuf(sC, "p2t", [128, 16 * 512], BF16)
                dtile = ctx.sbuf(sC, "dtile", [128, T], F32)
                nmax4 = ctx.sbuf(sC, "nmax4", [128, 4], F32)
                nmax = ctx.sbuf(sC, "nmax", [128, 1], F32)
                zpart = ctx.sbuf(sC, "zpart", [128, 4], F32)
                zr = ctx.sbuf(sC, "zr", [128, 1], F32)
                for c in range(NC):
                    ctx.dma("qs", kxs[c].ap, kx_d[c * 128:(c + 1) * 128, :], writes=[kxs[c]])

                for qb in range(4):
                    for c in range(NC):
                        ctx.dma("qs", qxs[c].ap,
                                qx_d[c * 128:(c + 1) * 128, qb * 512:(qb + 1) * 512],
                                writes=[qxs[c]])
                    for qt in range(4):
                        q0 = qb * 512 + qt * 128
                        for nt in range(4):
                            ps = sb.ps[nt]
                            parts = [(qxs[c][:, qt * 128:(qt + 1) * 128],
                                      kxs[c][:, nt * 512:(nt + 1) * 512],
                                      [qxs[c], kxs[c]]) for c in range(NC)]
                            ctx.mm(ps.ap, ps, parts)
                            ctx.op("v", lambda eng, nt=nt, ps=ps: eng.reduce_max(
                                nmax4[:, nt:nt + 1], ps.ap, AX.X, negate=True),
                                reads=[ps], writes=[nmax4])
                        ctx.op("v", lambda eng: eng.tensor_reduce(
                            nmax.ap, nmax4.ap, AX.X, ALU.min),
                            reads=[nmax4], writes=[nmax])
                        for nt in range(4):
                            ps = sb.ps[nt]
                            ctx.op("s", lambda eng, nt=nt, ps=ps: eng.activation(
                                p2[:, nt * 512:(nt + 1) * 512], ps.ap, AF.Exp,
                                bias=nmax[:, 0:1], accum_out=zpart[:, nt:nt + 1]),
                                reads=[ps, nmax], writes=[p2, zpart])
                        ctx.op("v", lambda eng: eng.reduce_sum(zr.ap, zpart.ap, AX.X),
                               reads=[zpart], writes=[zr])
                        ctx.op("v", lambda eng: eng.reciprocal(zr.ap, zr.ap),
                               reads=[zr], writes=[zr])
                        ctx.op("p", lambda eng, q0=q0: eng.iota(
                            dtile.ap, [[-1, T]], base=q0, channel_multiplier=1,
                            allow_small_or_imprecise_dtypes=True),
                            writes=[dtile])
                        ctx.op("s", lambda eng: eng.activation(
                            dtile.ap, dtile.ap, AF.Abs),
                            reads=[dtile], writes=[dtile])
                        ctx.op("s", lambda eng: eng.activation(
                            dtile.ap, dtile.ap, AF.Exp, scale=math.log(0.9)),
                            reads=[dtile], writes=[dtile])
                        ctx.op("v", lambda eng: eng.scalar_tensor_tensor(
                            p2.ap, p2.ap, zr[:, 0:1], dtile.ap, ALU.mult, ALU.mult),
                            reads=[p2, zr, dtile], writes=[p2])
                        for kc in range(16):
                            pt = sb.ps[4 + kc % 4]
                            ctx.op("t", lambda eng, kc=kc, pt=pt: eng.transpose(
                                pt[0:128, 0:128], p2[:, kc * 128:(kc + 1) * 128], ident.ap),
                                reads=[p2, ident], writes=[pt])
                            ctx.op("v", lambda eng, kc=kc, qt=qt, pt=pt: eng.tensor_copy(
                                p2t[:, kc * 512 + qt * 128: kc * 512 + (qt + 1) * 128],
                                pt[0:128, 0:128]),
                                reads=[pt], writes=[p2t])
                    for kc in range(16):
                        slab = wslab[kc % 2]
                        ctx.dma("qw%d" % (kc % 2), slab.ap,
                                v_d[kc * 128:(kc + 1) * 128, :], writes=[slab])
                        for dc in range(NC):
                            ctx.mm(sb.ps[dc].ap, sb.ps[dc],
                                   [(slab[:, dc * 128:(dc + 1) * 128],
                                     p2t[:, kc * 512:(kc + 1) * 512], [slab, p2t])],
                                   accumulate=(kc > 0), close=(kc == 15))
                    for dc in range(NC):
                        ctx.op("s", lambda eng, dc=dc, qb=qb: eng.activation(
                            xo[dc][:, qb * 512:(qb + 1) * 512], sb.ps[dc].ap, AF.Identity),
                            reads=[sb.ps[dc]], writes=[xo[dc]])
            ctx.fence()
            with contextlib.ExitStack() as sD:
                xd = [ctx.sbuf(sD, f"x2{c}", [128, T], F32) for c in range(NC)]
                sb.x = xd
                for c in range(NC):
                    xld(ctx, xd[c], c, q="qs")
                    ctx.op("v", lambda eng, a=xd[c], b=xo[c]: eng.tensor_add(a.ap, a.ap, b.ap),
                           reads=[xd[c], xo[c]], writes=[xd[c]])
                emit_ln_fm(ctx, sb, xd, xd, 1e-5)
                for c in range(NC):
                    xst(ctx, xd[c], c)


def build_l1_attn():
    nc = bass.Bass("TRN2", target_bir_lowering=False)
    aps = declare_l1_attn(nc)
    xin = nc.dram_tensor("xin", [D, T], F32, kind="ExternalInput").ap()
    xout = nc.dram_tensor("xout", [D, T], F32, kind="ExternalOutput").ap()
    with contextlib.ExitStack() as stack:
        ctx = Ctx(nc)
        ctx.make_sems(stack)
        with nc.Block() as block:
            emit_l1_attn(nc, ctx, stack, aps, plain_ld(xin), plain_st(xout))
            ctx.finish()
            ctx.build(block)
    return nc


def _xpos_tables():
    half = D // 2
    scale_vec = (np.arange(0, D, 2, dtype=np.float64) + 0.4 * D) / (1.4 * D)
    power = np.arange(T, dtype=np.float64)[:, None] / 512.0
    scale = scale_vec[None, :] ** power
    inv_freq = 1.0 / 10000.0 ** (np.arange(half, dtype=np.float64) / half)
    sinu = np.arange(T, dtype=np.float64)[:, None] * inv_freq[None, :]
    sin = np.repeat(np.sin(sinu), 2, axis=-1)
    cos = np.repeat(np.cos(sinu), 2, axis=-1)
    scale2 = np.repeat(scale, 2, axis=-1)
    sgn = np.where(np.arange(D) % 2 == 0, -1.0, 1.0)[None, :]
    csq = (cos * scale2).T
    ssq = (sin * scale2 * sgn).T
    csk = (cos / scale2).T
    ssk = (sin / scale2 * sgn).T
    return [np.ascontiguousarray(a, dtype=np.float32) for a in (csq, ssq, csk, ssk)]


def _weights_l1_attn(ins):
    csq, ssq, csk, ssk = _xpos_tables()
    return {
        "WQ": _f32c(ins["L1_attn_W_Q"]), "WK": _f32c(ins["L1_attn_W_K"]),
        "WV": _f32c(ins["L1_attn_W_V"]),
        "csq": csq, "ssq": ssq, "csk": csk, "ssk": ssk,
        "idin": np.eye(128, dtype=np.float32),
    }


def run_l1_attn(x_fm_cores, ins):
    return _run_block("l1_attn", build_l1_attn, x_fm_cores, ins)



def declare_l1_ffn(nc, p=""):
    H = 2048
    E = 8
    return {
        "rwT": nc.dram_tensor(p + "rwT", [D, E], F32, kind="ExternalInput").ap(),
        "ebias": nc.dram_tensor(p + "ebias", [E], F32, kind="ExternalInput").ap(),
        "w1": nc.dram_tensor(p + "w1", [E, D, H], F32, kind="ExternalInput").ap(),
        "w3": nc.dram_tensor(p + "w3", [E, D, H], F32, kind="ExternalInput").ap(),
        "w2": nc.dram_tensor(p + "w2", [E, H, D], F32, kind="ExternalInput").ap(),
        "sw1": nc.dram_tensor(p + "sw1", [D, H], F32, kind="ExternalInput").ap(),
        "sw3": nc.dram_tensor(p + "sw3", [D, H], F32, kind="ExternalInput").ap(),
        "sw2": nc.dram_tensor(p + "sw2", [H, D], F32, kind="ExternalInput").ap(),
        "idin": nc.dram_tensor(p + "idin", [128, 128], F32,
                               kind="ExternalInput").ap(),
    }


def emit_l1_ffn(nc, ctx, stack, aps, xld, xst):
    H = 2048
    E = 8
    rwT, ebias = aps["rwT"], aps["ebias"]
    w1, w3, w2 = aps["w1"], aps["w3"], aps["w2"]
    sw1, sw3, sw2 = aps["sw1"], aps["sw3"], aps["sw2"]
    idin = aps["idin"]
    if True:
        sb = _mk_lnbuf(ctx, stack, n_tmp=1)
        sb.ps = [ctx.psum(stack, f"ps{i}", [128, 512], F32) for i in range(8)]
        ident = ctx.sbuf(stack, "ident", [128, 128], F32)
        xb = [ctx.sbuf(stack, f"xb{c}", [128, T], BF16) for c in range(NC)]
        xo = [ctx.sbuf(stack, f"xo{c}", [128, T], F32) for c in range(NC)]
        w2res = [ctx.sbuf(stack, f"w2res{i}", [128, D], BF16) for i in range(16)]
        eo = [ctx.sbuf(stack, f"eo{i}", [128, 512], BF16) for i in range(16)]
        wsl = [ctx.sbuf(stack, f"wsl{i}", [128, 8 * 128], BF16) for i in range(4)]
        sb.wslab = wsl
        rwt = ctx.sbuf(stack, "rwt", [128, 64], BF16)
        ebrow = ctx.sbuf(stack, "ebrow", [1, 8], BF16)
        onesrow_b = ctx.sbuf(stack, "onesrow_b", [1, 128], BF16)
        lg = ctx.sbuf(stack, "lg", [128, 8], F32)
        l2 = ctx.sbuf(stack, "l2", [128, 8], F32)
        m1 = ctx.sbuf(stack, "m1", [128, 1], F32)
        m2 = ctx.sbuf(stack, "m2", [128, 1], F32)
        p1 = ctx.sbuf(stack, "p1", [128, 1], F32)
        p2s = ctx.sbuf(stack, "p2s", [128, 1], F32)
        mk1 = ctx.sbuf(stack, "mk1", [128, 8], F32)
        mk2 = ctx.sbuf(stack, "mk2", [128, 8], F32)
        cw = ctx.sbuf(stack, "cw", [128, 8], F32)
        cwT = ctx.sbuf(stack, "cwT", [8, T], F32)
        cwb = ctx.sbuf(stack, "cwb", [128, 512], F32)
        ga = ctx.sbuf(stack, "ga", [128, 512], BF16)
        ua = ctx.sbuf(stack, "ua", [128, 512], BF16)
        ohot = ctx.sbuf(stack, "ohot", [8, 8 * 128], F32)

        if True:
            emit_const_init(ctx, sb)
            ctx.op("v", lambda eng: eng.memset(onesrow_b.ap, 1.0), writes=[onesrow_b])
            ctx.op("p", lambda eng: eng.iota(ohot.ap, [[1, 8], [0, 128]],
                                             base=0, channel_multiplier=-1,
                                             allow_small_or_imprecise_dtypes=True),
                   writes=[ohot])
            ctx.op("v", lambda eng: eng.tensor_scalar(ohot.ap, ohot.ap, 0.0, None,
                                                      ALU.is_equal),
                   reads=[ohot], writes=[ohot])
            ctx.dma("qm", ident.ap, idin, writes=[ident])
            ctx.dma("qw0", ebrow.ap, ebias[None, :], writes=[ebrow])
            for c in range(NC):
                ctx.dma("qw1", rwt[:, c * 8:(c + 1) * 8],
                        rwT[c * 128:(c + 1) * 128, :], writes=[rwt])
            with contextlib.ExitStack() as sA:
                xtmp = [ctx.sbuf(sA, "xtmp0", [128, T], F32)]
                for c in range(NC):
                    b = xtmp[0]
                    xld(ctx, b, c, q=QS_RR[c % 2])
                    ctx.op("v", lambda eng, a=xb[c], b=b: eng.tensor_copy(a.ap, b.ap),
                           reads=[b], writes=[xb[c]])
            ctx.fence()
            for c in range(NC):
                ctx.op("v", lambda eng, a=xo[c]: eng.memset(a.ap, 0.0), writes=[xo[c]])
            for p in range(T // 128):
                ps = sb.ps[0]
                parts = [(xb[c][:, p * 128:(p + 1) * 128], rwt[:, c * 8:(c + 1) * 8],
                          [xb[c], rwt]) for c in range(NC)]
                parts.append((onesrow_b.ap, ebrow.ap, [onesrow_b, ebrow]))
                ctx.mm(ps[:, 0:8], ps, parts)
                ctx.op("v", lambda eng, ps=ps: eng.tensor_copy(lg.ap, ps[:, 0:8]),
                       reads=[ps], writes=[lg])
                ctx.op("v", lambda eng: eng.reduce_max(m1.ap, lg.ap, AX.X),
                       reads=[lg], writes=[m1])
                ctx.op("v", lambda eng: eng.tensor_scalar(mk1.ap, lg.ap, m1[:, 0:1],
                                                          None, ALU.is_ge),
                       reads=[lg, m1], writes=[mk1])
                ctx.op("v", lambda eng: eng.scalar_tensor_tensor(
                    l2.ap, mk1.ap, -1e30, lg.ap, ALU.mult, ALU.add),
                    reads=[mk1, lg], writes=[l2])
                ctx.op("v", lambda eng: eng.reduce_max(m2.ap, l2.ap, AX.X),
                       reads=[l2], writes=[m2])
                ctx.op("v", lambda eng: eng.tensor_scalar(mk2.ap, l2.ap, m2[:, 0:1],
                                                          None, ALU.is_ge),
                       reads=[l2, m2], writes=[mk2])
                ctx.op("v", lambda eng: eng.tensor_sub(p1.ap, m2.ap, m1.ap),
                       reads=[m1, m2], writes=[p1])
                ctx.op("s", lambda eng: eng.activation(p1.ap, p1.ap, AF.Sigmoid,
                                                       scale=-1.0),
                       reads=[p1], writes=[p1])
                ctx.op("v", lambda eng: eng.tensor_scalar(p2s.ap, p1.ap, -1.0, 1.0,
                                                          ALU.mult, ALU.add),
                       reads=[p1], writes=[p2s])
                ctx.op("v", lambda eng: eng.tensor_scalar(mk1.ap, mk1.ap, p1[:, 0:1],
                                                          None, ALU.mult),
                       reads=[mk1, p1], writes=[mk1])
                ctx.op("v", lambda eng: eng.scalar_tensor_tensor(
                    cw.ap, mk2.ap, p2s[:, 0:1], mk1.ap, ALU.mult, ALU.add),
                    reads=[mk2, p2s, mk1], writes=[cw])
                pt = sb.ps[7]
                ctx.op("t", lambda eng, pt=pt: eng.transpose(
                    pt[0:8, 0:128], cw.ap, ident.ap),
                    reads=[cw, ident], writes=[pt])
                ctx.op("v", lambda eng, p=p, pt=pt: eng.tensor_copy(
                    cwT[0:8, p * 128:(p + 1) * 128], pt[0:8, 0:128]),
                    reads=[pt], writes=[cwT])
            for e in range(E + 1):
                for hc in range(16):
                    wdram = sw2 if e == 0 else w2[e - 1]
                    ctx.dma("qw%d" % (hc % 4), w2res[hc].ap,
                            wdram[hc * 128:(hc + 1) * 128, :], writes=[w2res[hc]])
                for tt in range(NT):
                    sl = slice(tt * 512, (tt + 1) * 512)
                    if e == 0:
                        ctx.op("v", lambda eng: eng.memset(cwb.ap, 1.0), writes=[cwb])
                    else:
                        bc = sb.ps[6]
                        ctx.mm(bc.ap, bc, [(ohot[:, (e - 1) * 128:e * 128],
                                            cwT[0:8, sl],
                                            [ohot, cwT])])
                        ctx.op("v", lambda eng, bc=bc: eng.tensor_copy(cwb.ap, bc.ap),
                               reads=[bc], writes=[cwb])
                    for hc in range(16):
                        b0 = (hc % 2) * 2
                        slg = wsl[b0]
                        slu = wsl[b0 + 1]
                        wg = sw1 if e == 0 else w1[e - 1]
                        wu = sw3 if e == 0 else w3[e - 1]
                        ctx.dma("qw%d" % (hc % 2), slg[:, 0:1024],
                                wg[0:D, hc * 128:(hc + 1) * 128].rearrange(
                                    "(k p) c -> p k c", p=128),
                                writes=[slg])
                        ctx.dma("qw%d" % (2 + hc % 2), slu[:, 0:1024],
                                wu[0:D, hc * 128:(hc + 1) * 128].rearrange(
                                    "(k p) c -> p k c", p=128),
                                writes=[slu])
                        psg = sb.ps[0]
                        psu = sb.ps[1]
                        ctx.mm(psg[:, 0:128 * 4], psg,
                               [(slg[:, k * 128:(k + 1) * 128], xb[k][:, sl], [slg, xb[k]])
                                for k in range(NC)])
                        ctx.mm(psu[:, 0:128 * 4], psu,
                               [(slu[:, k * 128:(k + 1) * 128], xb[k][:, sl], [slu, xb[k]])
                                for k in range(NC)])
                        ctx.op("s", lambda eng, psg=psg: eng.activation(
                            ga.ap, psg.ap, AF.Silu), reads=[psg], writes=[ga])
                        ctx.op("v", lambda eng, psu=psu: eng.tensor_mul(
                            ua.ap, psu.ap, cwb.ap),
                            reads=[psu, cwb], writes=[ua])
                        ctx.op("v", lambda eng, hc=hc: eng.tensor_mul(
                            eo[hc].ap, ga.ap, ua.ap),
                            reads=[ga, ua], writes=[eo[hc]])
                    for hc in range(16):
                        for dc in range(NC):
                            ctx.mm(sb.ps[dc].ap, sb.ps[dc],
                                   [(w2res[hc][:, dc * 128:(dc + 1) * 128], eo[hc].ap,
                                     [w2res[hc], eo[hc]])],
                                   accumulate=(hc > 0), close=(hc == 15))
                    for dc in range(NC):
                        ctx.op("v", lambda eng, dc=dc, sl=sl: eng.tensor_add(
                            xo[dc][:, sl], xo[dc][:, sl], sb.ps[dc].ap),
                            reads=[sb.ps[dc], xo[dc]], writes=[xo[dc]])
            ctx.fence()
            with contextlib.ExitStack() as sD:
                xtmp = [ctx.sbuf(sD, "xr0", [128, T], F32)]
                for c in range(NC):
                    b = xtmp[0]
                    xld(ctx, b, c, q=QS_RR[c % 2])
                    ctx.op("v", lambda eng, a=xo[c], b=b: eng.tensor_add(
                        a.ap, a.ap, b.ap),
                        reads=[xo[c], b], writes=[xo[c]])
                emit_ln_fm(ctx, sb, xo, xo, 1e-5)
                for c in range(NC):
                    xst(ctx, xo[c], c)


def build_l1_ffn():
    nc = bass.Bass("TRN2", target_bir_lowering=False)
    aps = declare_l1_ffn(nc)
    xin = nc.dram_tensor("xin", [D, T], F32, kind="ExternalInput").ap()
    xout = nc.dram_tensor("xout", [D, T], F32, kind="ExternalOutput").ap()
    with contextlib.ExitStack() as stack:
        ctx = Ctx(nc)
        ctx.make_sems(stack)
        with nc.Block() as block:
            emit_l1_ffn(nc, ctx, stack, aps, plain_ld(xin), plain_st(xout))
            ctx.finish()
            ctx.build(block)
    return nc


def _weights_l1_ffn(ins):
    return {
        "rwT": _f32c(np.asarray(ins["L1_ffn_router_w"]).T),
        "ebias": _f32c(ins["L1_ffn_experts_bias"]),
        "w1": _f32c(ins["L1_ffn_w1"]), "w3": _f32c(ins["L1_ffn_w3"]),
        "w2": _f32c(ins["L1_ffn_w2"]),
        "sw1": _f32c(np.asarray(ins["L1_ffn_sw1"])[0]),
        "sw3": _f32c(np.asarray(ins["L1_ffn_sw3"])[0]),
        "sw2": _f32c(np.asarray(ins["L1_ffn_sw2"])[0]),
        "idin": np.eye(128, dtype=np.float32),
    }


def run_l1_ffn(x_fm_cores, ins):
    return _run_block("l1_ffn", build_l1_ffn, x_fm_cores, ins)



def declare_l2_ffn(nc, p=""):
    FF = 4096
    aps = {
        "w1S": nc.dram_tensor(p + "w1S", [FF // 128, 128, D], F32,
                              kind="ExternalInput").ap(),
        "w2S": nc.dram_tensor(p + "w2S", [FF // 128, 128, D], F32,
                              kind="ExternalInput").ap(),
        "h1S": nc.dram_tensor(p + "h1S", [FF // 128, 128, FF], F32,
                              kind="ExternalInput").ap(),
        "h2S": nc.dram_tensor(p + "h2S", [FF // 128, 128, FF], F32,
                              kind="ExternalInput").ap(),
        "w3T": nc.dram_tensor(p + "w3T", [FF, D], F32, kind="ExternalInput").ap(),
    }
    for name, ln in [("s1", FF), ("s2", FF), ("sh1", FF), ("sh2", FF), ("s3", D)]:
        aps[name] = nc.dram_tensor(p + name, [ln], F32, kind="ExternalInput").ap()
    return aps


def emit_l2_ffn(nc, ctx, stack, aps, xld, xst):
    FF = 4096
    w1S, w2S, h1S, h2S, w3T = (aps["w1S"], aps["w2S"], aps["h1S"], aps["h2S"],
                               aps["w3T"])
    vin = {n: aps[n] for n in ["s1", "s2", "sh1", "sh2", "s3"]}
    if True:
        sb = _mk_lnbuf(ctx, stack, Tl=512, n_tmp=1)
        sb.ps = [ctx.psum(stack, f"ps{i}", [128, 512], F32) for i in range(8)]
        xb = [ctx.sbuf(stack, f"xb{c}", [128, T], BF16) for c in range(NC)]
        hA = [ctx.sbuf(stack, f"hA{i}", [128, 512], BF16) for i in range(32)]
        hB = [ctx.sbuf(stack, f"hB{i}", [128, 512], BF16) for i in range(32)]
        wsl = [ctx.sbuf(stack, f"wsl{i}", [128, 4096], BF16) for i in range(2)]
        xr = ctx.sbuf(stack, "xr", [128, 512], F32)
        xnew = [ctx.sbuf(stack, f"xnew{i}", [128, 512], F32) for i in range(NC)]

        if True:
            emit_const_init(ctx, sb)
            cols = {n: load_vec(ctx, sb, ap, ln)
                    for (n, ln), ap in zip([("s1", FF), ("s2", FF), ("sh1", FF),
                                            ("sh2", FF), ("s3", D)],
                                           [vin["s1"], vin["s2"], vin["sh1"],
                                            vin["sh2"], vin["s3"]])}
            with contextlib.ExitStack() as sA:
                xtmp = ctx.sbuf(sA, "xtmp0", [128, T], F32)
                for c in range(NC):
                    xld(ctx, xtmp, c, q=QS_RR[c % 2])
                    ctx.op("v", lambda eng, a=xb[c], b=xtmp: eng.tensor_copy(a.ap, b.ap),
                           reads=[xtmp], writes=[xb[c]])
            ctx.fence()
            for tt in range(NT):
                sl = slice(tt * 512, (tt + 1) * 512)
                for fc in range(32):
                    for (wd, psl, buf) in ((w1S, 0, 0), (w2S, 1, 1)):
                        slab = wsl[buf]
                        ctx.dma("qw%d" % (buf * 2 + fc % 2),
                                slab[:, 0:D], wd[fc], writes=[slab])
                        ps = sb.ps[psl]
                        ctx.mm(ps.ap, ps,
                               [(slab[:, k * 128:(k + 1) * 128], xb[k][:, sl],
                                 [slab, xb[k]]) for k in range(NC)])
                    ctx.op("s", lambda eng, fc=fc: eng.activation(
                        sb.tmp[0][:, 0:512], sb.ps[0].ap, AF.Silu,
                        scale=sb.vcol(cols["s1"] + fc)),
                        reads=[sb.ps[0], sb.vecs], writes=[sb.tmp[0]])
                    ctx.op("v", lambda eng, fc=fc: eng.tensor_scalar(
                        hA[fc].ap, sb.ps[1].ap, sb.vcol(cols["s2"] + fc), None,
                        ALU.mult),
                        reads=[sb.ps[1], sb.vecs], writes=[hA[fc]])
                    ctx.op("v", lambda eng, fc=fc: eng.tensor_mul(
                        hA[fc].ap, hA[fc].ap, sb.tmp[0][:, 0:512]),
                        reads=[hA[fc], sb.tmp[0]], writes=[hA[fc]])
                for (wd, scol, src_h, dst_h) in ((h1S, cols["sh1"], hA, hB),
                                                 (h2S, cols["sh2"], hB, hA)):
                    for fo in range(32):
                        slab = wsl[fo % 2]
                        ctx.dma("qw%d" % ((fo % 2) * 2), slab[:, 0:2048],
                                wd[fo, :, 0:2048], writes=[slab])
                        ctx.dma("qw%d" % ((fo % 2) * 2 + 1), slab[:, 2048:4096],
                                wd[fo, :, 2048:4096], writes=[slab])
                        ps = sb.ps[2 + fo % 2]
                        ctx.mm(ps.ap, ps,
                               [(slab[:, fi * 128:(fi + 1) * 128], src_h[fi].ap,
                                 [slab, src_h[fi]]) for fi in range(32)])
                        ctx.op("s", lambda eng, fo=fo, ps=ps, scol=scol, dst=dst_h[fo]:
                               eng.activation(dst.ap, ps.ap, AF.Gelu,
                                              scale=sb.vcol(scol + fo)),
                               reads=[ps, sb.vecs], writes=[dst_h[fo]])
                for fc in range(32):
                    slab = wsl[fc % 2]
                    ctx.dma("qw%d" % (fc % 2), slab[:, 0:1024],
                            w3T[fc * 128:(fc + 1) * 128, 0:1024],
                            writes=[slab])
                    for dc in range(NC):
                        ctx.mm(sb.ps[dc].ap, sb.ps[dc],
                               [(slab[:, dc * 128:(dc + 1) * 128], hA[fc].ap,
                                 [slab, hA[fc]])],
                               accumulate=(fc > 0), close=(fc == 31))
                for dc in range(NC):
                    xld(ctx, xr, dc, col0=tt * 512, ncols=512, dcol0=0,
                        q=QS_RR[dc % 2])
                    ctx.op("v", lambda eng, dc=dc: eng.scalar_tensor_tensor(
                        xnew[dc].ap, sb.ps[dc].ap, sb.vcol(cols["s3"] + dc),
                        xr.ap, ALU.mult, ALU.add),
                        reads=[sb.ps[dc], sb.vecs, xr], writes=[xnew[dc]])
                emit_ln_fm(ctx, sb, xnew, xnew, 1e-5, Tl=512)
                for dc in range(NC):
                    xst(ctx, xnew[dc], dc, col0=tt * 512, ncols=512, scol0=0)


def build_l2_ffn():
    nc = bass.Bass("TRN2", target_bir_lowering=False)
    aps = declare_l2_ffn(nc)
    xin = nc.dram_tensor("xin", [D, T], F32, kind="ExternalInput").ap()
    xout = nc.dram_tensor("xout", [D, T], F32, kind="ExternalOutput").ap()
    with contextlib.ExitStack() as stack:
        ctx = Ctx(nc)
        ctx.make_sems(stack)
        with nc.Block() as block:
            emit_l2_ffn(nc, ctx, stack, aps, plain_ld(xin), plain_st(xout))
            ctx.finish()
            ctx.build(block)
    return nc


def _weights_l2_ffn(ins):
    return {
        "w1S": pack_lhsT(np.asarray(ins["L2_ffn_w1_w"]).T),
        "w2S": pack_lhsT(np.asarray(ins["L2_ffn_w2_w"]).T),
        "h1S": pack_lhsT(np.asarray(ins["L2_ffn_h1_w"]).T),
        "h2S": pack_lhsT(np.asarray(ins["L2_ffn_h2_w"]).T),
        "w3T": _f32c(np.asarray(ins["L2_ffn_w3_w"]).T),
        "s1": _f32c(ins["L2_ffn_w1_s"]), "s2": _f32c(ins["L2_ffn_w2_s"]),
        "sh1": _f32c(ins["L2_ffn_h1_s"]), "sh2": _f32c(ins["L2_ffn_h2_s"]),
        "s3": _f32c(ins["L2_ffn_w3_s"]),
    }


def run_l2_ffn(x_fm_cores, ins):
    return _run_block("l2_ffn", build_l2_ffn, x_fm_cores, ins)



_SM_RTS = np.roots([1.0, -3.0, 6.0, -6.0])
_SM_R = float(_SM_RTS[np.isreal(_SM_RTS)].real[0])
_SM_U = (_SM_R - 3.0) / 2.0
_SM_V = float((_SM_R * _SM_R - 3 * _SM_R + 6) - _SM_U * _SM_U)


def declare_l3_attn(nc, p=""):
    return {
        "wqkvT": nc.dram_tensor(p + "wqkvT", [D, 3 * D], F32,
                                kind="ExternalInput").ap(),
        "woT": nc.dram_tensor(p + "woT", [D, D], F32, kind="ExternalInput").ap(),
        "b_out": nc.dram_tensor(p + "b_out", [D], F32, kind="ExternalInput").ap(),
        "idin": nc.dram_tensor(p + "idin", [128, 128], F32,
                               kind="ExternalInput").ap(),
        "qk_d": nc.dram_tensor(p + "qk_d", [2 * D, T], BF16, kind="Internal").ap(),
        "v_d": nc.dram_tensor(p + "v_d", [T, D], BF16, kind="Internal").ap(),
    }


def emit_l3_attn(nc, ctx, stack, aps, xld, xst, nheads=16, mode='full'):
    H = 16
    HD = 64
    SC = float(HD) ** -0.5
    wqkvT, woT, b_out, idin = aps["wqkvT"], aps["woT"], aps["b_out"], aps["idin"]
    qk_d, v_d = aps["qk_d"], aps["v_d"]
    if True:
        sb = _mk_lnbuf(ctx, stack, Tl=512, n_tmp=1)
        sb.ps = [ctx.psum(stack, f"ps{i}", [128, 512], F32) for i in range(8)]
        ident = ctx.sbuf(stack, "ident", [128, 128], F32)
        wsl = [ctx.sbuf(stack, f"wsl{i}", [128, 1024], BF16) for i in range(2)]
        sb.wslab = wsl
        xb = [ctx.sbuf(stack, f"xb{c}", [128, T], BF16) for c in range(NC)]
        xo = [ctx.sbuf(stack, f"xo{c}", [128, T], BF16) for c in range(NC)]
        qh = ctx.sbuf(stack, "qh", [64, T], BF16)
        kh = ctx.sbuf(stack, "kh", [64, T], BF16)
        vhall = ctx.sbuf(stack, "vhall", [128, 1024], BF16)
        p2 = ctx.sbuf(stack, "p2", [128, T], F32)
        p2t = ctx.sbuf(stack, "p2t", [128, 16 * 512], BF16)
        sqb = ctx.sbuf(stack, "sqb", [128, T], F32)
        nmax4 = ctx.sbuf(stack, "nmax4", [128, 4], F32)
        nmax = ctx.sbuf(stack, "nmax", [128, 1], F32)
        bias1 = ctx.sbuf(stack, "bias1", [128, 1], F32)
        bias2 = ctx.sbuf(stack, "bias2", [128, 1], F32)
        zpart = ctx.sbuf(stack, "zpart", [128, 4], F32)
        zr = ctx.sbuf(stack, "zr", [128, 1], F32)
        xnew = [ctx.sbuf(stack, f"xnew{i}", [128, 512], F32) for i in range(NC)]
        xr = ctx.sbuf(stack, "xr", [128, 512], F32)
        qkvb = ctx.sbuf(stack, "qkvb", [128, T], BF16)

        if True:
            emit_const_init(ctx, sb)
            ctx.dma("qm", ident.ap, idin, writes=[ident])
            boc = load_vec(ctx, sb, b_out, D)
            with contextlib.ExitStack() as sA:
                xtmp = ctx.sbuf(sA, "xtmp0", [128, T], F32)
                for c in range(NC):
                    xld(ctx, xtmp, c, q=QS_RR[c % 2])
                    ctx.op("v", lambda eng, a=xb[c], b=xtmp: eng.tensor_copy(a.ap, b.ap),
                           reads=[xtmp], writes=[xb[c]])
            ctx.fence()
            for oc in range(16):
                slab = stream_w_slab(ctx, sb, wqkvT, oc % 2, oc * 128, 128)
                for tt in range(NT):
                    ps = sb.ps[tt % 2]
                    parts = [(slab[:, k * 128:(k + 1) * 128],
                              xb[k][:, tt * 512:(tt + 1) * 512],
                              [slab, xb[k]]) for k in range(NC)]
                    ctx.mm(ps.ap, ps, parts)
                    ctx.op("s", lambda eng, tt=tt, ps=ps: eng.activation(
                        qkvb[:, tt * 512:(tt + 1) * 512], ps.ap, AF.Identity),
                        reads=[ps], writes=[qkvb])
                ctx.dma("qo", qk_d[oc * 128:(oc + 1) * 128, :], qkvb.ap, reads=[qkvb])
            with contextlib.ExitStack() as sV:
                wv = [ctx.sbuf(sV, f"wv{c}", [128, D], BF16) for c in range(NC)]
                for c in range(NC):
                    ctx.dma("qw%d" % (c % 4), wv[c].ap,
                            wqkvT[c * 128:(c + 1) * 128, 2 * D:3 * D], writes=[wv[c]])
                for tt in range(T // 128):
                    for c in range(NC):
                        slab = wv[c]
                        for nh in range(2):
                            ctx.mm(sb.ps[2 + nh].ap, sb.ps[2 + nh],
                                   [(xb[c][:, tt * 128:(tt + 1) * 128],
                                     slab[:, nh * 512:(nh + 1) * 512], [xb[c], slab])],
                                   accumulate=(c > 0), close=(c == NC - 1))
                    for nh in range(2):
                        ctx.op("s", lambda eng, nh=nh: eng.activation(
                            qkvb[:, nh * 512:(nh + 1) * 512], sb.ps[2 + nh].ap, AF.Identity),
                            reads=[sb.ps[2 + nh]], writes=[qkvb])
                    ctx.dma("qo", v_d[tt * 128:(tt + 1) * 128, :], qkvb[:, 0:D],
                            reads=[qkvb])
            ctx.fence()
            for h in range(nheads):
                ctx.dma("qs", qh.ap, qk_d[h * 64:(h + 1) * 64, :], writes=[qh])
                ctx.dma("qs1", kh.ap, qk_d[D + h * 64:D + (h + 1) * 64, :], writes=[kh])
                ctx.dma("qs2", vhall.ap,
                        v_d[0:T, h * 64:(h + 1) * 64].rearrange(
                            "(k p) d -> p k d", p=128),
                        writes=[vhall])
                for qb in range(4):
                    for qt in range(4):
                        for nt in range(4):
                            ps = sb.ps[nt]
                            ctx.mm(ps.ap, ps,
                                   [(qh[:, (qb * 4 + qt) * 128:(qb * 4 + qt + 1) * 128],
                                     kh[:, nt * 512:(nt + 1) * 512], [qh, kh])])
                            ctx.op("v", lambda eng, nt=nt, ps=ps: eng.reduce_max(
                                nmax4[:, nt:nt + 1], ps.ap, AX.X, negate=True),
                                reads=[ps], writes=[nmax4])
                        if mode == "s":
                            ctx.op("v", lambda eng: eng.tensor_copy(
                                p2.ap, sqb.ap), reads=[sqb], writes=[p2])
                            continue
                        ctx.op("v", lambda eng: eng.tensor_reduce(
                            nmax.ap, nmax4.ap, AX.X, ALU.min),
                            reads=[nmax4], writes=[nmax])
                        ctx.op("v", lambda eng: eng.tensor_scalar(
                            bias1.ap, nmax.ap, SC, _SM_U, ALU.mult, ALU.add),
                            reads=[nmax], writes=[bias1])
                        ctx.op("v", lambda eng: eng.tensor_scalar(
                            bias2.ap, nmax.ap, SC, -_SM_R, ALU.mult, ALU.add),
                            reads=[nmax], writes=[bias2])
                        for nt in range(4):
                            ps = sb.ps[nt]
                            sl5 = slice(nt * 512, (nt + 1) * 512)
                            ctx.op("s", lambda eng, ps=ps, sl5=sl5: eng.activation(
                                sqb[:, sl5], ps.ap, AF.Square, bias=bias1[:, 0:1],
                                scale=SC),
                                reads=[ps, bias1], writes=[sqb])
                            if mode == "sm2":
                                ctx.op("v", lambda eng, sl5=sl5: eng.tensor_copy(
                                    p2[:, sl5], sqb[:, sl5]), reads=[sqb], writes=[p2])
                                continue
                            ctx.op("s", lambda eng, ps=ps, sl5=sl5: eng.activation(
                                p2[:, sl5], ps.ap, AF.Identity, bias=bias2[:, 0:1],
                                scale=SC),
                                reads=[ps, bias2], writes=[p2])
                            if mode == "sm3":
                                continue
                            ctx.op("v", lambda eng, sl5=sl5: eng.scalar_tensor_tensor(
                                p2[:, sl5], sqb[:, sl5], _SM_V, p2[:, sl5],
                                ALU.add, ALU.mult),
                                reads=[sqb, p2], writes=[p2])
                            if mode == "sm4":
                                continue
                            ctx.op("s", lambda eng, sl5=sl5: eng.activation(
                                p2[:, sl5], p2[:, sl5], AF.Ln, scale=-1.0 / 6.0),
                                reads=[p2], writes=[p2])
                            ctx.op("s", lambda eng, sl5=sl5, nt=nt: eng.activation(
                                p2[:, sl5], p2[:, sl5], AF.Exp, scale=-1.0,
                                accum_out=zpart[:, nt:nt + 1]),
                                reads=[p2], writes=[p2, zpart])
                        ctx.op("v", lambda eng: eng.reduce_sum(zr.ap, zpart.ap, AX.X),
                               reads=[zpart], writes=[zr])
                        ctx.op("v", lambda eng: eng.reciprocal(zr.ap, zr.ap),
                               reads=[zr], writes=[zr])
                        ctx.op("v", lambda eng: eng.tensor_scalar(
                            p2.ap, p2.ap, zr[:, 0:1], None, ALU.mult),
                            reads=[p2, zr], writes=[p2])
                        if mode.startswith("sm"):
                            continue
                        for kc in range(16):
                            pt = sb.ps[5 + 2 * (kc % 2)]
                            ctx.op("t", lambda eng, kc=kc, pt=pt: eng.transpose(
                                pt[0:128, 0:128], p2[:, kc * 128:(kc + 1) * 128],
                                ident.ap),
                                reads=[p2, ident], writes=[pt])
                            ctx.op("v", lambda eng, kc=kc, qt=qt, pt=pt: eng.tensor_copy(
                                p2t[:, kc * 512 + qt * 128: kc * 512 + (qt + 1) * 128],
                                pt[0:128, 0:128]),
                                reads=[pt], writes=[p2t])
                    if mode != "full":
                        continue
                    psav = sb.ps[4 + 2 * (qb % 2)]
                    for kc in range(16):
                        ctx.mm(psav[0:64, :], psav,
                               [(vhall[:, kc * 64:(kc + 1) * 64],
                                 p2t[:, kc * 512:(kc + 1) * 512],
                                 [vhall, p2t])],
                               accumulate=(kc > 0), close=(kc == 15))
                    ctx.op("s", lambda eng, h=h, qb=qb, psav=psav: eng.activation(
                        xo[h // 2][(h % 2) * 64:(h % 2) * 64 + 64,
                                   qb * 512:(qb + 1) * 512],
                        psav[0:64, :], AF.Identity),
                        reads=[psav], writes=[xo[h // 2]])
            ctx.fence()
            for tt in range(NT):
                sl = slice(tt * 512, (tt + 1) * 512)
                for oc in range(NC):
                    slab = stream_w_slab(ctx, sb, woT, oc % 2, oc * 128, 128)
                    ps = sb.ps[oc % 4]
                    parts = [(slab[:, k * 128:(k + 1) * 128], xo[k][:, sl],
                              [slab, xo[k]]) for k in range(NC)]
                    ctx.mm(ps.ap, ps, parts)
                    xld(ctx, xr, oc, col0=tt * 512, ncols=512, dcol0=0,
                        q=QS_RR[2 + oc % 2])
                    ctx.op("v", lambda eng, oc=oc, ps=ps: eng.tensor_scalar(
                        xnew[oc].ap, ps.ap, 1.0, sb.vcol(boc + oc), ALU.mult, ALU.add),
                        reads=[ps, sb.vecs], writes=[xnew[oc]])
                    ctx.op("v", lambda eng, oc=oc: eng.tensor_add(
                        xnew[oc].ap, xnew[oc].ap, xr.ap),
                        reads=[xnew[oc], xr], writes=[xnew[oc]])
                emit_ln_fm(ctx, sb, xnew, xnew, 1e-5, Tl=512)
                for oc in range(NC):
                    xst(ctx, xnew[oc], oc, col0=tt * 512, ncols=512, scol0=0)


def build_l3_attn(nheads=16, mode='full'):
    nc = bass.Bass("TRN2", target_bir_lowering=False)
    aps = declare_l3_attn(nc)
    xin = nc.dram_tensor("xin", [D, T], F32, kind="ExternalInput").ap()
    xout = nc.dram_tensor("xout", [D, T], F32, kind="ExternalOutput").ap()
    with contextlib.ExitStack() as stack:
        ctx = Ctx(nc)
        ctx.make_sems(stack)
        with nc.Block() as block:
            emit_l3_attn(nc, ctx, stack, aps, plain_ld(xin), plain_st(xout),
                         nheads, mode)
            ctx.finish()
            ctx.build(block)
    return nc


def _weights_l3_attn(ins):
    return {
        "wqkvT": _f32c(np.asarray(ins["L3_attn_w_qkv"]).T),
        "woT": _f32c(np.asarray(ins["L3_attn_w_out"]).T),
        "b_out": _f32c(ins["L3_attn_b_out"]),
        "idin": np.eye(128, dtype=np.float32),
    }


def run_l3_attn(x_fm_cores, ins):
    return _run_block("l3_attn", build_l3_attn, x_fm_cores, ins,
                      wkey="l3_attn")



L3F_DEPTH = 6
L3F_W = 256
L3F_DI = 2730
L3F_NA = 22
L3F_WIDTHS = [128] * 21 + [42]


def declare_l3_ffn(nc, p=""):
    aps = {
        "Wqkv": nc.dram_tensor(p + "Wqkv", [L3F_DEPTH, D, 3 * D], F32,
                               kind="ExternalInput").ap(),
        "Wo": nc.dram_tensor(p + "Wo", [L3F_DEPTH, D, D], F32,
                             kind="ExternalInput").ap(),
        "W1P": nc.dram_tensor(p + "W1P", [L3F_DEPTH, 2 * L3F_NA, 128, D], F32,
                              kind="ExternalInput").ap(),
        "W2P": nc.dram_tensor(p + "W2P", [L3F_DEPTH, L3F_NA * 128, D], F32,
                              kind="ExternalInput").ap(),
        "qscale": nc.dram_tensor(p + "qscale", [L3F_DEPTH, 64], F32,
                                 kind="ExternalInput").ap(),
        "kscale": nc.dram_tensor(p + "kscale", [L3F_DEPTH, 64], F32,
                                 kind="ExternalInput").ap(),
        "ln_g": nc.dram_tensor(p + "ln_g", [L3F_DEPTH, D], F32,
                               kind="ExternalInput").ap(),
        "ln_b": nc.dram_tensor(p + "ln_b", [L3F_DEPTH, D], F32,
                               kind="ExternalInput").ap(),
        "b1": nc.dram_tensor(p + "b1", [L3F_DEPTH, 5460], F32,
                             kind="ExternalInput").ap(),
        "b2": nc.dram_tensor(p + "b2", [L3F_DEPTH, D], F32,
                             kind="ExternalInput").ap(),
        "csq": nc.dram_tensor(p + "csq", [128, T], F32, kind="ExternalInput").ap(),
        "snq": nc.dram_tensor(p + "snq", [128, T], F32, kind="ExternalInput").ap(),
        "csk": nc.dram_tensor(p + "csk", [128, T], F32, kind="ExternalInput").ap(),
        "snk": nc.dram_tensor(p + "snk", [128, T], F32, kind="ExternalInput").ap(),
        "bho": nc.dram_tensor(p + "bho", [128, 8 * 16], F32,
                              kind="ExternalInput").ap(),
        "sel": nc.dram_tensor(p + "sel", [16, 8 * 128], F32,
                              kind="ExternalInput").ap(),
        "mw": nc.dram_tensor(p + "mw", [256, 512], F32, kind="ExternalInput").ap(),
        "idin": nc.dram_tensor(p + "idin", [128, 128], F32,
                               kind="ExternalInput").ap(),
        "qk_d": nc.dram_tensor(p + "qk_d", [2 * D, T], BF16, kind="Internal").ap(),
        "v_d": nc.dram_tensor(p + "v_d", [T, D], BF16, kind="Internal").ap(),
    }
    return aps


def emit_l3_ffn(nc, ctx, stack, aps, xld, xst):
    Wqkv, Wo, W1P, W2P = aps["Wqkv"], aps["Wo"], aps["W1P"], aps["W2P"]
    qscale, kscale = aps["qscale"], aps["kscale"]
    ln_g, ln_b, b1, b2 = aps["ln_g"], aps["ln_b"], aps["b1"], aps["b2"]
    csq, snq, csk, snk = aps["csq"], aps["snq"], aps["csk"], aps["snk"]
    bho, sel, mw, idin = aps["bho"], aps["sel"], aps["mw"], aps["idin"]
    qk_d, v_d = aps["qk_d"], aps["v_d"]
    if True:
        sb = _mk_lnbuf(ctx, stack, Tl=512, n_tmp=1)
        sb.ps = [ctx.psum(stack, f"ps{i}", [128, 512], F32) for i in range(8)]
        ident = ctx.sbuf(stack, "ident", [128, 128], F32)
        wsl = [ctx.sbuf(stack, f"wsl{i}", [128, 1024], BF16) for i in range(2)]
        sb.wslab = wsl
        x3 = [ctx.sbuf(stack, f"x3_{c}", [128, T], F32) for c in range(NC)]
        sxb = [ctx.sbuf(stack, f"sxb{c}", [128, T], BF16) for c in range(NC)]
        qtmp = [ctx.sbuf(stack, f"qtmp{c}", [128, T], BF16) for c in range(NC)]
        ao = qtmp
        sqs = ctx.sbuf(stack, "sqs", [128, 512], BF16)
        ss16 = ctx.sbuf(stack, "ss16", [16, T], F32)
        bhot = ctx.sbuf(stack, "bhot", [128, 8 * 16], BF16)
        selt = ctx.sbuf(stack, "selt", [16, 8 * 128], F32)
        rq = ctx.sbuf(stack, "rq", [128, 512], BF16)
        mwt = [ctx.sbuf(stack, f"mwt{i}", [128, 512], F32) for i in range(2)]
        nmax = ctx.sbuf(stack, "nmax", [128, 1], F32)
        bias1 = ctx.sbuf(stack, "bias1", [128, 1], F32)
        zr = ctx.sbuf(stack, "zr", [128, 1], F32)
        hb = [ctx.sbuf(stack, f"hb{i}", [128, 512], BF16) for i in range(L3F_NA)]
        xnb = [ctx.sbuf(stack, f"xnb{i}", [128, 512], BF16) for i in range(NC)]
        gat = ctx.sbuf(stack, "gat", [128, 512], F32)
        cs_t = ctx.sbuf_at("cs_t", [128, T], BF16, hb[0])
        sn_t = ctx.sbuf_at("sn_t", [128, T], BF16, hb[4])
        qn = ctx.sbuf_at("qn", [128, T], BF16, hb[8])
        rot = ctx.sbuf_at("rot", [128, T], BF16, hb[12])
        t1b = ctx.sbuf_at("t1b", [128, T], BF16, hb[16])
        qh = ctx.sbuf_at("qh", [64, T], BF16, hb[0])
        kh = ctx.sbuf_at("kh", [64, T], BF16, hb[4])
        pt2 = ctx.sbuf_at("pt2", [128, 1024], BF16, hb[8])
        smb = ctx.sbuf_at("smb", [128, 512], F32, hb[10])
        vh = [ctx.sbuf_at(f"vh{i}", [128, 64], BF16, hb[12], delta=i * 128)
              for i in range(16)]
        vhall = ctx.sbuf_at("vhall", [128, 1024], BF16, hb[12])

        if True:
            emit_const_init(ctx, sb)
            ctx.dma("qm", ident.ap, idin, writes=[ident])
            ctx.dma("qw0", bhot.ap, bho, writes=[bhot])
            ctx.dma("qm", selt.ap, sel, writes=[selt])
            for i in range(2):
                ctx.dma("qm", mwt[i].ap, mw[i * 128:(i + 1) * 128, :], writes=[mwt[i]])
            for c in range(NC):
                xld(ctx, x3[c], c, q=QS_RR[c % 4])
            vc_b1a = sb.valloc(22)
            vc_b1g = sb.valloc(22)
            vc_b2 = sb.valloc(8)
            vc_g = sb.valloc(8)
            vc_bb = sb.valloc(8)
            vc_qs = sb.valloc(1)
            vc_ks = sb.valloc(1)

            def load_lvec(dram_row, off, col0, length):
                nchv = ceil_div(length, 128)
                for j in range(nchv):
                    w = min(128, length - j * 128)
                    ctx.dma("qm", sb.vecs[0:w, col0 + j:col0 + j + 1],
                            dram_row[off + j * 128:off + j * 128 + w, None],
                            writes=[sb.vecs])

            for li in range(L3F_DEPTH):
                load_lvec(b1[li], 0, vc_b1a, L3F_DI)
                load_lvec(b1[li], L3F_DI, vc_b1g, L3F_DI)
                load_lvec(b2[li], 0, vc_b2, D)
                load_lvec(ln_g[li], 0, vc_g, D)
                load_lvec(ln_b[li], 0, vc_bb, D)
                for (vc, sc_d) in ((vc_qs, qscale), (vc_ks, kscale)):
                    ctx.dma("qm", sb.vecs[0:64, vc:vc + 1], sc_d[li][:, None],
                            writes=[sb.vecs])
                    ctx.dma("qm", sb.vecs[64:128, vc:vc + 1], sc_d[li][:, None],
                            writes=[sb.vecs])
                cast_stream(ctx, sb, x3, sxb, shift_chunks=(4, 5, 6, 7))
                for (part, cs_d, sn_d, vcs) in ((0, csq, snq, vc_qs),
                                                (1, csk, snk, vc_ks)):
                    ctx.dma("qw2", cs_t.ap, cs_d, writes=[cs_t])
                    ctx.dma("qw3", sn_t.ap, sn_d, writes=[sn_t])
                    for c in range(NC):
                        oc = part * 8 + c
                        slab = stream_w_slab(ctx, sb, Wqkv[li], c % 2, oc * 128, 128)
                        for tt in range(NT):
                            sl = slice(tt * 512, (tt + 1) * 512)
                            ps = sb.ps[tt % 2]
                            ctx.mm(ps.ap, ps,
                                   [(slab[:, k * 128:(k + 1) * 128], sxb[k][:, sl],
                                     [slab, sxb[k]]) for k in range(NC)])
                            ctx.op("s", lambda eng, ps=ps, c=c, sl=sl: eng.activation(
                                qtmp[c][:, sl], ps.ap, AF.Identity),
                                reads=[ps], writes=[qtmp[c]])
                            ctx.op("s", lambda eng, ps=ps: eng.activation(
                                sqs.ap, ps.ap, AF.Square),
                                reads=[ps], writes=[sqs])
                            ctx.mm(sb.ps[2 + tt][0:16, :], sb.ps[2 + tt],
                                   [(bhot[:, c * 16:(c + 1) * 16], sqs.ap,
                                     [bhot, sqs])],
                                   accumulate=(c > 0), close=(c == NC - 1))
                    for tt in range(NT):
                        ctx.op("v", lambda eng, tt=tt: eng.tensor_copy(
                            ss16[0:16, tt * 512:(tt + 1) * 512],
                            sb.ps[2 + tt][0:16, :]),
                            reads=[sb.ps[2 + tt]], writes=[ss16])
                    ctx.op("s", lambda eng: eng.activation(ss16.ap, ss16.ap, AF.Ln),
                           reads=[ss16], writes=[ss16])
                    ctx.op("s", lambda eng: eng.activation(ss16.ap, ss16.ap, AF.Exp,
                                                           scale=-0.5),
                           reads=[ss16], writes=[ss16])
                    ctx.op("v", lambda eng: eng.tensor_scalar(
                        ss16.ap, ss16.ap, 1e12, None, ALU.min),
                        reads=[ss16], writes=[ss16])
                    for c in range(NC):
                        oc = part * 8 + c
                        for tt in range(NT):
                            sl = slice(tt * 512, (tt + 1) * 512)
                            bc = sb.ps[6]
                            ctx.mm(bc.ap, bc,
                                   [(selt[:, c * 128:(c + 1) * 128],
                                     ss16[0:16, sl],
                                     [selt, ss16])])
                            ctx.op("v", lambda eng, bc=bc, vcs=vcs:
                                   eng.tensor_scalar(
                                       rq.ap, bc.ap,
                                       sb.vcol(vcs), None, ALU.mult),
                                   reads=[bc, sb.vecs], writes=[rq])
                            ctx.op("v", lambda eng, c=c, sl=sl: eng.tensor_mul(
                                qn[:, sl], qtmp[c][:, sl], rq.ap),
                                reads=[qtmp[c], rq], writes=[qn])
                        for (da, sa) in ((0, 32), (32, 0), (64, 96), (96, 64)):
                            ctx.dma("qs2" if da < 64 else "qs3",
                                    rot.h[da:da + 32, :], qn.h[sa:sa + 32, :],
                                    reads=[qn], writes=[rot])
                        ctx.op("v", lambda eng: eng.tensor_mul(t1b.ap, qn.ap, cs_t.ap),
                               reads=[qn, cs_t], writes=[t1b])
                        ctx.op("v", lambda eng: eng.tensor_mul(rot.ap, rot.ap, sn_t.ap),
                               reads=[rot, sn_t], writes=[rot])
                        ctx.op("v", lambda eng: eng.tensor_add(t1b.ap, t1b.ap, rot.ap),
                               reads=[t1b, rot], writes=[t1b])
                        ctx.dma("qo", qk_d[oc * 128:(oc + 1) * 128, :], t1b.ap,
                                reads=[t1b])
                for tt in range(T // 128):
                    for c in range(NC):
                        slab = wsl[c % 2]
                        ctx.dma("qw%d" % (2 + c % 2), slab.ap,
                                Wqkv[li][c * 128:(c + 1) * 128, 2 * D:3 * D],
                                writes=[slab])
                        for nh in range(2):
                            ctx.mm(sb.ps[nh].ap, sb.ps[nh],
                                   [(sxb[c][:, tt * 128:(tt + 1) * 128],
                                     slab[:, nh * 512:(nh + 1) * 512],
                                     [sxb[c], slab])],
                                   accumulate=(c > 0), close=(c == NC - 1))
                    for nh in range(2):
                        ctx.op("s", lambda eng, nh=nh: eng.activation(
                            t1b[:, nh * 512:(nh + 1) * 512], sb.ps[nh].ap,
                            AF.Identity),
                            reads=[sb.ps[nh]], writes=[t1b])
                    ctx.dma("qo", v_d[tt * 128:(tt + 1) * 128, :], t1b[:, 0:D],
                            reads=[t1b])
                ctx.fence()
                for h in range(16):
                    ctx.dma("qs", qh.ap, qk_d[h * 64:(h + 1) * 64, :], writes=[qh])
                    ctx.dma("qs1", kh.ap, qk_d[D + h * 64:D + (h + 1) * 64, :],
                            writes=[kh])
                    ctx.dma("qs2", vhall.ap,
                            v_d[0:T, h * 64:(h + 1) * 64].rearrange(
                                "(k p) d -> p k d", p=128),
                            writes=[vhall] + vh)
                    for w in range(8):
                        kw = 256 if w == 0 else 512
                        k0 = 0 if w == 0 else (w - 1) * 256
                        for qt2 in range(2):
                            qlo = w * 256 + qt2 * 128
                            ps = sb.ps[qt2]
                            ctx.mm(ps[:, 0:kw], ps,
                                   [(qh[:, qlo:qlo + 128], kh[:, k0:k0 + kw],
                                     [qh, kh])])
                            mtb = mwt[qt2]
                            mtap = mtb[:, 256:512] if w == 0 else mtb[:, 0:512]
                            ctx.op("v", lambda eng, ps=ps, mtap=mtap, kw=kw:
                                   eng.tensor_add(smb[:, 0:kw], ps[:, 0:kw],
                                                  mtap),
                                   reads=[ps, mtb], writes=[smb])
                            ctx.op("v", lambda eng, kw=kw: eng.reduce_max(
                                nmax[:, 0:1], smb[:, 0:kw], AX.X, negate=True),
                                reads=[smb], writes=[nmax])
                            ctx.op("v", lambda eng: eng.tensor_scalar(
                                bias1.ap, nmax.ap, 8.0, None, ALU.mult),
                                reads=[nmax], writes=[bias1])
                            ctx.op("s", lambda eng, kw=kw: eng.activation(
                                smb[:, 0:kw], smb[:, 0:kw], AF.Exp,
                                bias=bias1[:, 0:1], scale=8.0,
                                accum_out=zr[:, 0:1]),
                                reads=[smb, bias1], writes=[smb, zr])
                            ctx.op("v", lambda eng: eng.reciprocal(zr.ap, zr.ap),
                                   reads=[zr], writes=[zr])
                            ctx.op("v", lambda eng, kw=kw: eng.tensor_scalar(
                                smb[:, 0:kw], smb[:, 0:kw], zr[:, 0:1], None,
                                ALU.mult),
                                reads=[smb, zr], writes=[smb])
                            for kc2 in range(kw // 128):
                                pt = sb.ps[5 + (kc2 % 2)]
                                ctx.op("t", lambda eng, kc2=kc2, pt=pt: eng.transpose(
                                    pt[0:128, 0:128],
                                    smb[:, kc2 * 128:(kc2 + 1) * 128], ident.ap),
                                    reads=[smb, ident], writes=[pt])
                                ctx.op("v", lambda eng, kc2=kc2, qt2=qt2, pt=pt:
                                       eng.tensor_copy(
                                           pt2[:, kc2 * 256 + qt2 * 128:
                                               kc2 * 256 + (qt2 + 1) * 128],
                                           pt[0:128, 0:128]),
                                       reads=[pt], writes=[pt2])
                        nkc = kw // 128
                        psav = sb.ps[4] if w % 2 == 0 else sb.ps[2]
                        for kc2 in range(nkc):
                            kcg = (k0 // 128) + kc2
                            ctx.mm(psav[0:64, 0:256], psav,
                                   [(vh[kcg].ap, pt2[:, kc2 * 256:(kc2 + 1) * 256],
                                     [vh[kcg], pt2])],
                                   accumulate=(kc2 > 0), close=(kc2 == nkc - 1))
                        ctx.op("s", lambda eng, h=h, w=w, psav=psav: eng.activation(
                            ao[h // 2][(h % 2) * 64:(h % 2) * 64 + 64,
                                       w * 256:(w + 1) * 256],
                            psav[0:64, 0:256], AF.Identity),
                            reads=[psav], writes=[ao[h // 2]])
                ctx.fence()
                for tt in range(NT):
                    sl = slice(tt * 512, (tt + 1) * 512)
                    for oc in range(NC):
                        slab = stream_w_slab(ctx, sb, Wo[li], oc % 2, oc * 128, 128)
                        ps = sb.ps[oc % 4]
                        ctx.mm(ps.ap, ps,
                               [(slab[:, k * 128:(k + 1) * 128], ao[k][:, sl],
                                 [slab, ao[k]]) for k in range(NC)])
                        ctx.op("v", lambda eng, oc=oc, ps=ps, sl=sl: eng.tensor_add(
                            x3[oc][:, sl], x3[oc][:, sl], ps.ap),
                            reads=[ps, x3[oc]], writes=[x3[oc]])
                ctx.fence()
                cast_stream(ctx, sb, x3, sxb, shift_chunks=(4, 5, 6, 7))
                for tt in range(NT):
                    sl = slice(tt * 512, (tt + 1) * 512)
                    emit_ln_fm(ctx, sb, sxb, xnb, 1e-5, gamma_col=vc_g,
                               beta_col=vc_bb, src_bf16=True, Tl=512,
                               src_off=tt * 512, dst_off=0)
                    for j in range(L3F_NA):
                        wj = L3F_WIDTHS[j]
                        slab_a = wsl[0]
                        slab_g = wsl[1]
                        ctx.dma("qw0", slab_a.ap, W1P[li, j], writes=[slab_a])
                        ctx.dma("qw1", slab_g.ap, W1P[li, L3F_NA + j],
                                writes=[slab_g])
                        psa = sb.ps[0]
                        psg = sb.ps[1]
                        ctx.mm(psa.ap, psa,
                               [(slab_a[:, k * 128:(k + 1) * 128], xnb[k].ap,
                                 [slab_a, xnb[k]]) for k in range(NC)])
                        ctx.mm(psg.ap, psg,
                               [(slab_g[:, k * 128:(k + 1) * 128], xnb[k].ap,
                                 [slab_g, xnb[k]]) for k in range(NC)])
                        ctx.op("v", lambda eng, psa=psa, j=j, wj=wj: eng.tensor_scalar(
                            gat[0:wj, :], psa[0:wj, :], sb.vcol(vc_b1a + j)[0:wj, :],
                            None, ALU.add),
                            reads=[psa, sb.vecs], writes=[gat])
                        ctx.op("s", lambda eng, psg=psg, j=j, wj=wj: eng.activation(
                            sqs[0:wj, :], psg[0:wj, :], AF.Gelu,
                            bias=sb.vcol(vc_b1g + j)[0:wj, :]),
                            reads=[psg, sb.vecs], writes=[sqs])
                        ctx.op("v", lambda eng, j=j, wj=wj: eng.tensor_mul(
                            hb[j][0:wj, :], gat[0:wj, :], sqs[0:wj, :]),
                            reads=[gat, sqs], writes=[hb[j]])
                    for j in range(L3F_NA):
                        wj = L3F_WIDTHS[j]
                        slab = wsl[j % 2]
                        ctx.dma("qw%d" % (2 + j % 2), slab.ap,
                                W2P[li, j * 128:(j + 1) * 128, :], writes=[slab])
                        for dc in range(NC):
                            ctx.mm(sb.ps[dc].ap, sb.ps[dc],
                                   [(slab[0:wj, dc * 128:(dc + 1) * 128],
                                     hb[j][0:wj, :], [slab, hb[j]])],
                                   accumulate=(j > 0), close=(j == L3F_NA - 1))
                    for dc in range(NC):
                        ctx.op("v", lambda eng, dc=dc, sl=sl: eng.scalar_tensor_tensor(
                            x3[dc][:, sl], sb.ps[dc].ap, sb.vcol(vc_b2 + dc),
                            x3[dc][:, sl], ALU.add, ALU.add),
                            reads=[sb.ps[dc], sb.vecs, x3[dc]], writes=[x3[dc]])
                ctx.fence()
            xnew = [ctx.sbuf_at(f"xnew{i}", [128, 512], F32, hb[0], delta=i * 2048)
                    for i in range(NC)]
            xr = ctx.sbuf_at("xr", [128, 512], F32, hb[16])
            ctx.fence()
            for tt in range(NT):
                sl = slice(tt * 512, (tt + 1) * 512)
                for oc in range(NC):
                    xld(ctx, xr, oc, col0=tt * 512, ncols=512, dcol0=0,
                        q=QS_RR[oc % 2])
                    ctx.op("v", lambda eng, oc=oc, sl=sl: eng.tensor_add(
                        xnew[oc].ap, x3[oc][:, sl], xr.ap),
                        reads=[x3[oc], xr], writes=[xnew[oc]])
                emit_ln_fm(ctx, sb, xnew, xnew, 1e-5, Tl=512)
                for oc in range(NC):
                    xst(ctx, xnew[oc], oc, col0=tt * 512, ncols=512, scol0=0)


def build_l3_ffn():
    nc = bass.Bass("TRN2", target_bir_lowering=False)
    aps = declare_l3_ffn(nc)
    xin = nc.dram_tensor("xin", [D, T], F32, kind="ExternalInput").ap()
    xout = nc.dram_tensor("xout", [D, T], F32, kind="ExternalOutput").ap()
    with contextlib.ExitStack() as stack:
        ctx = Ctx(nc)
        ctx.make_sems(stack)
        with nc.Block() as block:
            emit_l3_ffn(nc, ctx, stack, aps, plain_ld(xin), plain_st(xout))
            ctx.finish()
            ctx.build(block)
    return nc


def _l3f_rotary_tables():
    DIM_HEAD = 64
    WINDOW = 256
    n = T
    half = np.arange(0, DIM_HEAD, 2, dtype=np.float64)
    inv_freq = 1.0 / 10000.0 ** (half / DIM_HEAD)
    t = np.arange(n, dtype=np.float64)
    freqs = t[:, None] * inv_freq[None, :]
    freqs = np.concatenate([freqs, freqs], -1)
    base_scale = (half + 0.4 * DIM_HEAD) / (1.4 * DIM_HEAD)
    power = (t - n // 2) / (WINDOW // 2)
    scale = base_scale[None, :] ** power[:, None]
    scale = np.concatenate([scale, scale], -1)
    cos = np.cos(freqs)
    sin = np.sin(freqs)
    sgn = np.where(np.arange(DIM_HEAD) < 32, -1.0, 1.0)[None, :]
    csq_ = (cos * scale).T
    snq_ = (sin * scale * sgn).T
    csk_ = (cos / scale).T
    snk_ = (sin / scale * sgn).T
    stack2 = lambda a: np.ascontiguousarray(
        np.concatenate([a, a], axis=0), dtype=np.float32)
    return stack2(csq_), stack2(snq_), stack2(csk_), stack2(snk_)


def _weights_l3_ffn(ins):
    csq_, snq_, csk_, snk_ = _l3f_rotary_tables()
    bho = np.zeros((128, 8 * 16), dtype=np.float32)
    sel = np.zeros((16, 8 * 128), dtype=np.float32)
    for c in range(8):
        bho[0:64, c * 16 + 2 * c] = 1.0
        bho[64:128, c * 16 + 2 * c + 1] = 1.0
        sel[2 * c, c * 128:c * 128 + 64] = 1.0
        sel[2 * c + 1, c * 128 + 64:(c + 1) * 128] = 1.0
    i_ = np.arange(256)[:, None]
    jj = np.arange(256)[None, :]
    m0 = np.where(jj <= i_, 0.0, -1e30).astype(np.float32)
    j_ = np.arange(512)[None, :]
    mwv = np.where((j_ >= i_) & (j_ <= i_ + 256), 0.0, -1e30).astype(np.float32)
    W1 = np.asarray(ins["L3_ffn_W1"], dtype=np.float32)
    W1X = np.zeros((L3F_DEPTH, 1024, 2 * L3F_NA * 128), dtype=np.float32)
    for b in range(L3F_NA):
        wj = L3F_WIDTHS[b]
        W1X[:, :, b * 128:b * 128 + wj] = W1[:, :, b * 128:b * 128 + wj]
        W1X[:, :, (L3F_NA + b) * 128:(L3F_NA + b) * 128 + wj] = \
            W1[:, :, L3F_DI + b * 128:L3F_DI + b * 128 + wj]
    W1Pv = np.stack([pack_lhsT(W1X[l]) for l in range(L3F_DEPTH)], axis=0)
    W2 = np.asarray(ins["L3_ffn_W2"], dtype=np.float32)
    W2Pv = np.zeros((L3F_DEPTH, L3F_NA * 128, D), dtype=np.float32)
    W2Pv[:, 0:L3F_DI, :] = W2
    return {
        "Wqkv": _f32c(ins["L3_ffn_Wqkv"]), "Wo": _f32c(ins["L3_ffn_Wo"]),
        "W1P": W1Pv, "W2P": W2Pv,
        "qscale": _f32c(ins["L3_ffn_q_scale"]), "kscale": _f32c(ins["L3_ffn_k_scale"]),
        "ln_g": _f32c(ins["L3_ffn_ln_g"]), "ln_b": _f32c(ins["L3_ffn_ln_b"]),
        "b1": _f32c(ins["L3_ffn_b1"]), "b2": _f32c(ins["L3_ffn_b2"]),
        "csq": csq_, "snq": snq_, "csk": csk_, "snk": snk_,
        "bho": bho, "sel": sel, "mw": mwv,
        "idin": np.eye(128, dtype=np.float32),
    }


def run_l3_ffn(x_fm_cores, ins):
    return _run_block("l3_ffn", build_l3_ffn, x_fm_cores, ins)




def declare_l2_attn(nc, p=""):
    return {
        "winT": nc.dram_tensor(p + "winT", [D, 3 * D], F32,
                               kind="ExternalInput").ap(),
        "b_in": nc.dram_tensor(p + "b_in", [3 * D], F32, kind="ExternalInput").ap(),
        "kde": nc.dram_tensor(p + "kde", [11, 64], F32, kind="ExternalInput").ap(),
        "vde": nc.dram_tensor(p + "vde", [11, 64], F32, kind="ExternalInput").ap(),
        "woT": nc.dram_tensor(p + "woT", [D, D], F32, kind="ExternalInput").ap(),
        "b_out": nc.dram_tensor(p + "b_out", [D], F32, kind="ExternalInput").ap(),
        "bho": nc.dram_tensor(p + "bho", [128, 8 * 16], F32,
                              kind="ExternalInput").ap(),
        "sel": nc.dram_tensor(p + "sel", [16, 8 * 128], F32,
                              kind="ExternalInput").ap(),
        "idin": nc.dram_tensor(p + "idin", [128, 128], F32,
                               kind="ExternalInput").ap(),
    }


def emit_l2_attn(nc, ctx, stack, aps, xld, xst):
    S8 = 8
    B256 = 256
    winT, b_in, kde, vde = aps["winT"], aps["b_in"], aps["kde"], aps["vde"]
    woT, b_out, bho, sel, idin = (aps["woT"], aps["b_out"], aps["bho"],
                                  aps["sel"], aps["idin"])
    if True:
        sb = _mk_lnbuf(ctx, stack, Tl=512, n_tmp=1)
        sb.ps = [ctx.psum(stack, f"ps{i}", [128, 512], F32) for i in range(8)]
        ident = ctx.sbuf(stack, "ident", [128, 128], F32)
        wsl = [ctx.sbuf(stack, f"wsl{i}", [128, 1024], BF16) for i in range(2)]
        sb.wslab = wsl
        xb = [ctx.sbuf(stack, f"xb{c}", [128, T], BF16) for c in range(NC)]
        qc = [ctx.sbuf(stack, f"qc{c}", [128, T], BF16) for c in range(NC)]
        kc = [ctx.sbuf(stack, f"kc{c}", [128, T], BF16) for c in range(NC)]
        vc = [ctx.sbuf(stack, f"vc{c}", [128, T], BF16) for c in range(NC)]
        kdt = ctx.sbuf(stack, "kdt", [128, 11], F32)
        vdt = ctx.sbuf(stack, "vdt", [128, 11], F32)
        bhot = ctx.sbuf(stack, "bhot", [128, 8 * 16], BF16)
        selt = ctx.sbuf(stack, "selt", [16, 8 * 128], F32)
        Lb = [ctx.sbuf(stack, f"Lb{i}", [128, 1024], F32) for i in range(2)]
        Pb = [ctx.sbuf(stack, f"Pb{i}", [128, 1024], F32) for i in range(2)]
        gm = ctx.sbuf(stack, "gm", [128, 512], F32)
        gz = ctx.sbuf(stack, "gz", [128, 128], F32)
        pt16 = ctx.sbuf(stack, "pt16", [16, 256], F32)
        vpl = ctx.sbuf(stack, "vpl", [128, 256], BF16)
        tmp1 = ctx.sbuf(stack, "tmp1", [128, 256], F32)
        tmp1b = ctx.sbuf(stack, "tmp1b", [128, 128], BF16)
        scr16 = ctx.sbuf(stack, "scr16", [128, 16], F32)
        out1 = [ctx.sbuf_at(f"out1_{c}", [128, T], F32,
                            qc[2 * c] if c < 4 else kc[2 * (c - 4)])
                for c in range(NC)]
        ob = [ctx.sbuf_at(f"ob{c}", [128, T], BF16, vc[c]) for c in range(NC)]
        xnew = [ctx.sbuf_at(f"xnew{i}", [128, 512], F32, Lb[0], delta=i * 2048)
                for i in range(NC)]
        xr = ctx.sbuf_at("xr", [128, 512], F32, gm)

        if True:
            emit_const_init(ctx, sb)
            ctx.dma("qm", ident.ap, idin, writes=[ident])
            ctx.dma("qw0", bhot.ap, bho, writes=[bhot])
            ctx.dma("qm", selt.ap, sel, writes=[selt])
            for j in range(11):
                for half in range(2):
                    ctx.dma("qm", kdt[half * 64:(half + 1) * 64, j:j + 1],
                            kde[j][:, None], writes=[kdt])
                    ctx.dma("qm", vdt[half * 64:(half + 1) * 64, j:j + 1],
                            vde[j][:, None], writes=[vdt])
            bic = load_vec(ctx, sb, b_in, 3 * D)
            boc = load_vec(ctx, sb, b_out, D)
            ctx.op("v", lambda eng: eng.tensor_scalar(
                sb.vecs[:, bic:bic + 8], sb.vecs[:, bic:bic + 8], 0.125, None,
                ALU.mult),
                reads=[sb.vecs], writes=[sb.vecs])
            with contextlib.ExitStack() as sA:
                xtmp = ctx.sbuf(sA, "xtmp0", [128, T], F32)
                for c in range(NC):
                    xld(ctx, xtmp, c, q=QS_RR[c % 2])
                    ctx.op("v", lambda eng, a=xb[c], b=xtmp: eng.tensor_copy(a.ap, b.ap),
                           reads=[xtmp], writes=[xb[c]])
            ctx.fence()
            for grp, dst in ((0, qc), (1, kc), (2, vc)):
                for c in range(NC):
                    oc = grp * 8 + c
                    slab = stream_w_slab(ctx, sb, winT, c % 2, oc * 128, 128)
                    for tt in range(NT):
                        sl = slice(tt * 512, (tt + 1) * 512)
                        ps = sb.ps[tt % 2]
                        ctx.mm(ps.ap, ps,
                               [(slab[:, k * 128:(k + 1) * 128], xb[k][:, sl],
                                 [slab, xb[k]]) for k in range(NC)])
                        sc = 0.125 if grp == 0 else 1.0
                        ctx.op("s", lambda eng, ps=ps, dst=dst, c=c, sl=sl, sc=sc,
                               oc=oc: eng.activation(
                                   dst[c][:, sl], ps.ap, AF.Identity,
                                   bias=sb.vcol(bic + oc), scale=sc),
                               reads=[ps, sb.vecs], writes=[dst[c]])
            for bh in range(2):
                for s in range(S8):
                    for t in range(S8):
                        dist = abs(s - t)
                        p16 = sb.ps[4]
                        for c in range(NC):
                            ksl = kc[c][:, t * B256 + bh * 128:
                                        t * B256 + (bh + 1) * 128]
                            qsl = qc[c][:, s * B256 + bh * 128:
                                        s * B256 + (bh + 1) * 128]
                            ctx.op("v", lambda eng, ksl=ksl, c=c, dist=dist:
                                   eng.tensor_scalar(
                                       vpl[:, 0:128], ksl,
                                       kdt[:, dist:dist + 1], None, ALU.add),
                                   reads=[kc[c], kdt], writes=[vpl])
                            ctx.op("v", lambda eng, qsl=qsl, c=c: eng.tensor_mul(
                                tmp1b.ap, qsl, vpl[:, 0:128]),
                                reads=[qc[c], vpl], writes=[tmp1b])
                            ctx.mm(p16[0:16, 0:128], p16,
                                   [(bhot[:, c * 16:(c + 1) * 16],
                                     tmp1b.ap, [bhot, tmp1b])],
                                   accumulate=(c > 0), close=(c == NC - 1))
                        ctx.op("v", lambda eng, p16=p16: eng.tensor_copy(
                            pt16[0:16, 0:128], p16[0:16, 0:128]),
                            reads=[p16], writes=[pt16])
                        pt = sb.ps[5]
                        ctx.op("t", lambda eng: eng.transpose(
                            pt[0:128, 0:16], pt16[0:16, 0:128], ident[0:16, 0:16]),
                            reads=[pt16, ident], writes=[pt])
                        ctx.op("v", lambda eng, bh=bh, s=s, t=t, pt=pt: eng.tensor_copy(
                            bass.AP(Lb[bh].h, s * 128 + t, [[1024, 128], [8, 16]]),
                            pt[0:128, 0:16]),
                            reads=[pt], writes=[Lb[bh]])
                L = Lb[bh]
                P = Pb[bh]
                ctx.op("v", lambda eng, L=L: eng.tensor_max(
                    gm[:, 0:512], bass.AP(L.h, 0, [[1024, 128], [2, 512]]),
                    bass.AP(L.h, 1, [[1024, 128], [2, 512]])),
                    reads=[L], writes=[gm])
                ctx.op("v", lambda eng: eng.tensor_max(
                    gm[:, 0:256], bass.AP(gm.h, 0, [[512, 128], [2, 256]]),
                    bass.AP(gm.h, 1, [[512, 128], [2, 256]])),
                    reads=[gm], writes=[gm])
                ctx.op("v", lambda eng: eng.tensor_max(
                    gz[:, 0:128], bass.AP(gm.h, 0, [[512, 128], [2, 128]]),
                    bass.AP(gm.h, 1, [[512, 128], [2, 128]])),
                    reads=[gm], writes=[gz])
                ctx.op("v", lambda eng, L=L, P=P: eng.tensor_sub(
                    P.ap, L.ap,
                    bass.AP(gz.h, 0, [[128, 128], [1, 128], [0, 8]])),
                    reads=[L, gz], writes=[P])
                ctx.op("s", lambda eng, P=P: eng.activation(P.ap, P.ap, AF.Exp),
                       reads=[P], writes=[P])
                ctx.op("v", lambda eng, P=P: eng.tensor_add(
                    gm[:, 0:512], bass.AP(P.h, 0, [[1024, 128], [2, 512]]),
                    bass.AP(P.h, 1, [[1024, 128], [2, 512]])),
                    reads=[P], writes=[gm])
                ctx.op("v", lambda eng: eng.tensor_add(
                    gm[:, 0:256], bass.AP(gm.h, 0, [[512, 128], [2, 256]]),
                    bass.AP(gm.h, 1, [[512, 128], [2, 256]])),
                    reads=[gm], writes=[gm])
                ctx.op("v", lambda eng: eng.tensor_add(
                    gz[:, 0:128], bass.AP(gm.h, 0, [[512, 128], [2, 128]]),
                    bass.AP(gm.h, 1, [[512, 128], [2, 128]])),
                    reads=[gm], writes=[gz])
                ctx.op("v", lambda eng: eng.reciprocal(gz[:, 0:128], gz[:, 0:128]),
                       reads=[gz], writes=[gz])
                ctx.op("v", lambda eng, P=P: eng.tensor_mul(
                    P.ap, P.ap,
                    bass.AP(gz.h, 0, [[128, 128], [1, 128], [0, 8]])),
                    reads=[P, gz], writes=[P])
            ctx.fence()
            for c in range(NC):
                ctx.op("v", lambda eng, c=c: eng.memset(out1[c].ap, 0.0),
                       writes=[out1[c]])
            for bh in range(2):
                for s in range(S8):
                    for t in range(S8):
                        dist = abs(s - t)
                        ctx.op("v", lambda eng, bh=bh, s=s, t=t: eng.tensor_copy(
                            scr16.ap,
                            bass.AP(Pb[bh].h, s * 128 + t, [[1024, 128], [8, 16]])),
                            reads=[Pb[bh]], writes=[scr16])
                        pt = sb.ps[5]
                        ctx.op("t", lambda eng: eng.transpose(
                            pt[0:16, 0:128], scr16.ap, ident.ap),
                            reads=[scr16, ident], writes=[pt])
                        ctx.op("v", lambda eng, pt=pt: eng.tensor_copy(
                            pt16[0:16, 0:128], pt[0:16, 0:128]),
                            reads=[pt], writes=[pt16])
                        for c in range(NC):
                            bc = sb.ps[6]
                            ctx.mm(bc[:, 0:128], bc,
                                   [(selt[:, c * 128:(c + 1) * 128],
                                     pt16[0:16, 0:128], [selt, pt16])])
                            vsl = vc[c][:, t * B256 + bh * 128:
                                        t * B256 + (bh + 1) * 128]
                            ctx.op("v", lambda eng, vsl=vsl, c=c, dist=dist:
                                   eng.tensor_scalar(
                                       vpl[:, 0:128], vsl,
                                       vdt[:, dist:dist + 1], None, ALU.add),
                                   reads=[vc[c], vdt], writes=[vpl])
                            ctx.op("v", lambda eng, bc=bc: eng.tensor_mul(
                                tmp1[:, 0:128], vpl[:, 0:128], bc[:, 0:128]),
                                reads=[vpl, bc], writes=[tmp1])
                            osl = slice(s * B256 + bh * 128,
                                        s * B256 + (bh + 1) * 128)
                            ctx.op("v", lambda eng, c=c, osl=osl: eng.tensor_add(
                                out1[c][:, osl], out1[c][:, osl], tmp1[:, 0:128]),
                                reads=[out1[c], tmp1], writes=[out1[c]])
            ctx.fence()
            for c in range(NC):
                ctx.op("v", lambda eng, c=c: eng.tensor_copy(ob[c].ap, out1[c].ap),
                       reads=[out1[c]], writes=[ob[c]])
            ctx.fence()
            for tt in range(NT):
                sl = slice(tt * 512, (tt + 1) * 512)
                for oc in range(NC):
                    slab = stream_w_slab(ctx, sb, woT, oc % 2, oc * 128, 128)
                    ps = sb.ps[oc % 4]
                    ctx.mm(ps.ap, ps,
                           [(slab[:, k * 128:(k + 1) * 128], ob[k][:, sl],
                             [slab, ob[k]]) for k in range(NC)])
                    xld(ctx, xr, oc, col0=tt * 512, ncols=512, dcol0=0,
                        q=QS_RR[2 + oc % 2])
                    ctx.op("v", lambda eng, oc=oc, ps=ps: eng.tensor_scalar(
                        xnew[oc].ap, ps.ap, 1.0, sb.vcol(boc + oc), ALU.mult,
                        ALU.add),
                        reads=[ps, sb.vecs], writes=[xnew[oc]])
                    ctx.op("v", lambda eng, oc=oc: eng.tensor_add(
                        xnew[oc].ap, xnew[oc].ap, xr.ap),
                        reads=[xnew[oc], xr], writes=[xnew[oc]])
                emit_ln_fm(ctx, sb, xnew, xnew, 1e-5, Tl=512)
                for oc in range(NC):
                    xst(ctx, xnew[oc], oc, col0=tt * 512, ncols=512, scol0=0)


def build_l2_attn():
    nc = bass.Bass("TRN2", target_bir_lowering=False)
    aps = declare_l2_attn(nc)
    xin = nc.dram_tensor("xin", [D, T], F32, kind="ExternalInput").ap()
    xout = nc.dram_tensor("xout", [D, T], F32, kind="ExternalOutput").ap()
    with contextlib.ExitStack() as stack:
        ctx = Ctx(nc)
        ctx.make_sems(stack)
        with nc.Block() as block:
            emit_l2_attn(nc, ctx, stack, aps, plain_ld(xin), plain_st(xout))
            ctx.finish()
            ctx.build(block)
    return nc


def _weights_l2_attn(ins):
    bho = np.zeros((128, 8 * 16), dtype=np.float32)
    sel = np.zeros((16, 8 * 128), dtype=np.float32)
    for c in range(8):
        bho[0:64, c * 16 + 2 * c] = 1.0
        bho[64:128, c * 16 + 2 * c + 1] = 1.0
        sel[2 * c, c * 128:c * 128 + 64] = 1.0
        sel[2 * c + 1, c * 128 + 64:(c + 1) * 128] = 1.0
    return {
        "winT": _f32c(np.asarray(ins["L2_attn_W_in"]).T),
        "b_in": _f32c(ins["L2_attn_b_in"]),
        "kde": _f32c(ins["L2_attn_k_dist_emb"]),
        "vde": _f32c(ins["L2_attn_v_dist_emb"]),
        "woT": _f32c(np.asarray(ins["L2_attn_W_out"]).T),
        "b_out": _f32c(ins["L2_attn_b_out"]),
        "bho": bho, "sel": sel, "idin": np.eye(128, dtype=np.float32),
    }


def run_l2_attn(x_fm_cores, ins):
    return _run_block("l2_attn", build_l2_attn, x_fm_cores, ins)


_BUILT = {}


def _get(name, builder):
    if name not in _BUILT:
        _BUILT[name] = builder()
    return _BUILT[name]


def _f32c(a):
    return np.ascontiguousarray(a, dtype=np.float32)


def pack_lhsT(Wt):
    K, M = Wt.shape
    nk, nm = K // 128, M // 128
    out = Wt.reshape(nk, 128, nm, 128).transpose(2, 1, 0, 3).reshape(nm, 128, nk * 128)
    return np.ascontiguousarray(out, dtype=np.float32)


def _weights_l0_attn(ins):
    base = {}
    for short, full in [("Wg", "Wg"), ("Wi", "Wi"), ("Wfg", "Wfg"), ("Wfu", "Wfu"),
                        ("Wfd", "Wfd"), ("Wo", "Wo"), ("bg", "bg"), ("bi", "bi"),
                        ("A", "A"), ("Bp", "Bp"), ("Cp", "Cp"), ("Dp", "Dp"),
                        ("gamma", "gamma"), ("beta", "beta"), ("bfg", "bfg"),
                        ("bfu", "bfu"), ("bfd", "bfd"), ("bo", "bo")]:
        base[short] = _f32c(ins["L0_attn_" + full])
    return base


def _weights_l0_ffn(ins):
    return {
        "W1T": _f32c(np.asarray(ins["L0_ffn_W1"]).T),
        "W2T": _f32c(np.asarray(ins["L0_ffn_W2"]).T),
        "b1": _f32c(ins["L0_ffn_b1"]), "b2": _f32c(ins["L0_ffn_b2"]),
    }


IN_MAPS = {"l0_attn": _weights_l0_attn, "l0_ffn": _weights_l0_ffn,
           "l1_attn": _weights_l1_attn, "l1_ffn": _weights_l1_ffn,
           "l2_attn": _weights_l2_attn,
           "l2_ffn": _weights_l2_ffn, "l3_attn": _weights_l3_attn,
           "l3_ffn": _weights_l3_ffn}


def _run_block(name, builder, x_fm_cores, ins, wkey=None):
    nc = _get(name, builder)
    base = IN_MAPS[wkey or name](ins)
    in_maps = [dict(base, xin=_f32c(x_fm_cores[c])) for c in range(NCORES)]
    res = run_bass_kernel_spmd(nc, in_maps, core_ids=list(range(NCORES)))
    return [r["xout"] for r in res.results]


def run_l0_attn(x_fm_cores, ins):
    return _run_block("l0_attn", build_l0_attn, x_fm_cores, ins)


def _to_l2_layout(x_fm):
    F = np.stack([np.asarray(f).T for f in x_fm], axis=0)
    out = []
    for c in range(NCORES):
        sl = F[:, c * 256:(c + 1) * 256, :]
        out.append(np.ascontiguousarray(sl.reshape(8 * 256, D).T))
    return out


def _from_l2_layout(cores):
    F = np.zeros((NCORES, T, D), dtype=np.float32)
    for c in range(NCORES):
        sl = np.asarray(cores[c]).T.reshape(8, 256, D)
        F[:, c * 256:(c + 1) * 256, :] = sl
    return [np.ascontiguousarray(F[b].T) for b in range(NCORES)]



FUSE_SPECS = [
    ("l0a_", declare_l0_attn, emit_l0_attn, "_weights_l0_attn"),
    ("l0f_", declare_l0_ffn, emit_l0_ffn, "_weights_l0_ffn"),
    ("l1a_", declare_l1_attn, emit_l1_attn, "_weights_l1_attn"),
    ("l1f_", declare_l1_ffn, emit_l1_ffn, "_weights_l1_ffn"),
    ("l2a_", declare_l2_attn, emit_l2_attn, "_weights_l2_attn"),
    ("l2f_", declare_l2_ffn, emit_l2_ffn, "_weights_l2_ffn"),
    ("l3a_", declare_l3_attn, emit_l3_attn, "_weights_l3_attn"),
    ("l3f_", declare_l3_ffn, emit_l3_ffn, "_weights_l3_ffn"),
]


def build_fused(nb=8):
    nc = bass.Bass("TRN2", target_bir_lowering=False)
    xin0 = nc.dram_tensor("xin", [D, T], F32, kind="ExternalInput").ap()
    bnd = []
    for i in range(nb):
        last = (i == nb - 1)
        kind = "ExternalOutput" if last else "Internal"
        name = "xout" if last else f"hb{i}"
        shape = [8 * D, 256] if i in (3, 5) else [D, T]
        bnd.append(nc.dram_tensor(name, shape, F32, kind=kind).ap())
    a2a1_out = (nc.dram_tensor("a2a1_out", [8 * D, 256], F32, kind="Internal").ap()
                if nb > 4 else None)
    a2a2_out = (nc.dram_tensor("a2a2_out", [8 * D, 256], F32, kind="Internal").ap()
                if nb > 6 else None)
    with contextlib.ExitStack() as stack:
        ctx = Ctx(nc)
        ctx.make_sems(stack)
        cc1 = stack.enter_context(nc.semaphore("cc1"))
        cc2 = stack.enter_context(nc.semaphore("cc2"))
        tiny = ctx.sbuf(stack, "cc_tiny", [1, 8], F32)
        with nc.Block() as block:
            for i in range(nb):
                p, dec, emit, _w = FUSE_SPECS[i]
                aps = dec(nc, p)
                if i == 0:
                    ld = plain_ld(xin0)
                elif i == 4:
                    ld = a2a_ld(a2a1_out, "a2a1_out")
                elif i == 6:
                    ld = a2a_ld(a2a2_out, "a2a2_out")
                else:
                    ld = plain_ld(bnd[i - 1])
                mk_st = a2a_st if i in (3, 5) else plain_st
                st = mk_st(bnd[i], f"st{i}")
                ctx.prefix = p
                with contextlib.ExitStack() as bs:
                    emit(nc, ctx, bs, aps, ld, st)
                ctx.prefix = ""
                ctx.fence()
                if i == 3 and nb > 4:
                    emit_a2a(ctx, cc1, bnd[3], a2a1_out, "st3", "a2a1_out", tiny)
                if i == 5 and nb > 6:
                    emit_a2a(ctx, cc2, bnd[5], a2a2_out, "st5", "a2a2_out", tiny)
            ctx.finish()
            ctx.build(block)
    return nc


def fused_in_maps(inputs, x):
    base = {}
    for p, _dec, _emit, wname in FUSE_SPECS:
        for k, v in globals()[wname](inputs).items():
            base[p + k] = v
    return [dict(base, xin=np.ascontiguousarray(x[c].T)) for c in range(NCORES)]


def kernel(**inputs):
    x = np.asarray(inputs["L0_attn_x"], dtype=np.float32)
    nc = _get("fused", build_fused)
    in_maps = fused_in_maps(inputs, x)
    res = run_bass_kernel_spmd(nc, in_maps, core_ids=list(range(NCORES)))
    return np.stack([np.ascontiguousarray(np.asarray(r["xout"]).T)
                     for r in res.results], axis=0)


# version 04
# speedup vs baseline: 1.4498x; 1.1501x over previous
import math
import contextlib
import numpy as np
import ml_dtypes
import concourse.bass as bass
import concourse.mybir as mybir
from concourse.bass_utils import run_bass_kernel_spmd

F32 = mybir.dt.float32
BF16 = mybir.dt.bfloat16
AF = mybir.ActivationFunctionType
ALU = mybir.AluOpType
AX = mybir.AxisListType

NCORES = 8
D = 1024
T = 2048
NC = D // 128
NT = T // 512

ENGINES = ("t", "v", "s", "p")
QUEUES = ("qs", "qs1", "qs2", "qs3", "qo", "qw0", "qw1", "qw2", "qw3",
          "qm", "qa")
Q_ISSUER = {"qs": "sync", "qs1": "sync", "qs2": "sync", "qs3": "sync",
            "qo": "sync", "qw0": "p", "qw1": "p", "qw2": "p", "qw3": "p",
            "qm": "sync", "qa": "s"}


def ceil_div(a, b):
    return -(-a // b)


class TB:

    def __init__(self, h, key):
        self.h = h
        self.key = key

    def __getitem__(self, idx):
        return self.h[idx]

    @property
    def ap(self):
        return self.h[:]


def R(*bufs):
    return [b.key for b in bufs]


class Ctx:
    def __init__(self, nc):
        self.nc = nc
        self.ops = {"t": [], "v": [], "s": [], "p": [], "sync": []}
        self.cnt = {e: 0 for e in ENGINES}
        self.cnt.update({q: 0 for q in QUEUES})
        self.sems = {}
        self.last_write = {}
        self.readers = {}
        self.pending = {e: [] for e in ENGINES}
        self.waited = {}
        self._names = set()
        self.prefix = ""

    def make_sems(self, stack):
        for name in ENGINES + QUEUES:
            self.sems[name] = stack.enter_context(self.nc.semaphore("sem_" + name))

    def sbuf(self, stack, name, shape, dtype):
        name = self.prefix + name
        assert name not in self._names
        self._names.add(name)
        h = stack.enter_context(self.nc.sbuf_tensor(name, shape, dtype))
        return TB(h, name)

    def sbuf_at(self, name, shape, dtype, at, delta=0):
        name = self.prefix + name
        assert name not in self._names
        self._names.add(name)
        addr = self.nc.lookup_mloc(at.h if isinstance(at, TB) else at).addr + delta
        h = self.nc.alloc_sbuf_tensor_at(name, shape, dtype, offset=addr)
        return TB(h, name)

    def psum(self, stack, name, shape, dtype=F32):
        name = self.prefix + name
        assert name not in self._names
        self._names.add(name)
        h = stack.enter_context(self.nc.psum_tensor(name, shape, dtype))
        return TB(h, name)

    _fence = None

    def _need_waits(self, stream, reads, writes):
        waits = {}

        def _tgt(src, c):
            return c

        def _add(src, c):
            if src == stream == "t":
                return
            if c is None:
                assert src == stream == "t", \
                    f"pending-count dependency from {src} seen by {stream}"
                return
            waits[src] = max(waits.get(src, 0), _tgt(src, c))

        for k in reads:
            lw = self.last_write.get(k)
            if lw is not None:
                _add(*lw)
        for k in writes:
            lw = self.last_write.get(k)
            if lw is not None:
                _add(*lw)
            for src, c in self.readers.get(k, {}).items():
                _add(src, c)
        if self._fence is not None:
            done = self._fence_done.setdefault(stream, False)
            if not done:
                for src, val in self._fence.items():
                    if src == stream == "t":
                        pass
                    waits[src] = max(waits.get(src, 0), val)
                self._fence_done[stream] = True
        out = []
        for src, val in waits.items():
            seen = self.waited.get((stream, src), 0)
            if val > seen:
                self.waited[(stream, src)] = val
                out.append((self.sems[src], val))
        return out

    def op(self, e, build, reads=(), writes=(), inc=True):
        assert e in ENGINES
        reads = [b.key if isinstance(b, TB) else b for b in reads]
        writes = [b.key if isinstance(b, TB) else b for b in writes]
        waits = self._need_waits(e, reads, writes)
        if inc:
            self.cnt[e] += 1
            cnt = self.cnt[e]
        else:
            cnt = None
        sem = self.sems[e]

        def run(eng, waits=waits, build=build, inc=inc, sem=sem):
            for s, v in waits:
                eng.wait_ge(s, v)
            ins = build(eng)
            if inc:
                ins.then_inc(sem, 1)

        self.ops[e].append(run)
        if cnt is None:
            for k in reads:
                self.pending[e].append(("r", k))
            for k in writes:
                self.pending[e].append(("w", k))
                self.last_write[k] = (e, None)
                self.readers[k] = {}
        else:
            for kind, k in self.pending[e]:
                if kind == "r":
                    self.readers.setdefault(k, {})[e] = cnt
                else:
                    if self.last_write.get(k, (None, 0))[0] == e and \
                       self.last_write[k][1] is None:
                        self.last_write[k] = (e, cnt)
            self.pending[e] = []
            for k in reads:
                self.readers.setdefault(k, {})[e] = cnt
            for k in writes:
                self.last_write[k] = (e, cnt)
                self.readers[k] = {}
        return cnt

    def dma(self, q, out, in_, reads=(), writes=(), **kw):
        assert q in QUEUES
        reads = [b.key if isinstance(b, TB) else b for b in reads]
        writes = [b.key if isinstance(b, TB) else b for b in writes]
        issuer = Q_ISSUER[q]
        waits = self._need_waits(issuer, reads, writes)
        prev = self.cnt[q]
        self.cnt[q] += 16
        cnt = self.cnt[q]
        sem = self.sems[q]

        def run(eng, waits=waits, out=out, in_=in_, kw=kw, sem=sem, prev=prev):
            if prev > 0:
                eng.wait_ge(sem, prev)
            for s, v in waits:
                eng.wait_ge(s, v)
            eng.dma_start(out=out, in_=in_, **kw).then_inc(sem, 16)

        self.ops[issuer].append(run)
        for k in reads:
            self.readers.setdefault(k, {})[q] = cnt
        for k in writes:
            self.last_write[k] = (q, cnt)
            self.readers[k] = {}
        return cnt

    def fence(self):
        self._fence = {s: self.cnt[s] for s in ENGINES + QUEUES if self.cnt[s] > 0}
        self._fence_done = {}

    def finish(self):
        targets = []
        for src in ENGINES + QUEUES:
            v = self.cnt[src]
            if v > 0:
                targets.append((self.sems[src], v))

        def run(eng, targets=targets):
            for s, v in targets:
                eng.wait_ge(s, v)

        self.ops["sync"].append(run)

    def build(self, block):
        for name, attr in (("t", "tensor"), ("v", "vector"), ("s", "scalar"),
                           ("p", "gpsimd"), ("sync", "sync")):
            ops = self.ops[name]
            if not ops:
                continue
            getattr(block, attr)(lambda eng, ops=ops: [f(eng) for f in ops])

    def mm(self, psum_ap, psum_buf, parts, accumulate=False, close=True):
        n = len(parts)
        for i, (lhsT, rhs, rks) in enumerate(parts):
            start = (i == 0) and not accumulate
            stop = (i == n - 1) and close
            last = (i == n - 1)
            self.op(
                "t",
                lambda eng, lhsT=lhsT, rhs=rhs, start=start, stop=stop:
                    eng.matmul(psum_ap, lhsT, rhs, start=start, stop=stop),
                reads=rks,
                writes=[psum_buf] if (stop or i == 0) else [],
                inc=last,
            )



class StdBuf:
    def __init__(self, ctx, stack, Tl=T, n_tmp=3, n_wslab=2, wslab_cols=1024,
                 psum_f32=6, psum_bf16=1):
        self.T = Tl
        self.x = [ctx.sbuf(stack, f"x{c}", [128, Tl], F32) for c in range(NC)]
        self.xb = [ctx.sbuf(stack, f"xb{c}", [128, Tl], BF16) for c in range(NC)]
        self.tmp = [ctx.sbuf(stack, f"tmp{i}", [128, Tl], F32) for i in range(n_tmp)]
        self.mu = ctx.sbuf(stack, "mu", [128, Tl], F32)
        self.rstd = ctx.sbuf(stack, "rstd", [128, Tl], F32)
        self.rows = ctx.sbuf(stack, "rows", [1, Tl], F32)
        self.rows2 = ctx.sbuf(stack, "rows2", [1, Tl], F32)
        self.ones_b = ctx.sbuf(stack, "ones_b", [128, 1], BF16)
        self.ones_f = ctx.sbuf(stack, "ones_f", [128, 1], F32)
        self.onesrow = ctx.sbuf(stack, "onesrow", [1, 128], F32)
        self.vecs = ctx.sbuf(stack, "vecs", [128, 128], F32)
        self.nvec = 0
        self.consts = ctx.sbuf(stack, "consts", [128, 8], F32)
        self._cmap = {}
        self._ctx = ctx
        self.wslab = [ctx.sbuf(stack, f"wslab{i}", [128, wslab_cols], BF16)
                      for i in range(n_wslab)]
        self.ps = [ctx.psum(stack, f"ps{i}", [128, 512], F32) for i in range(psum_f32)]
        self.pb = [ctx.psum(stack, f"pb{i}", [128, 1024], BF16) for i in range(psum_bf16)]

    def vcol(self, col):
        return self.vecs[:, col:col + 1]

    def const_ap(self, value):
        value = float(value)
        if value not in self._cmap:
            col = len(self._cmap)
            assert col < 8
            self._cmap[value] = col
            self._ctx.op("v", lambda eng, col=col, value=value: eng.memset(
                self.consts[:, col:col + 1], value), writes=[self.consts])
        return self.consts[:, self._cmap[value]:self._cmap[value] + 1]

    def valloc(self, n):
        c0 = self.nvec
        self.nvec += n
        assert self.nvec <= 160
        return c0

    def vkey(self, col):
        return self.vecs.key


def emit_const_init(ctx, sb):
    ctx.op("v", lambda eng: eng.memset(sb.ones_b.ap, 1.0), writes=[sb.ones_b])
    ctx.op("v", lambda eng: eng.memset(sb.ones_f.ap, 1.0), writes=[sb.ones_f])
    ctx.op("v", lambda eng: eng.memset(sb.onesrow.ap, 1.0), writes=[sb.onesrow])


def load_vec(ctx, sb, dram_ap, length=D):
    nch = length // 128
    col0 = sb.valloc(nch)
    for j in range(nch):
        ctx.dma("qm", sb.vecs[:, col0 + j:col0 + j + 1],
                dram_ap[j * 128:(j + 1) * 128, None], writes=[sb.vecs])
    return col0


QS_RR = ("qs", "qs1", "qs2", "qs3")


def load_stream(ctx, sb, x_dram, dst=None):
    dst = dst or sb.x
    for c in range(NC):
        ctx.dma(QS_RR[c % 4], dst[c].ap, x_dram[c * 128:(c + 1) * 128, :],
                writes=[dst[c]])


def store_stream(ctx, sb, out_dram, src=None):
    src = src or sb.x
    for c in range(NC):
        ctx.dma("qo", out_dram[c * 128:(c + 1) * 128, :], src[c].ap, reads=[src[c]])


def plain_ld(xin, rd_key=None):
    rks = [rd_key] if rd_key else []

    def f(ctx, dst, c, col0=0, ncols=T, dcol0=None, q=None):
        d0 = col0 if dcol0 is None else dcol0
        ctx.dma(q or QS_RR[c % 4], dst.h[:, d0:d0 + ncols],
                xin[c * 128:(c + 1) * 128, col0:col0 + ncols],
                reads=rks, writes=[dst])
    return f


def plain_st(xout, wr_key=None):
    wks = [wr_key] if wr_key else []

    def f(ctx, src, c, col0=0, ncols=T, scol0=None, q="qo"):
        s0 = col0 if scol0 is None else scol0
        ctx.dma(q, xout[c * 128:(c + 1) * 128, col0:col0 + ncols],
                src.h[:, s0:s0 + ncols], reads=[src], writes=wks)
    return f


def a2a_ld(buf, rd_key):
    def f(ctx, dst, c, col0=0, ncols=T, dcol0=None, q=None):
        d0 = col0 if dcol0 is None else dcol0
        assert col0 % 256 == 0 and ncols % 256 == 0
        for k in range(ncols // 256):
            s = col0 // 256 + k
            ctx.dma(q or QS_RR[c % 4], dst.h[:, d0 + k * 256:d0 + (k + 1) * 256],
                    buf[s * 1024 + c * 128:s * 1024 + (c + 1) * 128, :],
                    reads=[rd_key], writes=[dst])
    return f


def a2a_st(buf, wr_key):
    def f(ctx, src, c, col0=0, ncols=T, scol0=None, q="qo"):
        s0 = col0 if scol0 is None else scol0
        assert col0 % 256 == 0 and ncols % 256 == 0
        for k in range(ncols // 256):
            j = col0 // 256 + k
            ctx.dma(q, buf[j * 1024 + c * 128:j * 1024 + (c + 1) * 128, :],
                    src.h[:, s0 + k * 256:s0 + (k + 1) * 256],
                    reads=[src], writes=[wr_key])
    return f


def emit_a2a(ctx, cc_sem, in_ap, out_ap, in_key, out_key, tiny):
    from concourse.aot_env import allow_ici_all_to_all

    def coll(eng):
        with allow_ici_all_to_all():
            return eng.collective_compute(
                "AllToAll", ALU.bypass, replica_groups=[list(range(NCORES))],
                ins=[in_ap.opt()], outs=[out_ap.opt()]).then_inc(cc_sem)
    ctx.op("p", coll, reads=[in_key], inc=False)

    def marker(eng):
        eng.wait_ge(cc_sem, 1)
        return eng.memset(tiny.ap, 0.0)
    ctx.op("p", marker, writes=[out_key, tiny], inc=True)


def cast_stream(ctx, sb, src_list, dst_list, shift_chunks=()):
    for c in range(NC):
        src, dst = src_list[c], dst_list[c]
        if c in shift_chunks:
            w = dst.h.shape[1]
            ctx.op("v", lambda eng, s=src, d=dst, w=w: eng.tensor_copy(
                d[:, 1:w], s[:, 0:w - 1]),
                reads=[src], writes=[dst])
            ctx.op("v", lambda eng, d=dst: eng.memset(d[:, 0:1], 0.0), writes=[dst])
        else:
            ctx.op("v", lambda eng, s=src, d=dst: eng.tensor_copy(d.ap, s.ap),
                   reads=[src], writes=[dst])


def stream_w_slab(ctx, sb, w_dram, buf_idx, col0, ncols, kchunks=NC, krow0=0):
    slab = sb.wslab[buf_idx]
    view = w_dram[krow0:krow0 + kchunks * 128,
                  col0:col0 + ncols].rearrange("(k p) c -> p k c", p=128)
    ctx.dma("qw%d" % (buf_idx % 4), slab[:, 0:kchunks * ncols], view, writes=[slab])
    return slab


def proj_fm(ctx, sb, w_dram, dout, xbufs, out_cb, kchunks=NC, pslots=(0, 1, 2, 3),
            Tl=None, wbufs=(0, 1)):
    Tl = Tl or sb.T
    ntt = Tl // 512
    noc = dout // 128
    for oc in range(noc):
        slab = stream_w_slab(ctx, sb, w_dram, wbufs[oc % len(wbufs)], oc * 128, 128,
                             kchunks)
        for tt in range(ntt):
            ps = sb.ps[pslots[(oc * ntt + tt) % len(pslots)]]
            parts = [(slab[:, k * 128:(k + 1) * 128],
                      xbufs[k][:, tt * 512:(tt + 1) * 512],
                      [slab, xbufs[k]]) for k in range(kchunks)]
            ctx.mm(ps.ap, ps, parts)
            out_cb(oc, tt, ps)


def emit_ln_fm(ctx, sb, src_list, dst_list, eps, gamma_col=None, beta_col=None,
               src_bf16=False, Tl=None, nch=NC, dim=D, src_off=0, dst_off=0):
    Tl = Tl or sb.T
    ntt = Tl // 512
    sq = sb.tmp[-1]
    ones = sb.ones_b if src_bf16 else sb.ones_f
    stat = sb.ps[4]
    bc = sb.ps[5]
    for tt in range(ntt):
        sl = slice(src_off + tt * 512, src_off + (tt + 1) * 512)
        lo = slice(tt * 512, (tt + 1) * 512)
        parts = [(ones.ap, src_list[c][:, sl], [ones, src_list[c]])
                 for c in range(nch)]
        ctx.mm(stat[0:1, :], stat, parts)
        for c in range(nch):
            ctx.op("s", lambda eng, c=c, sl=sl: eng.activation(
                sq[:, 0:512], src_list[c][:, sl], AF.Square),
                reads=[src_list[c]], writes=[sq])
            ctx.mm(stat[32:33, :], stat,
                   [(sb.ones_f.ap, sq[:, 0:512], [sb.ones_f, sq])],
                   accumulate=(c > 0), close=(c == nch - 1))
        ctx.op("v", lambda eng, lo=lo: eng.tensor_copy(sb.rows[0:1, lo], stat[0:1, :]),
               reads=[stat], writes=[sb.rows])
        ctx.op("v", lambda eng, lo=lo: eng.tensor_copy(sb.rows2[0:1, lo], stat[32:33, :]),
               reads=[stat], writes=[sb.rows2])
        ctx.mm(bc.ap, bc, [(sb.onesrow.ap, sb.rows[0:1, lo], [sb.onesrow, sb.rows])])
        ctx.op("v", lambda eng, lo=lo: eng.tensor_scalar(
            sb.mu[:, lo], bc.ap, 1.0 / dim, None, ALU.mult),
            reads=[bc], writes=[sb.mu])
        ctx.mm(bc.ap, bc, [(sb.onesrow.ap, sb.rows2[0:1, lo], [sb.onesrow, sb.rows2])])
        ctx.op("v", lambda eng, lo=lo: eng.tensor_scalar(
            sb.rstd[:, lo], bc.ap, 1.0 / dim, None, ALU.mult),
            reads=[bc], writes=[sb.rstd])
    ctx.op("v", lambda eng: eng.tensor_mul(sq[:, 0:Tl], sb.mu[:, 0:Tl], sb.mu[:, 0:Tl]),
           reads=[sb.mu], writes=[sq])
    ctx.op("v", lambda eng: eng.tensor_sub(sb.rstd[:, 0:Tl], sb.rstd[:, 0:Tl], sq[:, 0:Tl]),
           reads=[sb.rstd, sq], writes=[sb.rstd])
    epsap = sb.const_ap(eps)
    ctx.op("s", lambda eng: eng.activation(sb.rstd[:, 0:Tl], sb.rstd[:, 0:Tl], AF.Ln,
                                           bias=epsap),
           reads=[sb.rstd, sb.consts], writes=[sb.rstd])
    ctx.op("s", lambda eng: eng.activation(sb.rstd[:, 0:Tl], sb.rstd[:, 0:Tl], AF.Exp,
                                           scale=-0.5),
           reads=[sb.rstd], writes=[sb.rstd])
    for c in range(nch):
        src, dst = src_list[c], dst_list[c]
        ssl = slice(src_off, src_off + Tl)
        dsl = slice(dst_off, dst_off + Tl)
        ctx.op("v", lambda eng, src=src, ssl=ssl: eng.tensor_sub(
            sq[:, 0:Tl], src[:, ssl], sb.mu[:, 0:Tl]),
            reads=[src, sb.mu], writes=[sq])
        if gamma_col is not None:
            ctx.op("v", lambda eng: eng.tensor_mul(sq[:, 0:Tl], sq[:, 0:Tl],
                                                   sb.rstd[:, 0:Tl]),
                   reads=[sq, sb.rstd], writes=[sq])
            ctx.op("v", lambda eng, dst=dst, c=c, dsl=dsl: eng.tensor_scalar(
                dst[:, dsl], sq[:, 0:Tl], sb.vcol(gamma_col + c),
                sb.vcol(beta_col + c), ALU.mult, ALU.add),
                reads=[sq, sb.vecs], writes=[dst])
        else:
            ctx.op("v", lambda eng, dst=dst, dsl=dsl: eng.tensor_mul(
                dst[:, dsl], sq[:, 0:Tl], sb.rstd[:, 0:Tl]),
                reads=[sq, sb.rstd], writes=[dst])



L0A_VECS = ["bg", "bi", "A", "Bp", "Cp", "Dp", "gamma", "beta",
            "bfg", "bfu", "bfd", "bo"]


def declare_l0_attn(nc, p=""):
    aps = {}
    for name in ["Wg", "Wi", "Wfg", "Wfu", "Wfd", "Wo"]:
        aps[name] = nc.dram_tensor(p + name, [D, D], F32, kind="ExternalInput").ap()
    for name in L0A_VECS:
        aps[name] = nc.dram_tensor(p + name, [D], F32, kind="ExternalInput").ap()
    return aps


def emit_l0_attn(nc, ctx, stack, aps, xld, xst):
    wd = aps
    vin = {n: aps[n] for n in L0A_VECS}
    sb = StdBuf(ctx, stack)
    y = [ctx.sbuf(stack, f"y{c}", [128, T], BF16) for c in range(NC)]
    if True:
            emit_const_init(ctx, sb)
            cols = {n: load_vec(ctx, sb, ap) for n, ap in vin.items()}
            for c in range(NC):
                xld(ctx, sb.x[c], c, q=QS_RR[c % 4])
            cast_stream(ctx, sb, sb.x, sb.xb)
            cb0 = sb.valloc(NC)
            for c in range(NC):
                ctx.op("v", lambda eng, c=c: eng.tensor_mul(
                    sb.vcol(cb0 + c), sb.vcol(cols["Cp"] + c), sb.vcol(cols["Bp"] + c)),
                    reads=[sb.vecs], writes=[sb.vecs])

            gate, xi, tscr = sb.tmp[0], sb.tmp[1], sb.tmp[2]

            def dual_proj(srcbufs, wa, ba, wb, bb, fa=AF.Silu, fb=AF.Identity):
                for oc in range(NC):
                    for (wkey, dst, bcol, func, wbuf, psl) in (
                            (wa, gate, ba, fa, 0, (0, 1)),
                            (wb, xi, bb, fb, 1, (2, 3))):
                        slab = stream_w_slab(ctx, sb, wd[wkey], wbuf, oc * 128, 128)
                        for tt in range(NT):
                            ps = sb.ps[psl[tt % 2]]
                            parts = [(slab[:, k * 128:(k + 1) * 128],
                                      srcbufs[k][:, tt * 512:(tt + 1) * 512],
                                      [slab, srcbufs[k]]) for k in range(NC)]
                            ctx.mm(ps.ap, ps, parts)
                            ctx.op("s", lambda eng, tt=tt, ps=ps, dst=dst, bcol=bcol,
                                   func=func, oc=oc: eng.activation(
                                       dst[:, tt * 512:(tt + 1) * 512], ps.ap, func,
                                       bias=sb.vcol(bcol + oc)),
                                   reads=[ps, sb.vecs], writes=[dst])
                    yield oc

            for oc in dual_proj(sb.xb, "Wg", cols["bg"], "Wi", cols["bi"]):
                ctx.op("v", lambda eng: eng.tensor_mul(xi.ap, gate.ap, xi.ap),
                       reads=[gate, xi], writes=[xi])
                ctx.op("v", lambda eng, oc=oc: eng.tensor_tensor_scan(
                    gate.ap, sb.vcol(cols["A"] + oc).broadcast_to((128, T)), xi.ap,
                    0.0, ALU.mult, ALU.add),
                    reads=[xi, sb.vecs], writes=[gate])
                ctx.op("v", lambda eng, oc=oc: eng.tensor_scalar(
                    tscr.ap, xi.ap, sb.vcol(cols["Dp"] + oc), None, ALU.mult),
                    reads=[xi, sb.vecs], writes=[tscr])
                ctx.op("v", lambda eng, oc=oc: eng.scalar_tensor_tensor(
                    y[oc].ap, gate.ap, sb.vcol(cb0 + oc), tscr.ap, ALU.mult, ALU.add),
                    reads=[gate, sb.vecs, tscr], writes=[y[oc]])

            emit_ln_fm(ctx, sb, y, sb.xb, 1e-3, gamma_col=cols["gamma"],
                       beta_col=cols["beta"], src_bf16=True)
            for oc in dual_proj(sb.xb, "Wfg", cols["bfg"], "Wfu", cols["bfu"]):
                ctx.op("v", lambda eng, oc=oc: eng.tensor_mul(y[oc].ap, gate.ap, xi.ap),
                       reads=[gate, xi], writes=[y[oc]])
            def evac_to_xb(oc, tt, ps):
                ctx.op("s", lambda eng, oc=oc, tt=tt, ps=ps: eng.activation(
                    sb.xb[oc][:, tt * 512:(tt + 1) * 512], ps.ap, AF.Identity,
                    bias=sb.vcol(cols["bfd"] + oc)),
                    reads=[ps, sb.vecs], writes=[sb.xb[oc]])
            proj_fm(ctx, sb, wd["Wfd"], D, y, evac_to_xb)
            def evac_res(oc, tt, ps):
                sl = slice(tt * 512, (tt + 1) * 512)
                ctx.op("v", lambda eng, oc=oc, sl=sl, ps=ps: eng.tensor_scalar(
                    tscr[:, sl], ps.ap, sb.vcol(cols["bo"] + oc), None, ALU.add),
                    reads=[ps, sb.vecs], writes=[tscr])
                ctx.op("v", lambda eng, oc=oc, sl=sl: eng.tensor_add(
                    sb.x[oc][:, sl], sb.x[oc][:, sl], tscr[:, sl]),
                    reads=[tscr, sb.x[oc]], writes=[sb.x[oc]])
            proj_fm(ctx, sb, wd["Wo"], D, sb.xb, evac_res)
            emit_ln_fm(ctx, sb, sb.x, sb.x, 1e-5)
            for c in range(NC):
                xst(ctx, sb.x[c], c)


def build_l0_attn():
    nc = bass.Bass("TRN2", target_bir_lowering=False)
    aps = declare_l0_attn(nc)
    xin = nc.dram_tensor("xin", [D, T], F32, kind="ExternalInput").ap()
    xout = nc.dram_tensor("xout", [D, T], F32, kind="ExternalOutput").ap()
    with contextlib.ExitStack() as stack:
        ctx = Ctx(nc)
        ctx.make_sems(stack)
        with nc.Block() as block:
            emit_l0_attn(nc, ctx, stack, aps, plain_ld(xin), plain_st(xout))
            ctx.finish()
            ctx.build(block)
    return nc




RND = float(1.5 * 2 ** 23)


def emit_quantize_weight(ctx, sb, w_dram, wq_dram, rows, cols, wf, racc, rtmp, qs_col):
    nch = rows // 128
    cw = min(cols, 2048)
    ncs = cols // cw
    assert cols % cw == 0
    ctx.op("v", lambda eng: eng.memset(racc.ap, 0.0), writes=[racc])
    for r in range(nch):
        for s in range(ncs):
            buf = wf[(r * ncs + s) % 2]
            ctx.dma("qw%d" % ((r * ncs + s) % 2), buf[:, 0:cw],
                    w_dram[r * 128:(r + 1) * 128, s * cw:(s + 1) * cw],
                    writes=[buf])
            ctx.op("v", lambda eng, buf=buf: eng.reduce_sum(
                rtmp.ap, buf[:, 0:cw], AX.X, apply_absolute_value=True),
                reads=[buf], writes=[rtmp])
            ctx.op("v", lambda eng: eng.tensor_add(racc.ap, racc.ap, rtmp.ap),
                   reads=[rtmp, racc], writes=[racc])
    stat = sb.ps[4]
    ctx.mm(stat[0:1, 0:1], stat, [(sb.ones_f.ap, racc.ap, [sb.ones_f, racc])])
    ctx.op("v", lambda eng: eng.tensor_scalar(
        sb.rows[0:1, 0:1], stat[0:1, 0:1], 1.0 / (rows * cols), 1e-5, ALU.mult, ALU.add),
        reads=[stat], writes=[sb.rows])
    ctx.op("v", lambda eng: eng.reciprocal(sb.rows[0:1, 1:2], sb.rows[0:1, 0:1]),
           reads=[sb.rows], writes=[sb.rows])
    bc = sb.ps[5]
    ctx.mm(bc[:, 0:1], bc, [(sb.onesrow.ap, sb.rows[0:1, 1:2], [sb.onesrow, sb.rows])])
    ctx.op("v", lambda eng: eng.tensor_copy(sb.vcol(qs_col), bc[:, 0:1]),
           reads=[bc], writes=[sb.vecs])
    qb = sb.wslab[0]
    for r in range(nch):
        for s in range(ncs):
            buf = wf[(r * ncs + s) % 2]
            ctx.dma("qw%d" % ((r * ncs + s) % 2), buf[:, 0:cw],
                    w_dram[r * 128:(r + 1) * 128, s * cw:(s + 1) * cw],
                    writes=[buf])
            ctx.op("v", lambda eng, buf=buf: eng.tensor_scalar(
                buf[:, 0:cw], buf[:, 0:cw], sb.vcol(qs_col), RND, ALU.mult, ALU.add),
                reads=[buf, sb.vecs], writes=[buf])
            ctx.op("v", lambda eng, buf=buf: eng.tensor_scalar(
                buf[:, 0:cw], buf[:, 0:cw], RND, -1.0, ALU.subtract, ALU.max),
                reads=[buf], writes=[buf])
            ctx.op("v", lambda eng, buf=buf: eng.tensor_scalar(
                qb[:, 0:cw], buf[:, 0:cw], 1.0, None, ALU.min),
                reads=[buf], writes=[qb])
            ctx.dma("qo", wq_dram[r * 128:(r + 1) * 128, s * cw:(s + 1) * cw],
                    qb[:, 0:cw], reads=[qb], writes=["wq_dram"])


def declare_l0_ffn(nc, p=""):
    FF = 4096
    return {
        "W1T": nc.dram_tensor(p + "W1T", [D, FF], F32, kind="ExternalInput").ap(),
        "W2T": nc.dram_tensor(p + "W2T", [FF, D], F32, kind="ExternalInput").ap(),
        "b1": nc.dram_tensor(p + "b1", [FF], F32, kind="ExternalInput").ap(),
        "b2": nc.dram_tensor(p + "b2", [D], F32, kind="ExternalInput").ap(),
        "w1q": nc.dram_tensor(p + "w1q", [D, FF], BF16, kind="Internal").ap(),
        "w2q": nc.dram_tensor(p + "w2q", [FF, D], BF16, kind="Internal").ap(),
    }


def emit_l0_ffn(nc, ctx, stack, aps, xld, xst):
    FF = 4096
    W1T, W2T, b1, b2 = aps["W1T"], aps["W2T"], aps["b1"], aps["b2"]
    w1q, w2q = aps["w1q"], aps["w2q"]
    if True:
        sb = StdBuf(ctx, stack, n_tmp=1, n_wslab=2, wslab_cols=2048, psum_bf16=0)
        wf = [ctx.sbuf(stack, f"wf{i}", [128, 2048], F32) for i in range(2)]
        racc = ctx.sbuf(stack, "racc", [128, 1], F32)
        rtmp = ctx.sbuf(stack, "rtmp", [128, 1], F32)
        h = [ctx.sbuf(stack, f"h{i}", [128, 512], BF16) for i in range(32)]
        xbt = sb.xb
        ps8 = [ctx.psum(stack, f"ps8_{i}", [128, 512], F32) for i in range(2)]
        if True:
            emit_const_init(ctx, sb)
            b1c = load_vec(ctx, sb, b1, FF)
            b2c = load_vec(ctx, sb, b2, D)
            qs1 = sb.valloc(1)
            qs2 = sb.valloc(1)
            for c in range(NC):
                xld(ctx, sb.x[c], c, q=QS_RR[c % 4])
            cast_stream(ctx, sb, sb.x, xbt)
            emit_quantize_weight(ctx, sb, W1T, w1q, D, FF, wf, racc, rtmp, qs1)
            emit_quantize_weight(ctx, sb, W2T, w2q, FF, D, wf, racc, rtmp, qs2)
            for tt in range(NT):
                sl = slice(tt * 512, (tt + 1) * 512)
                for fc in range(32):
                    slab = sb.wslab[fc % 2]
                    ctx.dma("qw%d" % (fc % 2), slab[:, 0:1024],
                            w1q[0:D, fc * 128:(fc + 1) * 128].rearrange(
                                "(k p) c -> p k c", p=128),
                            reads=["wq_dram"], writes=[slab])
                    ps = ps8[fc % 2]
                    parts = [(slab[:, k * 128:(k + 1) * 128], xbt[k][:, sl],
                              [slab, xbt[k]]) for k in range(NC)]
                    ctx.mm(ps.ap, ps, parts)
                    ctx.op("s", lambda eng, ps=ps, fc=fc: eng.activation(
                        h[fc].ap, ps.ap, AF.Relu, bias=sb.vcol(b1c + fc)),
                        reads=[ps, sb.vecs], writes=[h[fc]])
                pslots = sb.ps[0:4] + ps8 + sb.ps[4:6]
                for fc in range(32):
                    slab = sb.wslab[fc % 2]
                    ctx.dma("qw%d" % (fc % 2), slab[:, 0:1024],
                            w2q[fc * 128:(fc + 1) * 128, 0:1024],
                            reads=["wq_dram"], writes=[slab])
                    for dc in range(NC):
                        ctx.mm(pslots[dc].ap, pslots[dc],
                               [(slab[:, dc * 128:(dc + 1) * 128], h[fc].ap,
                                 [slab, h[fc]])], accumulate=(fc > 0),
                               close=(fc == 31))
                for dc in range(NC):
                    ctx.op("v", lambda eng, dc=dc, sl=sl: eng.scalar_tensor_tensor(
                        sb.x[dc][:, sl], pslots[dc].ap, sb.vcol(b2c + dc),
                        sb.x[dc][:, sl], ALU.add, ALU.add),
                        reads=[pslots[dc], sb.vecs, sb.x[dc]], writes=[sb.x[dc]])
            emit_ln_fm(ctx, sb, sb.x, sb.x, 1e-5)
            for c in range(NC):
                xst(ctx, sb.x[c], c)


def build_l0_ffn():
    nc = bass.Bass("TRN2", target_bir_lowering=False)
    aps = declare_l0_ffn(nc)
    xin = nc.dram_tensor("xin", [D, T], F32, kind="ExternalInput").ap()
    xout = nc.dram_tensor("xout", [D, T], F32, kind="ExternalOutput").ap()
    with contextlib.ExitStack() as stack:
        ctx = Ctx(nc)
        ctx.make_sems(stack)
        with nc.Block() as block:
            emit_l0_ffn(nc, ctx, stack, aps, plain_ld(xin), plain_st(xout))
            ctx.finish()
            ctx.build(block)
    return nc


def run_l0_ffn(x_fm_cores, ins):
    return _run_block("l0_ffn", build_l0_ffn, x_fm_cores, ins)



class NS:
    pass


def _mk_lnbuf(ctx, stack, Tl=T, n_tmp=2):
    sb = NS()
    sb.T = Tl
    sb.tmp = [ctx.sbuf(stack, f"tmp{i}", [128, Tl], F32) for i in range(n_tmp)]
    sb.mu = ctx.sbuf(stack, "mu", [128, Tl], F32)
    sb.rstd = ctx.sbuf(stack, "rstd", [128, Tl], F32)
    sb.rows = TB(sb.mu.h, "mu")
    sb.rows2 = TB(sb.rstd.h, "rstd")
    sb.ones_b = ctx.sbuf(stack, "ones_b", [128, 1], BF16)
    sb.ones_f = ctx.sbuf(stack, "ones_f", [128, 1], F32)
    sb.onesrow = ctx.sbuf(stack, "onesrow", [1, 128], F32)
    sb.vecs = ctx.sbuf(stack, "vecs", [128, 160], F32)
    sb.nvec = 0
    sb.consts = ctx.sbuf(stack, "consts", [128, 8], F32)
    sb._cmap = {}
    sb._ctx = ctx
    sb.vcol = lambda col: sb.vecs[:, col:col + 1]
    import types
    sb.valloc = types.MethodType(StdBuf.valloc, sb)
    sb.const_ap = types.MethodType(StdBuf.const_ap, sb)
    return sb


SHUF_PAIRSWAP = [i ^ 1 for i in range(32)]


def declare_l1_attn(nc, p=""):
    aps = {}
    for name in ["WQ", "WK", "WV"]:
        aps[name] = nc.dram_tensor(p + name, [D, D], F32, kind="ExternalInput").ap()
    for name in ["csq", "ssq", "csk", "ssk"]:
        aps[name] = nc.dram_tensor(p + name, [D, T], F32, kind="ExternalInput").ap()
    aps["idin"] = nc.dram_tensor(p + "idin", [128, 128], F32,
                                 kind="ExternalInput").ap()
    aps["qx_d"] = nc.dram_tensor(p + "qx_d", [D, T], BF16, kind="Internal").ap()
    aps["kx_d"] = nc.dram_tensor(p + "kx_d", [D, T], BF16, kind="Internal").ap()
    aps["v_d"] = nc.dram_tensor(p + "v_d", [T, D], BF16, kind="Internal").ap()
    return aps


def emit_l1_attn(nc, ctx, stack, aps, xld, xst):
    WQ, WK, WV = aps["WQ"], aps["WK"], aps["WV"]
    csq, ssq, csk, ssk = aps["csq"], aps["ssq"], aps["csk"], aps["ssk"]
    idin = aps["idin"]
    qx_d, kx_d, v_d = aps["qx_d"], aps["kx_d"], aps["v_d"]
    if True:
        sb = _mk_lnbuf(ctx, stack)
        sb.ps = [ctx.psum(stack, f"ps{i}", [128, 512], F32) for i in range(8)]
        wslab = [ctx.sbuf(stack, f"wslab{i}", [128, 1024], BF16) for i in range(2)]
        sb.wslab = wslab
        ident = ctx.sbuf(stack, "ident", [128, 128], F32)
        xb = [ctx.sbuf(stack, f"xb{c}", [128, T], BF16) for c in range(NC)]
        xo = [ctx.sbuf(stack, f"xo{c}", [128, T], BF16) for c in range(NC)]

        if True:
            emit_const_init(ctx, sb)
            ctx.dma("qm", ident.ap, idin, writes=[ident])
            with contextlib.ExitStack() as sA:
                x = [ctx.sbuf(sA, f"x{c}", [128, T], F32) for c in range(NC)]
                for c in range(NC):
                    xld(ctx, x[c], c, q="qs")
                    ctx.op("v", lambda eng, c=c: eng.tensor_copy(xb[c].ap, x[c].ap),
                           reads=[x[c]], writes=[xb[c]])
            ctx.fence()
            with contextlib.ExitStack() as sB:
                qraw = ctx.sbuf(sB, "qraw", [128, T], BF16)
                shuf = ctx.sbuf(sB, "shuf", [128, T], BF16)
                qxb = ctx.sbuf(sB, "qxb", [128, T], BF16)
                cs_sl = ctx.sbuf(sB, "cs_sl", [128, T], F32)
                ss_sl = ctx.sbuf(sB, "ss_sl", [128, T], F32)
                vbuf = ctx.sbuf(sB, "vbuf", [128, D], BF16)
                for (W, tcs, tss, out_d) in ((WQ, csq, ssq, qx_d), (WK, csk, ssk, kx_d)):
                    for oc in range(NC):
                        slab = stream_w_slab(ctx, sb, W, oc % 2, oc * 128, 128)
                        for tt in range(NT):
                            ps = sb.ps[tt % 2]
                            parts = [(slab[:, k * 128:(k + 1) * 128],
                                      xb[k][:, tt * 512:(tt + 1) * 512],
                                      [slab, xb[k]]) for k in range(NC)]
                            ctx.mm(ps.ap, ps, parts)
                            ctx.op("s", lambda eng, tt=tt, ps=ps: eng.activation(
                                qraw[:, tt * 512:(tt + 1) * 512], ps.ap, AF.Identity),
                                reads=[ps], writes=[qraw])
                        ctx.dma("qs", cs_sl.ap, tcs[oc * 128:(oc + 1) * 128, :], writes=[cs_sl])
                        ctx.dma("qs", ss_sl.ap, tss[oc * 128:(oc + 1) * 128, :], writes=[ss_sl])
                        ctx.op("v", lambda eng: eng.stream_shuffle(shuf.ap, qraw.ap, SHUF_PAIRSWAP),
                               reads=[qraw], writes=[shuf])
                        ctx.op("v", lambda eng: eng.tensor_mul(sb.tmp[0].ap, qraw.ap, cs_sl.ap),
                               reads=[qraw, cs_sl], writes=[sb.tmp[0]])
                        ctx.op("v", lambda eng: eng.tensor_mul(sb.tmp[1].ap, shuf.ap, ss_sl.ap),
                               reads=[shuf, ss_sl], writes=[sb.tmp[1]])
                        ctx.op("v", lambda eng: eng.tensor_add(qxb.ap, sb.tmp[0].ap, sb.tmp[1].ap),
                               reads=[sb.tmp[0], sb.tmp[1]], writes=[qxb])
                        ctx.dma("qo", out_d[oc * 128:(oc + 1) * 128, :], qxb.ap, reads=[qxb])
                wv = [ctx.sbuf(sB, f"wv{c}", [128, D], BF16) for c in range(NC)]
                for c in range(NC):
                    ctx.dma("qw%d" % (c % 4), wv[c].ap,
                            WV[c * 128:(c + 1) * 128, :], writes=[wv[c]])
                for tt in range(T // 128):
                    for c in range(NC):
                        slab = wv[c]
                        for nh in range(2):
                            ctx.mm(sb.ps[2 + nh].ap, sb.ps[2 + nh],
                                   [(xb[c][:, tt * 128:(tt + 1) * 128],
                                     slab[:, nh * 512:(nh + 1) * 512], [xb[c], slab])],
                                   accumulate=(c > 0), close=(c == NC - 1))
                    for nh in range(2):
                        ctx.op("s", lambda eng, nh=nh: eng.activation(
                            vbuf[:, nh * 512:(nh + 1) * 512], sb.ps[2 + nh].ap, AF.Identity),
                            reads=[sb.ps[2 + nh]], writes=[vbuf])
                    ctx.dma("qo", v_d[tt * 128:(tt + 1) * 128, :], vbuf.ap, reads=[vbuf])
            ctx.fence()
            with contextlib.ExitStack() as sC:
                qxs = [ctx.sbuf(sC, f"qxs{c}", [128, 512], BF16) for c in range(NC)]
                kxs = [ctx.sbuf(sC, f"kxs{c}", [128, T], BF16) for c in range(NC)]
                p2 = ctx.sbuf(sC, "p2", [128, T], F32)
                p2t = ctx.sbuf(sC, "p2t", [128, 16 * 512], BF16)
                dtile = ctx.sbuf(sC, "dtile", [128, T], F32)
                nmax4 = ctx.sbuf(sC, "nmax4", [128, 4], F32)
                nmax = ctx.sbuf(sC, "nmax", [128, 1], F32)
                zpart = ctx.sbuf(sC, "zpart", [128, 4], F32)
                zr = ctx.sbuf(sC, "zr", [128, 1], F32)
                for c in range(NC):
                    ctx.dma("qs", kxs[c].ap, kx_d[c * 128:(c + 1) * 128, :], writes=[kxs[c]])

                for qb in range(4):
                    for c in range(NC):
                        ctx.dma("qs", qxs[c].ap,
                                qx_d[c * 128:(c + 1) * 128, qb * 512:(qb + 1) * 512],
                                writes=[qxs[c]])
                    for qt in range(4):
                        q0 = qb * 512 + qt * 128
                        for nt in range(4):
                            ps = sb.ps[nt]
                            parts = [(qxs[c][:, qt * 128:(qt + 1) * 128],
                                      kxs[c][:, nt * 512:(nt + 1) * 512],
                                      [qxs[c], kxs[c]]) for c in range(NC)]
                            ctx.mm(ps.ap, ps, parts)
                            ctx.op("v", lambda eng, nt=nt, ps=ps: eng.reduce_max(
                                nmax4[:, nt:nt + 1], ps.ap, AX.X, negate=True),
                                reads=[ps], writes=[nmax4])
                        ctx.op("v", lambda eng: eng.tensor_reduce(
                            nmax.ap, nmax4.ap, AX.X, ALU.min),
                            reads=[nmax4], writes=[nmax])
                        for nt in range(4):
                            ps = sb.ps[nt]
                            ctx.op("s", lambda eng, nt=nt, ps=ps: eng.activation(
                                p2[:, nt * 512:(nt + 1) * 512], ps.ap, AF.Exp,
                                bias=nmax[:, 0:1], accum_out=zpart[:, nt:nt + 1]),
                                reads=[ps, nmax], writes=[p2, zpart])
                        ctx.op("v", lambda eng: eng.reduce_sum(zr.ap, zpart.ap, AX.X),
                               reads=[zpart], writes=[zr])
                        ctx.op("v", lambda eng: eng.reciprocal(zr.ap, zr.ap),
                               reads=[zr], writes=[zr])
                        ctx.op("p", lambda eng, q0=q0: eng.iota(
                            dtile.ap, [[-1, T]], base=q0, channel_multiplier=1,
                            allow_small_or_imprecise_dtypes=True),
                            writes=[dtile])
                        ctx.op("s", lambda eng: eng.activation(
                            dtile.ap, dtile.ap, AF.Abs),
                            reads=[dtile], writes=[dtile])
                        ctx.op("s", lambda eng: eng.activation(
                            dtile.ap, dtile.ap, AF.Exp, scale=math.log(0.9)),
                            reads=[dtile], writes=[dtile])
                        ctx.op("v", lambda eng: eng.scalar_tensor_tensor(
                            p2.ap, p2.ap, zr[:, 0:1], dtile.ap, ALU.mult, ALU.mult),
                            reads=[p2, zr, dtile], writes=[p2])
                        for kc in range(16):
                            pt = sb.ps[4 + kc % 4]
                            ctx.op("t", lambda eng, kc=kc, pt=pt: eng.transpose(
                                pt[0:128, 0:128], p2[:, kc * 128:(kc + 1) * 128], ident.ap),
                                reads=[p2, ident], writes=[pt])
                            ctx.op("v", lambda eng, kc=kc, qt=qt, pt=pt: eng.tensor_copy(
                                p2t[:, kc * 512 + qt * 128: kc * 512 + (qt + 1) * 128],
                                pt[0:128, 0:128]),
                                reads=[pt], writes=[p2t])
                    for kc in range(16):
                        slab = wslab[kc % 2]
                        ctx.dma("qw%d" % (kc % 2), slab.ap,
                                v_d[kc * 128:(kc + 1) * 128, :], writes=[slab])
                        for dc in range(NC):
                            ctx.mm(sb.ps[dc].ap, sb.ps[dc],
                                   [(slab[:, dc * 128:(dc + 1) * 128],
                                     p2t[:, kc * 512:(kc + 1) * 512], [slab, p2t])],
                                   accumulate=(kc > 0), close=(kc == 15))
                    for dc in range(NC):
                        ctx.op("s", lambda eng, dc=dc, qb=qb: eng.activation(
                            xo[dc][:, qb * 512:(qb + 1) * 512], sb.ps[dc].ap, AF.Identity),
                            reads=[sb.ps[dc]], writes=[xo[dc]])
            ctx.fence()
            with contextlib.ExitStack() as sD:
                xd = [ctx.sbuf(sD, f"x2{c}", [128, T], F32) for c in range(NC)]
                sb.x = xd
                for c in range(NC):
                    xld(ctx, xd[c], c, q="qs")
                    ctx.op("v", lambda eng, a=xd[c], b=xo[c]: eng.tensor_add(a.ap, a.ap, b.ap),
                           reads=[xd[c], xo[c]], writes=[xd[c]])
                emit_ln_fm(ctx, sb, xd, xd, 1e-5)
                for c in range(NC):
                    xst(ctx, xd[c], c)


def build_l1_attn():
    nc = bass.Bass("TRN2", target_bir_lowering=False)
    aps = declare_l1_attn(nc)
    xin = nc.dram_tensor("xin", [D, T], F32, kind="ExternalInput").ap()
    xout = nc.dram_tensor("xout", [D, T], F32, kind="ExternalOutput").ap()
    with contextlib.ExitStack() as stack:
        ctx = Ctx(nc)
        ctx.make_sems(stack)
        with nc.Block() as block:
            emit_l1_attn(nc, ctx, stack, aps, plain_ld(xin), plain_st(xout))
            ctx.finish()
            ctx.build(block)
    return nc


def _xpos_tables():
    half = D // 2
    scale_vec = (np.arange(0, D, 2, dtype=np.float64) + 0.4 * D) / (1.4 * D)
    power = np.arange(T, dtype=np.float64)[:, None] / 512.0
    scale = scale_vec[None, :] ** power
    inv_freq = 1.0 / 10000.0 ** (np.arange(half, dtype=np.float64) / half)
    sinu = np.arange(T, dtype=np.float64)[:, None] * inv_freq[None, :]
    sin = np.repeat(np.sin(sinu), 2, axis=-1)
    cos = np.repeat(np.cos(sinu), 2, axis=-1)
    scale2 = np.repeat(scale, 2, axis=-1)
    sgn = np.where(np.arange(D) % 2 == 0, -1.0, 1.0)[None, :]
    csq = (cos * scale2).T
    ssq = (sin * scale2 * sgn).T
    csk = (cos / scale2).T
    ssk = (sin / scale2 * sgn).T
    return [np.ascontiguousarray(a, dtype=np.float32) for a in (csq, ssq, csk, ssk)]


def _weights_l1_attn(ins):
    csq, ssq, csk, ssk = _xpos_tables()
    return {
        "WQ": _f32c(ins["L1_attn_W_Q"]), "WK": _f32c(ins["L1_attn_W_K"]),
        "WV": _f32c(ins["L1_attn_W_V"]),
        "csq": csq, "ssq": ssq, "csk": csk, "ssk": ssk,
        "idin": np.eye(128, dtype=np.float32),
    }


def run_l1_attn(x_fm_cores, ins):
    return _run_block("l1_attn", build_l1_attn, x_fm_cores, ins)



def declare_l1_ffn(nc, p=""):
    H = 2048
    E = 8
    return {
        "rwT": nc.dram_tensor(p + "rwT", [D, E], F32, kind="ExternalInput").ap(),
        "ebias": nc.dram_tensor(p + "ebias", [E], F32, kind="ExternalInput").ap(),
        "w1": nc.dram_tensor(p + "w1", [E, D, H], F32, kind="ExternalInput").ap(),
        "w3": nc.dram_tensor(p + "w3", [E, D, H], F32, kind="ExternalInput").ap(),
        "w2": nc.dram_tensor(p + "w2", [E, H, D], F32, kind="ExternalInput").ap(),
        "sw1": nc.dram_tensor(p + "sw1", [D, H], F32, kind="ExternalInput").ap(),
        "sw3": nc.dram_tensor(p + "sw3", [D, H], F32, kind="ExternalInput").ap(),
        "sw2": nc.dram_tensor(p + "sw2", [H, D], F32, kind="ExternalInput").ap(),
        "idin": nc.dram_tensor(p + "idin", [128, 128], F32,
                               kind="ExternalInput").ap(),
    }


def emit_l1_ffn(nc, ctx, stack, aps, xld, xst):
    H = 2048
    E = 8
    rwT, ebias = aps["rwT"], aps["ebias"]
    w1, w3, w2 = aps["w1"], aps["w3"], aps["w2"]
    sw1, sw3, sw2 = aps["sw1"], aps["sw3"], aps["sw2"]
    idin = aps["idin"]
    if True:
        sb = _mk_lnbuf(ctx, stack, n_tmp=1)
        sb.ps = [ctx.psum(stack, f"ps{i}", [128, 512], F32) for i in range(8)]
        ident = ctx.sbuf(stack, "ident", [128, 128], F32)
        xb = [ctx.sbuf(stack, f"xb{c}", [128, T], BF16) for c in range(NC)]
        xo = [ctx.sbuf(stack, f"xo{c}", [128, T], F32) for c in range(NC)]
        w2res = [ctx.sbuf(stack, f"w2res{i}", [128, D], BF16) for i in range(16)]
        eo = [ctx.sbuf(stack, f"eo{i}", [128, 512], BF16) for i in range(16)]
        wsl = [ctx.sbuf(stack, f"wsl{i}", [128, 8 * 128], BF16) for i in range(4)]
        sb.wslab = wsl
        rwt = ctx.sbuf(stack, "rwt", [128, 64], BF16)
        ebrow = ctx.sbuf(stack, "ebrow", [1, 8], BF16)
        onesrow_b = ctx.sbuf(stack, "onesrow_b", [1, 128], BF16)
        lg = ctx.sbuf(stack, "lg", [128, 8], F32)
        l2 = ctx.sbuf(stack, "l2", [128, 8], F32)
        m1 = ctx.sbuf(stack, "m1", [128, 1], F32)
        m2 = ctx.sbuf(stack, "m2", [128, 1], F32)
        p1 = ctx.sbuf(stack, "p1", [128, 1], F32)
        p2s = ctx.sbuf(stack, "p2s", [128, 1], F32)
        mk1 = ctx.sbuf(stack, "mk1", [128, 8], F32)
        mk2 = ctx.sbuf(stack, "mk2", [128, 8], F32)
        cw = ctx.sbuf(stack, "cw", [128, 8], F32)
        cwT = ctx.sbuf(stack, "cwT", [8, T], F32)
        cwb = ctx.sbuf(stack, "cwb", [128, 512], F32)
        ga = ctx.sbuf(stack, "ga", [128, 512], BF16)
        ua = ctx.sbuf(stack, "ua", [128, 512], BF16)
        ohot = ctx.sbuf(stack, "ohot", [8, 8 * 128], F32)

        if True:
            emit_const_init(ctx, sb)
            ctx.op("v", lambda eng: eng.memset(onesrow_b.ap, 1.0), writes=[onesrow_b])
            ctx.op("p", lambda eng: eng.iota(ohot.ap, [[1, 8], [0, 128]],
                                             base=0, channel_multiplier=-1,
                                             allow_small_or_imprecise_dtypes=True),
                   writes=[ohot])
            ctx.op("v", lambda eng: eng.tensor_scalar(ohot.ap, ohot.ap, 0.0, None,
                                                      ALU.is_equal),
                   reads=[ohot], writes=[ohot])
            ctx.dma("qm", ident.ap, idin, writes=[ident])
            ctx.dma("qw0", ebrow.ap, ebias[None, :], writes=[ebrow])
            for c in range(NC):
                ctx.dma("qw1", rwt[:, c * 8:(c + 1) * 8],
                        rwT[c * 128:(c + 1) * 128, :], writes=[rwt])
            with contextlib.ExitStack() as sA:
                xtmp = [ctx.sbuf(sA, "xtmp0", [128, T], F32)]
                for c in range(NC):
                    b = xtmp[0]
                    xld(ctx, b, c, q=QS_RR[c % 2])
                    ctx.op("v", lambda eng, a=xb[c], b=b: eng.tensor_copy(a.ap, b.ap),
                           reads=[b], writes=[xb[c]])
            ctx.fence()
            for c in range(NC):
                ctx.op("v", lambda eng, a=xo[c]: eng.memset(a.ap, 0.0), writes=[xo[c]])
            for p in range(T // 128):
                ps = sb.ps[0]
                parts = [(xb[c][:, p * 128:(p + 1) * 128], rwt[:, c * 8:(c + 1) * 8],
                          [xb[c], rwt]) for c in range(NC)]
                parts.append((onesrow_b.ap, ebrow.ap, [onesrow_b, ebrow]))
                ctx.mm(ps[:, 0:8], ps, parts)
                ctx.op("v", lambda eng, ps=ps: eng.tensor_copy(lg.ap, ps[:, 0:8]),
                       reads=[ps], writes=[lg])
                ctx.op("v", lambda eng: eng.reduce_max(m1.ap, lg.ap, AX.X),
                       reads=[lg], writes=[m1])
                ctx.op("v", lambda eng: eng.tensor_scalar(mk1.ap, lg.ap, m1[:, 0:1],
                                                          None, ALU.is_ge),
                       reads=[lg, m1], writes=[mk1])
                ctx.op("v", lambda eng: eng.scalar_tensor_tensor(
                    l2.ap, mk1.ap, -1e30, lg.ap, ALU.mult, ALU.add),
                    reads=[mk1, lg], writes=[l2])
                ctx.op("v", lambda eng: eng.reduce_max(m2.ap, l2.ap, AX.X),
                       reads=[l2], writes=[m2])
                ctx.op("v", lambda eng: eng.tensor_scalar(mk2.ap, l2.ap, m2[:, 0:1],
                                                          None, ALU.is_ge),
                       reads=[l2, m2], writes=[mk2])
                ctx.op("v", lambda eng: eng.tensor_sub(p1.ap, m2.ap, m1.ap),
                       reads=[m1, m2], writes=[p1])
                ctx.op("s", lambda eng: eng.activation(p1.ap, p1.ap, AF.Sigmoid,
                                                       scale=-1.0),
                       reads=[p1], writes=[p1])
                ctx.op("v", lambda eng: eng.tensor_scalar(p2s.ap, p1.ap, -1.0, 1.0,
                                                          ALU.mult, ALU.add),
                       reads=[p1], writes=[p2s])
                ctx.op("v", lambda eng: eng.tensor_scalar(mk1.ap, mk1.ap, p1[:, 0:1],
                                                          None, ALU.mult),
                       reads=[mk1, p1], writes=[mk1])
                ctx.op("v", lambda eng: eng.scalar_tensor_tensor(
                    cw.ap, mk2.ap, p2s[:, 0:1], mk1.ap, ALU.mult, ALU.add),
                    reads=[mk2, p2s, mk1], writes=[cw])
                pt = sb.ps[7]
                ctx.op("t", lambda eng, pt=pt: eng.transpose(
                    pt[0:8, 0:128], cw.ap, ident.ap),
                    reads=[cw, ident], writes=[pt])
                ctx.op("v", lambda eng, p=p, pt=pt: eng.tensor_copy(
                    cwT[0:8, p * 128:(p + 1) * 128], pt[0:8, 0:128]),
                    reads=[pt], writes=[cwT])
            for e in range(E + 1):
                for hc in range(16):
                    wdram = sw2 if e == 0 else w2[e - 1]
                    ctx.dma("qw%d" % (hc % 4), w2res[hc].ap,
                            wdram[hc * 128:(hc + 1) * 128, :], writes=[w2res[hc]])
                for tt in range(NT):
                    sl = slice(tt * 512, (tt + 1) * 512)
                    if e == 0:
                        ctx.op("v", lambda eng: eng.memset(cwb.ap, 1.0), writes=[cwb])
                    else:
                        bc = sb.ps[6]
                        ctx.mm(bc.ap, bc, [(ohot[:, (e - 1) * 128:e * 128],
                                            cwT[0:8, sl],
                                            [ohot, cwT])])
                        ctx.op("v", lambda eng, bc=bc: eng.tensor_copy(cwb.ap, bc.ap),
                               reads=[bc], writes=[cwb])
                    for hc in range(16):
                        b0 = (hc % 2) * 2
                        slg = wsl[b0]
                        slu = wsl[b0 + 1]
                        wg = sw1 if e == 0 else w1[e - 1]
                        wu = sw3 if e == 0 else w3[e - 1]
                        ctx.dma("qw%d" % (hc % 2), slg[:, 0:1024],
                                wg[0:D, hc * 128:(hc + 1) * 128].rearrange(
                                    "(k p) c -> p k c", p=128),
                                writes=[slg])
                        ctx.dma("qw%d" % (2 + hc % 2), slu[:, 0:1024],
                                wu[0:D, hc * 128:(hc + 1) * 128].rearrange(
                                    "(k p) c -> p k c", p=128),
                                writes=[slu])
                        psg = sb.ps[0]
                        psu = sb.ps[1]
                        ctx.mm(psg[:, 0:128 * 4], psg,
                               [(slg[:, k * 128:(k + 1) * 128], xb[k][:, sl], [slg, xb[k]])
                                for k in range(NC)])
                        ctx.mm(psu[:, 0:128 * 4], psu,
                               [(slu[:, k * 128:(k + 1) * 128], xb[k][:, sl], [slu, xb[k]])
                                for k in range(NC)])
                        ctx.op("s", lambda eng, psg=psg: eng.activation(
                            ga.ap, psg.ap, AF.Silu), reads=[psg], writes=[ga])
                        ctx.op("v", lambda eng, psu=psu: eng.tensor_mul(
                            ua.ap, psu.ap, cwb.ap),
                            reads=[psu, cwb], writes=[ua])
                        ctx.op("v", lambda eng, hc=hc: eng.tensor_mul(
                            eo[hc].ap, ga.ap, ua.ap),
                            reads=[ga, ua], writes=[eo[hc]])
                    for hc in range(16):
                        for dc in range(NC):
                            ctx.mm(sb.ps[dc].ap, sb.ps[dc],
                                   [(w2res[hc][:, dc * 128:(dc + 1) * 128], eo[hc].ap,
                                     [w2res[hc], eo[hc]])],
                                   accumulate=(hc > 0), close=(hc == 15))
                    for dc in range(NC):
                        ctx.op("v", lambda eng, dc=dc, sl=sl: eng.tensor_add(
                            xo[dc][:, sl], xo[dc][:, sl], sb.ps[dc].ap),
                            reads=[sb.ps[dc], xo[dc]], writes=[xo[dc]])
            ctx.fence()
            with contextlib.ExitStack() as sD:
                xtmp = [ctx.sbuf(sD, "xr0", [128, T], F32)]
                for c in range(NC):
                    b = xtmp[0]
                    xld(ctx, b, c, q=QS_RR[c % 2])
                    ctx.op("v", lambda eng, a=xo[c], b=b: eng.tensor_add(
                        a.ap, a.ap, b.ap),
                        reads=[xo[c], b], writes=[xo[c]])
                emit_ln_fm(ctx, sb, xo, xo, 1e-5)
                for c in range(NC):
                    xst(ctx, xo[c], c)


def build_l1_ffn():
    nc = bass.Bass("TRN2", target_bir_lowering=False)
    aps = declare_l1_ffn(nc)
    xin = nc.dram_tensor("xin", [D, T], F32, kind="ExternalInput").ap()
    xout = nc.dram_tensor("xout", [D, T], F32, kind="ExternalOutput").ap()
    with contextlib.ExitStack() as stack:
        ctx = Ctx(nc)
        ctx.make_sems(stack)
        with nc.Block() as block:
            emit_l1_ffn(nc, ctx, stack, aps, plain_ld(xin), plain_st(xout))
            ctx.finish()
            ctx.build(block)
    return nc


def _weights_l1_ffn(ins):
    return {
        "rwT": _f32c(np.asarray(ins["L1_ffn_router_w"]).T),
        "ebias": _f32c(ins["L1_ffn_experts_bias"]),
        "w1": _f32c(ins["L1_ffn_w1"]), "w3": _f32c(ins["L1_ffn_w3"]),
        "w2": _f32c(ins["L1_ffn_w2"]),
        "sw1": _f32c(np.asarray(ins["L1_ffn_sw1"])[0]),
        "sw3": _f32c(np.asarray(ins["L1_ffn_sw3"])[0]),
        "sw2": _f32c(np.asarray(ins["L1_ffn_sw2"])[0]),
        "idin": np.eye(128, dtype=np.float32),
    }


def run_l1_ffn(x_fm_cores, ins):
    return _run_block("l1_ffn", build_l1_ffn, x_fm_cores, ins)



def declare_l2_ffn(nc, p=""):
    FF = 4096
    aps = {
        "w1S": nc.dram_tensor(p + "w1S", [FF // 128, 128, D], F32,
                              kind="ExternalInput").ap(),
        "w2S": nc.dram_tensor(p + "w2S", [FF // 128, 128, D], F32,
                              kind="ExternalInput").ap(),
        "h1S": nc.dram_tensor(p + "h1S", [FF // 128, 128, FF], F32,
                              kind="ExternalInput").ap(),
        "h2S": nc.dram_tensor(p + "h2S", [FF // 128, 128, FF], F32,
                              kind="ExternalInput").ap(),
        "w3T": nc.dram_tensor(p + "w3T", [FF, D], F32, kind="ExternalInput").ap(),
    }
    for name, ln in [("s1", FF), ("s2", FF), ("sh1", FF), ("sh2", FF), ("s3", D)]:
        aps[name] = nc.dram_tensor(p + name, [ln], F32, kind="ExternalInput").ap()
    return aps


def emit_l2_ffn(nc, ctx, stack, aps, xld, xst):
    FF = 4096
    w1S, w2S, h1S, h2S, w3T = (aps["w1S"], aps["w2S"], aps["h1S"], aps["h2S"],
                               aps["w3T"])
    vin = {n: aps[n] for n in ["s1", "s2", "sh1", "sh2", "s3"]}
    if True:
        sb = _mk_lnbuf(ctx, stack, Tl=512, n_tmp=1)
        sb.ps = [ctx.psum(stack, f"ps{i}", [128, 512], F32) for i in range(8)]
        xb = [ctx.sbuf(stack, f"xb{c}", [128, T], BF16) for c in range(NC)]
        hA = [ctx.sbuf(stack, f"hA{i}", [128, 512], BF16) for i in range(32)]
        hB = [ctx.sbuf(stack, f"hB{i}", [128, 512], BF16) for i in range(32)]
        wsl = [ctx.sbuf(stack, f"wsl{i}", [128, 4096], BF16) for i in range(2)]
        xr = ctx.sbuf(stack, "xr", [128, 512], F32)
        xnew = [ctx.sbuf(stack, f"xnew{i}", [128, 512], F32) for i in range(NC)]

        if True:
            emit_const_init(ctx, sb)
            cols = {n: load_vec(ctx, sb, ap, ln)
                    for (n, ln), ap in zip([("s1", FF), ("s2", FF), ("sh1", FF),
                                            ("sh2", FF), ("s3", D)],
                                           [vin["s1"], vin["s2"], vin["sh1"],
                                            vin["sh2"], vin["s3"]])}
            with contextlib.ExitStack() as sA:
                xtmp = ctx.sbuf(sA, "xtmp0", [128, T], F32)
                for c in range(NC):
                    xld(ctx, xtmp, c, q=QS_RR[c % 2])
                    ctx.op("v", lambda eng, a=xb[c], b=xtmp: eng.tensor_copy(a.ap, b.ap),
                           reads=[xtmp], writes=[xb[c]])
            ctx.fence()
            for tt in range(NT):
                sl = slice(tt * 512, (tt + 1) * 512)
                for fc in range(32):
                    for (wd, psl, buf) in ((w1S, 0, 0), (w2S, 1, 1)):
                        slab = wsl[buf]
                        ctx.dma("qw%d" % (buf * 2 + fc % 2),
                                slab[:, 0:D], wd[fc], writes=[slab])
                        ps = sb.ps[psl]
                        ctx.mm(ps.ap, ps,
                               [(slab[:, k * 128:(k + 1) * 128], xb[k][:, sl],
                                 [slab, xb[k]]) for k in range(NC)])
                    ctx.op("s", lambda eng, fc=fc: eng.activation(
                        sb.tmp[0][:, 0:512], sb.ps[0].ap, AF.Silu,
                        scale=sb.vcol(cols["s1"] + fc)),
                        reads=[sb.ps[0], sb.vecs], writes=[sb.tmp[0]])
                    ctx.op("v", lambda eng, fc=fc: eng.tensor_scalar(
                        hA[fc].ap, sb.ps[1].ap, sb.vcol(cols["s2"] + fc), None,
                        ALU.mult),
                        reads=[sb.ps[1], sb.vecs], writes=[hA[fc]])
                    ctx.op("v", lambda eng, fc=fc: eng.tensor_mul(
                        hA[fc].ap, hA[fc].ap, sb.tmp[0][:, 0:512]),
                        reads=[hA[fc], sb.tmp[0]], writes=[hA[fc]])
                for (wd, scol, src_h, dst_h) in ((h1S, cols["sh1"], hA, hB),
                                                 (h2S, cols["sh2"], hB, hA)):
                    for fo in range(32):
                        slab = wsl[fo % 2]
                        ctx.dma("qw%d" % ((fo % 2) * 2), slab[:, 0:2048],
                                wd[fo, :, 0:2048], writes=[slab])
                        ctx.dma("qw%d" % ((fo % 2) * 2 + 1), slab[:, 2048:4096],
                                wd[fo, :, 2048:4096], writes=[slab])
                        ps = sb.ps[2 + fo % 2]
                        ctx.mm(ps.ap, ps,
                               [(slab[:, fi * 128:(fi + 1) * 128], src_h[fi].ap,
                                 [slab, src_h[fi]]) for fi in range(32)])
                        ctx.op("s", lambda eng, fo=fo, ps=ps, scol=scol, dst=dst_h[fo]:
                               eng.activation(dst.ap, ps.ap, AF.Gelu,
                                              scale=sb.vcol(scol + fo)),
                               reads=[ps, sb.vecs], writes=[dst_h[fo]])
                for fc in range(32):
                    slab = wsl[fc % 2]
                    ctx.dma("qw%d" % (fc % 2), slab[:, 0:1024],
                            w3T[fc * 128:(fc + 1) * 128, 0:1024],
                            writes=[slab])
                    for dc in range(NC):
                        ctx.mm(sb.ps[dc].ap, sb.ps[dc],
                               [(slab[:, dc * 128:(dc + 1) * 128], hA[fc].ap,
                                 [slab, hA[fc]])],
                               accumulate=(fc > 0), close=(fc == 31))
                for dc in range(NC):
                    xld(ctx, xr, dc, col0=tt * 512, ncols=512, dcol0=0,
                        q=QS_RR[dc % 2])
                    ctx.op("v", lambda eng, dc=dc: eng.scalar_tensor_tensor(
                        xnew[dc].ap, sb.ps[dc].ap, sb.vcol(cols["s3"] + dc),
                        xr.ap, ALU.mult, ALU.add),
                        reads=[sb.ps[dc], sb.vecs, xr], writes=[xnew[dc]])
                emit_ln_fm(ctx, sb, xnew, xnew, 1e-5, Tl=512)
                for dc in range(NC):
                    xst(ctx, xnew[dc], dc, col0=tt * 512, ncols=512, scol0=0)


def build_l2_ffn():
    nc = bass.Bass("TRN2", target_bir_lowering=False)
    aps = declare_l2_ffn(nc)
    xin = nc.dram_tensor("xin", [D, T], F32, kind="ExternalInput").ap()
    xout = nc.dram_tensor("xout", [D, T], F32, kind="ExternalOutput").ap()
    with contextlib.ExitStack() as stack:
        ctx = Ctx(nc)
        ctx.make_sems(stack)
        with nc.Block() as block:
            emit_l2_ffn(nc, ctx, stack, aps, plain_ld(xin), plain_st(xout))
            ctx.finish()
            ctx.build(block)
    return nc


def _weights_l2_ffn(ins):
    return {
        "w1S": pack_lhsT(np.asarray(ins["L2_ffn_w1_w"]).T),
        "w2S": pack_lhsT(np.asarray(ins["L2_ffn_w2_w"]).T),
        "h1S": pack_lhsT(np.asarray(ins["L2_ffn_h1_w"]).T),
        "h2S": pack_lhsT(np.asarray(ins["L2_ffn_h2_w"]).T),
        "w3T": _f32c(np.asarray(ins["L2_ffn_w3_w"]).T),
        "s1": _f32c(ins["L2_ffn_w1_s"]), "s2": _f32c(ins["L2_ffn_w2_s"]),
        "sh1": _f32c(ins["L2_ffn_h1_s"]), "sh2": _f32c(ins["L2_ffn_h2_s"]),
        "s3": _f32c(ins["L2_ffn_w3_s"]),
    }


def run_l2_ffn(x_fm_cores, ins):
    return _run_block("l2_ffn", build_l2_ffn, x_fm_cores, ins)



_SM_RTS = np.roots([1.0, -3.0, 6.0, -6.0])
_SM_R = float(_SM_RTS[np.isreal(_SM_RTS)].real[0])
_SM_U = (_SM_R - 3.0) / 2.0
_SM_V = float((_SM_R * _SM_R - 3 * _SM_R + 6) - _SM_U * _SM_U)


def declare_l3_attn(nc, p=""):
    return {
        "wqkvT": nc.dram_tensor(p + "wqkvT", [D, 3 * D], F32,
                                kind="ExternalInput").ap(),
        "woT": nc.dram_tensor(p + "woT", [D, D], F32, kind="ExternalInput").ap(),
        "b_out": nc.dram_tensor(p + "b_out", [D], F32, kind="ExternalInput").ap(),
        "idin": nc.dram_tensor(p + "idin", [128, 128], F32,
                               kind="ExternalInput").ap(),
        "qk_d": nc.dram_tensor(p + "qk_d", [2 * D, T], BF16, kind="Internal").ap(),
        "v_d": nc.dram_tensor(p + "v_d", [T, D], BF16, kind="Internal").ap(),
    }


def emit_l3_attn(nc, ctx, stack, aps, xld, xst, nheads=16, mode='full'):
    H = 16
    HD = 64
    SC = float(HD) ** -0.5
    wqkvT, woT, b_out, idin = aps["wqkvT"], aps["woT"], aps["b_out"], aps["idin"]
    qk_d, v_d = aps["qk_d"], aps["v_d"]
    if True:
        sb = _mk_lnbuf(ctx, stack, Tl=512, n_tmp=1)
        sb.ps = [ctx.psum(stack, f"ps{i}", [128, 512], F32) for i in range(8)]
        ident = ctx.sbuf(stack, "ident", [128, 128], F32)
        wsl = [ctx.sbuf(stack, f"wsl{i}", [128, 1024], BF16) for i in range(2)]
        sb.wslab = wsl
        xb = [ctx.sbuf(stack, f"xb{c}", [128, T], BF16) for c in range(NC)]
        xo = [ctx.sbuf(stack, f"xo{c}", [128, T], BF16) for c in range(NC)]
        qh = ctx.sbuf(stack, "qh", [64, T], BF16)
        kh = ctx.sbuf(stack, "kh", [64, T], BF16)
        vhall = ctx.sbuf(stack, "vhall", [128, 1024], BF16)
        p2 = ctx.sbuf(stack, "p2", [128, T], F32)
        p2t = ctx.sbuf(stack, "p2t", [128, 16 * 512], BF16)
        sqb = ctx.sbuf(stack, "sqb", [128, T], F32)
        nmax4 = ctx.sbuf(stack, "nmax4", [128, 4], F32)
        nmax = ctx.sbuf(stack, "nmax", [128, 1], F32)
        bias1 = ctx.sbuf(stack, "bias1", [128, 1], F32)
        bias2 = ctx.sbuf(stack, "bias2", [128, 1], F32)
        zpart = ctx.sbuf(stack, "zpart", [128, 4], F32)
        zr = ctx.sbuf(stack, "zr", [128, 1], F32)
        xnew = [ctx.sbuf(stack, f"xnew{i}", [128, 512], F32) for i in range(NC)]
        xr = ctx.sbuf(stack, "xr", [128, 512], F32)
        qkvb = ctx.sbuf(stack, "qkvb", [128, T], BF16)

        if True:
            emit_const_init(ctx, sb)
            ctx.dma("qm", ident.ap, idin, writes=[ident])
            boc = load_vec(ctx, sb, b_out, D)
            with contextlib.ExitStack() as sA:
                xtmp = ctx.sbuf(sA, "xtmp0", [128, T], F32)
                for c in range(NC):
                    xld(ctx, xtmp, c, q=QS_RR[c % 2])
                    ctx.op("v", lambda eng, a=xb[c], b=xtmp: eng.tensor_copy(a.ap, b.ap),
                           reads=[xtmp], writes=[xb[c]])
            ctx.fence()
            for oc in range(16):
                slab = stream_w_slab(ctx, sb, wqkvT, oc % 2, oc * 128, 128)
                for tt in range(NT):
                    ps = sb.ps[tt % 2]
                    parts = [(slab[:, k * 128:(k + 1) * 128],
                              xb[k][:, tt * 512:(tt + 1) * 512],
                              [slab, xb[k]]) for k in range(NC)]
                    ctx.mm(ps.ap, ps, parts)
                    ctx.op("s", lambda eng, tt=tt, ps=ps: eng.activation(
                        qkvb[:, tt * 512:(tt + 1) * 512], ps.ap, AF.Identity),
                        reads=[ps], writes=[qkvb])
                ctx.dma("qo", qk_d[oc * 128:(oc + 1) * 128, :], qkvb.ap, reads=[qkvb])
            with contextlib.ExitStack() as sV:
                wv = [ctx.sbuf(sV, f"wv{c}", [128, D], BF16) for c in range(NC)]
                for c in range(NC):
                    ctx.dma("qw%d" % (c % 4), wv[c].ap,
                            wqkvT[c * 128:(c + 1) * 128, 2 * D:3 * D], writes=[wv[c]])
                for tt in range(T // 128):
                    for c in range(NC):
                        slab = wv[c]
                        for nh in range(2):
                            ctx.mm(sb.ps[2 + nh].ap, sb.ps[2 + nh],
                                   [(xb[c][:, tt * 128:(tt + 1) * 128],
                                     slab[:, nh * 512:(nh + 1) * 512], [xb[c], slab])],
                                   accumulate=(c > 0), close=(c == NC - 1))
                    for nh in range(2):
                        ctx.op("s", lambda eng, nh=nh: eng.activation(
                            qkvb[:, nh * 512:(nh + 1) * 512], sb.ps[2 + nh].ap, AF.Identity),
                            reads=[sb.ps[2 + nh]], writes=[qkvb])
                    ctx.dma("qo", v_d[tt * 128:(tt + 1) * 128, :], qkvb[:, 0:D],
                            reads=[qkvb])
            ctx.fence()
            for h in range(nheads):
                ctx.dma("qs", qh.ap, qk_d[h * 64:(h + 1) * 64, :], writes=[qh])
                ctx.dma("qs1", kh.ap, qk_d[D + h * 64:D + (h + 1) * 64, :], writes=[kh])
                ctx.dma("qs2", vhall.ap,
                        v_d[0:T, h * 64:(h + 1) * 64].rearrange(
                            "(k p) d -> p k d", p=128),
                        writes=[vhall])
                for qb in range(4):
                    for qt in range(4):
                        for nt in range(4):
                            ps = sb.ps[nt]
                            ctx.mm(ps.ap, ps,
                                   [(qh[:, (qb * 4 + qt) * 128:(qb * 4 + qt + 1) * 128],
                                     kh[:, nt * 512:(nt + 1) * 512], [qh, kh])])
                            ctx.op("v", lambda eng, nt=nt, ps=ps: eng.reduce_max(
                                nmax4[:, nt:nt + 1], ps.ap, AX.X, negate=True),
                                reads=[ps], writes=[nmax4])
                        ctx.op("v", lambda eng: eng.tensor_reduce(
                            nmax.ap, nmax4.ap, AX.X, ALU.min),
                            reads=[nmax4], writes=[nmax])
                        ctx.op("v", lambda eng: eng.tensor_scalar(
                            bias1.ap, nmax.ap, SC, _SM_U, ALU.mult, ALU.add),
                            reads=[nmax], writes=[bias1])
                        ctx.op("v", lambda eng: eng.tensor_scalar(
                            bias2.ap, nmax.ap, SC, -_SM_R, ALU.mult, ALU.add),
                            reads=[nmax], writes=[bias2])
                        for nt in range(4):
                            ps = sb.ps[nt]
                            sl5 = slice(nt * 512, (nt + 1) * 512)
                            ctx.op("s", lambda eng, ps=ps, sl5=sl5: eng.activation(
                                sqb[:, sl5], ps.ap, AF.Square, bias=bias1[:, 0:1],
                                scale=SC),
                                reads=[ps, bias1], writes=[sqb])
                            ctx.op("s", lambda eng, ps=ps, sl5=sl5: eng.activation(
                                p2[:, sl5], ps.ap, AF.Identity, bias=bias2[:, 0:1],
                                scale=SC),
                                reads=[ps, bias2], writes=[p2])
                        ctx.op("v", lambda eng: eng.scalar_tensor_tensor(
                            p2.ap, sqb.ap, _SM_V, p2.ap, ALU.add, ALU.mult),
                            reads=[sqb, p2], writes=[p2])
                        ctx.op("s", lambda eng: eng.activation(
                            p2.ap, p2.ap, AF.Ln, scale=-1.0 / 6.0),
                            reads=[p2], writes=[p2])
                        ctx.op("s", lambda eng: eng.activation(
                            p2.ap, p2.ap, AF.Exp, scale=-1.0,
                            accum_out=zr[:, 0:1]),
                            reads=[p2], writes=[p2, zr])
                        ctx.op("v", lambda eng: eng.reciprocal(zr.ap, zr.ap),
                               reads=[zr], writes=[zr])
                        ctx.op("v", lambda eng: eng.tensor_scalar(
                            p2.ap, p2.ap, zr[:, 0:1], None, ALU.mult),
                            reads=[p2, zr], writes=[p2])
                        for kc in range(16):
                            pt = sb.ps[5 + 2 * (kc % 2)]
                            ctx.op("t", lambda eng, kc=kc, pt=pt: eng.transpose(
                                pt[0:128, 0:128], p2[:, kc * 128:(kc + 1) * 128],
                                ident.ap),
                                reads=[p2, ident], writes=[pt])
                            dsl = slice(kc * 512 + qt * 128, kc * 512 + (qt + 1) * 128)
                            if kc % 2 == 0:
                                ctx.op("v", lambda eng, dsl=dsl, pt=pt: eng.tensor_copy(
                                    p2t[:, dsl], pt[0:128, 0:128]),
                                    reads=[pt], writes=[p2t])
                            else:
                                ctx.op("s", lambda eng, dsl=dsl, pt=pt: eng.activation(
                                    p2t[:, dsl], pt[0:128, 0:128], AF.Identity),
                                    reads=[pt], writes=[p2t])
                    if mode != "full":
                        continue
                    psav = sb.ps[4 + 2 * (qb % 2)]
                    for kc in range(16):
                        ctx.mm(psav[0:64, :], psav,
                               [(vhall[:, kc * 64:(kc + 1) * 64],
                                 p2t[:, kc * 512:(kc + 1) * 512],
                                 [vhall, p2t])],
                               accumulate=(kc > 0), close=(kc == 15))
                    ctx.op("s", lambda eng, h=h, qb=qb, psav=psav: eng.activation(
                        xo[h // 2][(h % 2) * 64:(h % 2) * 64 + 64,
                                   qb * 512:(qb + 1) * 512],
                        psav[0:64, :], AF.Identity),
                        reads=[psav], writes=[xo[h // 2]])
            ctx.fence()
            for tt in range(NT):
                sl = slice(tt * 512, (tt + 1) * 512)
                for oc in range(NC):
                    slab = stream_w_slab(ctx, sb, woT, oc % 2, oc * 128, 128)
                    ps = sb.ps[oc % 4]
                    parts = [(slab[:, k * 128:(k + 1) * 128], xo[k][:, sl],
                              [slab, xo[k]]) for k in range(NC)]
                    ctx.mm(ps.ap, ps, parts)
                    xld(ctx, xr, oc, col0=tt * 512, ncols=512, dcol0=0,
                        q=QS_RR[2 + oc % 2])
                    ctx.op("v", lambda eng, oc=oc, ps=ps: eng.tensor_scalar(
                        xnew[oc].ap, ps.ap, 1.0, sb.vcol(boc + oc), ALU.mult, ALU.add),
                        reads=[ps, sb.vecs], writes=[xnew[oc]])
                    ctx.op("v", lambda eng, oc=oc: eng.tensor_add(
                        xnew[oc].ap, xnew[oc].ap, xr.ap),
                        reads=[xnew[oc], xr], writes=[xnew[oc]])
                emit_ln_fm(ctx, sb, xnew, xnew, 1e-5, Tl=512)
                for oc in range(NC):
                    xst(ctx, xnew[oc], oc, col0=tt * 512, ncols=512, scol0=0)


def build_l3_attn(nheads=16, mode='full'):
    nc = bass.Bass("TRN2", target_bir_lowering=False)
    aps = declare_l3_attn(nc)
    xin = nc.dram_tensor("xin", [D, T], F32, kind="ExternalInput").ap()
    xout = nc.dram_tensor("xout", [D, T], F32, kind="ExternalOutput").ap()
    with contextlib.ExitStack() as stack:
        ctx = Ctx(nc)
        ctx.make_sems(stack)
        with nc.Block() as block:
            emit_l3_attn(nc, ctx, stack, aps, plain_ld(xin), plain_st(xout),
                         nheads, mode)
            ctx.finish()
            ctx.build(block)
    return nc


def _weights_l3_attn(ins):
    return {
        "wqkvT": _f32c(np.asarray(ins["L3_attn_w_qkv"]).T),
        "woT": _f32c(np.asarray(ins["L3_attn_w_out"]).T),
        "b_out": _f32c(ins["L3_attn_b_out"]),
        "idin": np.eye(128, dtype=np.float32),
    }


def run_l3_attn(x_fm_cores, ins):
    return _run_block("l3_attn", build_l3_attn, x_fm_cores, ins,
                      wkey="l3_attn")



L3F_DEPTH = 6
L3F_W = 256
L3F_DI = 2730
L3F_NA = 22
L3F_WIDTHS = [128] * 21 + [42]


def declare_l3_ffn(nc, p=""):
    aps = {
        "Wqkv": nc.dram_tensor(p + "Wqkv", [L3F_DEPTH, D, 3 * D], F32,
                               kind="ExternalInput").ap(),
        "Wo": nc.dram_tensor(p + "Wo", [L3F_DEPTH, D, D], F32,
                             kind="ExternalInput").ap(),
        "W1P": nc.dram_tensor(p + "W1P", [L3F_DEPTH, 2 * L3F_NA, 128, D], F32,
                              kind="ExternalInput").ap(),
        "W2P": nc.dram_tensor(p + "W2P", [L3F_DEPTH, L3F_NA * 128, D], F32,
                              kind="ExternalInput").ap(),
        "qscale": nc.dram_tensor(p + "qscale", [L3F_DEPTH, 64], F32,
                                 kind="ExternalInput").ap(),
        "kscale": nc.dram_tensor(p + "kscale", [L3F_DEPTH, 64], F32,
                                 kind="ExternalInput").ap(),
        "ln_g": nc.dram_tensor(p + "ln_g", [L3F_DEPTH, D], F32,
                               kind="ExternalInput").ap(),
        "ln_b": nc.dram_tensor(p + "ln_b", [L3F_DEPTH, D], F32,
                               kind="ExternalInput").ap(),
        "b1": nc.dram_tensor(p + "b1", [L3F_DEPTH, 5460], F32,
                             kind="ExternalInput").ap(),
        "b2": nc.dram_tensor(p + "b2", [L3F_DEPTH, D], F32,
                             kind="ExternalInput").ap(),
        "csq": nc.dram_tensor(p + "csq", [128, T], F32, kind="ExternalInput").ap(),
        "snq": nc.dram_tensor(p + "snq", [128, T], F32, kind="ExternalInput").ap(),
        "csk": nc.dram_tensor(p + "csk", [128, T], F32, kind="ExternalInput").ap(),
        "snk": nc.dram_tensor(p + "snk", [128, T], F32, kind="ExternalInput").ap(),
        "bho": nc.dram_tensor(p + "bho", [128, 8 * 16], F32,
                              kind="ExternalInput").ap(),
        "sel": nc.dram_tensor(p + "sel", [16, 8 * 128], F32,
                              kind="ExternalInput").ap(),
        "mw": nc.dram_tensor(p + "mw", [256, 512], F32, kind="ExternalInput").ap(),
        "idin": nc.dram_tensor(p + "idin", [128, 128], F32,
                               kind="ExternalInput").ap(),
        "qk_d": nc.dram_tensor(p + "qk_d", [2 * D, T], BF16, kind="Internal").ap(),
        "v_d": nc.dram_tensor(p + "v_d", [T, D], BF16, kind="Internal").ap(),
    }
    return aps


def emit_l3_ffn(nc, ctx, stack, aps, xld, xst):
    Wqkv, Wo, W1P, W2P = aps["Wqkv"], aps["Wo"], aps["W1P"], aps["W2P"]
    qscale, kscale = aps["qscale"], aps["kscale"]
    ln_g, ln_b, b1, b2 = aps["ln_g"], aps["ln_b"], aps["b1"], aps["b2"]
    csq, snq, csk, snk = aps["csq"], aps["snq"], aps["csk"], aps["snk"]
    bho, sel, mw, idin = aps["bho"], aps["sel"], aps["mw"], aps["idin"]
    qk_d, v_d = aps["qk_d"], aps["v_d"]
    if True:
        sb = _mk_lnbuf(ctx, stack, Tl=512, n_tmp=1)
        sb.ps = [ctx.psum(stack, f"ps{i}", [128, 512], F32) for i in range(8)]
        ident = ctx.sbuf(stack, "ident", [128, 128], F32)
        wsl = [ctx.sbuf(stack, f"wsl{i}", [128, 1024], BF16) for i in range(2)]
        sb.wslab = wsl
        x3 = [ctx.sbuf(stack, f"x3_{c}", [128, T], F32) for c in range(NC)]
        sxb = [ctx.sbuf(stack, f"sxb{c}", [128, T], BF16) for c in range(NC)]
        qtmp = [ctx.sbuf(stack, f"qtmp{c}", [128, T], BF16) for c in range(NC)]
        ao = qtmp
        sqs = ctx.sbuf(stack, "sqs", [128, 512], BF16)
        ss16 = ctx.sbuf(stack, "ss16", [16, T], F32)
        bhot = ctx.sbuf(stack, "bhot", [128, 8 * 16], BF16)
        selt = ctx.sbuf(stack, "selt", [16, 8 * 128], F32)
        rq = ctx.sbuf(stack, "rq", [128, 512], BF16)
        mwt = [ctx.sbuf(stack, f"mwt{i}", [128, 512], F32) for i in range(2)]
        nmax = ctx.sbuf(stack, "nmax", [128, 1], F32)
        bias1 = ctx.sbuf(stack, "bias1", [128, 1], F32)
        zr = ctx.sbuf(stack, "zr", [128, 1], F32)
        hb = [ctx.sbuf(stack, f"hb{i}", [128, 512], BF16) for i in range(L3F_NA)]
        xnb = [ctx.sbuf(stack, f"xnb{i}", [128, 512], BF16) for i in range(NC)]
        gat = ctx.sbuf(stack, "gat", [128, 512], F32)
        cs_t = ctx.sbuf_at("cs_t", [128, T], BF16, hb[0])
        sn_t = ctx.sbuf_at("sn_t", [128, T], BF16, hb[4])
        qn = ctx.sbuf_at("qn", [128, T], BF16, hb[8])
        rot = ctx.sbuf_at("rot", [128, T], BF16, hb[12])
        t1b = ctx.sbuf_at("t1b", [128, T], BF16, hb[16])
        qh = ctx.sbuf_at("qh", [64, T], BF16, hb[0])
        kh = ctx.sbuf_at("kh", [64, T], BF16, hb[4])
        pt2 = ctx.sbuf_at("pt2", [128, 1024], BF16, hb[8])
        smb = ctx.sbuf_at("smb", [128, 512], F32, hb[10])
        vh = [ctx.sbuf_at(f"vh{i}", [128, 64], BF16, hb[12], delta=i * 128)
              for i in range(16)]
        vhall = ctx.sbuf_at("vhall", [128, 1024], BF16, hb[12])

        if True:
            emit_const_init(ctx, sb)
            ctx.dma("qm", ident.ap, idin, writes=[ident])
            ctx.dma("qw0", bhot.ap, bho, writes=[bhot])
            ctx.dma("qm", selt.ap, sel, writes=[selt])
            for i in range(2):
                ctx.dma("qm", mwt[i].ap, mw[i * 128:(i + 1) * 128, :], writes=[mwt[i]])
            for c in range(NC):
                xld(ctx, x3[c], c, q=QS_RR[c % 4])
            vc_b1a = sb.valloc(22)
            vc_b1g = sb.valloc(22)
            vc_b2 = sb.valloc(8)
            vc_g = sb.valloc(8)
            vc_bb = sb.valloc(8)
            vc_qs = sb.valloc(1)
            vc_ks = sb.valloc(1)

            def load_lvec(dram_row, off, col0, length):
                nchv = ceil_div(length, 128)
                for j in range(nchv):
                    w = min(128, length - j * 128)
                    ctx.dma("qm", sb.vecs[0:w, col0 + j:col0 + j + 1],
                            dram_row[off + j * 128:off + j * 128 + w, None],
                            writes=[sb.vecs])

            for li in range(L3F_DEPTH):
                load_lvec(b1[li], 0, vc_b1a, L3F_DI)
                load_lvec(b1[li], L3F_DI, vc_b1g, L3F_DI)
                load_lvec(b2[li], 0, vc_b2, D)
                load_lvec(ln_g[li], 0, vc_g, D)
                load_lvec(ln_b[li], 0, vc_bb, D)
                for (vc, sc_d) in ((vc_qs, qscale), (vc_ks, kscale)):
                    ctx.dma("qm", sb.vecs[0:64, vc:vc + 1], sc_d[li][:, None],
                            writes=[sb.vecs])
                    ctx.dma("qm", sb.vecs[64:128, vc:vc + 1], sc_d[li][:, None],
                            writes=[sb.vecs])
                cast_stream(ctx, sb, x3, sxb, shift_chunks=(4, 5, 6, 7))
                for (part, cs_d, sn_d, vcs) in ((0, csq, snq, vc_qs),
                                                (1, csk, snk, vc_ks)):
                    ctx.dma("qw2", cs_t.ap, cs_d, writes=[cs_t])
                    ctx.dma("qw3", sn_t.ap, sn_d, writes=[sn_t])
                    for c in range(NC):
                        oc = part * 8 + c
                        slab = stream_w_slab(ctx, sb, Wqkv[li], c % 2, oc * 128, 128)
                        for tt in range(NT):
                            sl = slice(tt * 512, (tt + 1) * 512)
                            ps = sb.ps[tt % 2]
                            ctx.mm(ps.ap, ps,
                                   [(slab[:, k * 128:(k + 1) * 128], sxb[k][:, sl],
                                     [slab, sxb[k]]) for k in range(NC)])
                            ctx.op("s", lambda eng, ps=ps, c=c, sl=sl: eng.activation(
                                qtmp[c][:, sl], ps.ap, AF.Identity),
                                reads=[ps], writes=[qtmp[c]])
                            ctx.op("s", lambda eng, ps=ps: eng.activation(
                                sqs.ap, ps.ap, AF.Square),
                                reads=[ps], writes=[sqs])
                            ctx.mm(sb.ps[2 + tt][0:16, :], sb.ps[2 + tt],
                                   [(bhot[:, c * 16:(c + 1) * 16], sqs.ap,
                                     [bhot, sqs])],
                                   accumulate=(c > 0), close=(c == NC - 1))
                    for tt in range(NT):
                        ctx.op("v", lambda eng, tt=tt: eng.tensor_copy(
                            ss16[0:16, tt * 512:(tt + 1) * 512],
                            sb.ps[2 + tt][0:16, :]),
                            reads=[sb.ps[2 + tt]], writes=[ss16])
                    ctx.op("s", lambda eng: eng.activation(ss16.ap, ss16.ap, AF.Ln),
                           reads=[ss16], writes=[ss16])
                    ctx.op("s", lambda eng: eng.activation(ss16.ap, ss16.ap, AF.Exp,
                                                           scale=-0.5),
                           reads=[ss16], writes=[ss16])
                    ctx.op("v", lambda eng: eng.tensor_scalar(
                        ss16.ap, ss16.ap, 1e12, None, ALU.min),
                        reads=[ss16], writes=[ss16])
                    for c in range(NC):
                        oc = part * 8 + c
                        for tt in range(NT):
                            sl = slice(tt * 512, (tt + 1) * 512)
                            bc = sb.ps[6]
                            ctx.mm(bc.ap, bc,
                                   [(selt[:, c * 128:(c + 1) * 128],
                                     ss16[0:16, sl],
                                     [selt, ss16])])
                            ctx.op("v", lambda eng, bc=bc, vcs=vcs:
                                   eng.tensor_scalar(
                                       rq.ap, bc.ap,
                                       sb.vcol(vcs), None, ALU.mult),
                                   reads=[bc, sb.vecs], writes=[rq])
                            ctx.op("v", lambda eng, c=c, sl=sl: eng.tensor_mul(
                                qn[:, sl], qtmp[c][:, sl], rq.ap),
                                reads=[qtmp[c], rq], writes=[qn])
                        for (da, sa) in ((0, 32), (32, 0), (64, 96), (96, 64)):
                            ctx.dma("qs2" if da < 64 else "qs3",
                                    rot.h[da:da + 32, :], qn.h[sa:sa + 32, :],
                                    reads=[qn], writes=[rot])
                        ctx.op("v", lambda eng: eng.tensor_mul(t1b.ap, qn.ap, cs_t.ap),
                               reads=[qn, cs_t], writes=[t1b])
                        ctx.op("v", lambda eng: eng.tensor_mul(rot.ap, rot.ap, sn_t.ap),
                               reads=[rot, sn_t], writes=[rot])
                        ctx.op("v", lambda eng: eng.tensor_add(t1b.ap, t1b.ap, rot.ap),
                               reads=[t1b, rot], writes=[t1b])
                        ctx.dma("qo", qk_d[oc * 128:(oc + 1) * 128, :], t1b.ap,
                                reads=[t1b])
                for tt in range(T // 128):
                    for c in range(NC):
                        slab = wsl[c % 2]
                        ctx.dma("qw%d" % (2 + c % 2), slab.ap,
                                Wqkv[li][c * 128:(c + 1) * 128, 2 * D:3 * D],
                                writes=[slab])
                        for nh in range(2):
                            ctx.mm(sb.ps[nh].ap, sb.ps[nh],
                                   [(sxb[c][:, tt * 128:(tt + 1) * 128],
                                     slab[:, nh * 512:(nh + 1) * 512],
                                     [sxb[c], slab])],
                                   accumulate=(c > 0), close=(c == NC - 1))
                    for nh in range(2):
                        ctx.op("s", lambda eng, nh=nh: eng.activation(
                            t1b[:, nh * 512:(nh + 1) * 512], sb.ps[nh].ap,
                            AF.Identity),
                            reads=[sb.ps[nh]], writes=[t1b])
                    ctx.dma("qo", v_d[tt * 128:(tt + 1) * 128, :], t1b[:, 0:D],
                            reads=[t1b])
                ctx.fence()
                for h in range(16):
                    ctx.dma("qs", qh.ap, qk_d[h * 64:(h + 1) * 64, :], writes=[qh])
                    ctx.dma("qs1", kh.ap, qk_d[D + h * 64:D + (h + 1) * 64, :],
                            writes=[kh])
                    ctx.dma("qs2", vhall.ap,
                            v_d[0:T, h * 64:(h + 1) * 64].rearrange(
                                "(k p) d -> p k d", p=128),
                            writes=[vhall] + vh)
                    for w in range(8):
                        kw = 256 if w == 0 else 512
                        k0 = 0 if w == 0 else (w - 1) * 256
                        for qt2 in range(2):
                            qlo = w * 256 + qt2 * 128
                            ps = sb.ps[qt2]
                            ctx.mm(ps[:, 0:kw], ps,
                                   [(qh[:, qlo:qlo + 128], kh[:, k0:k0 + kw],
                                     [qh, kh])])
                            mtb = mwt[qt2]
                            mtap = mtb[:, 256:512] if w == 0 else mtb[:, 0:512]
                            ctx.op("v", lambda eng, ps=ps, mtap=mtap, kw=kw:
                                   eng.tensor_add(smb[:, 0:kw], ps[:, 0:kw],
                                                  mtap),
                                   reads=[ps, mtb], writes=[smb])
                            ctx.op("v", lambda eng, kw=kw: eng.reduce_max(
                                nmax[:, 0:1], smb[:, 0:kw], AX.X, negate=True),
                                reads=[smb], writes=[nmax])
                            ctx.op("v", lambda eng: eng.tensor_scalar(
                                bias1.ap, nmax.ap, 8.0, None, ALU.mult),
                                reads=[nmax], writes=[bias1])
                            ctx.op("s", lambda eng, kw=kw: eng.activation(
                                smb[:, 0:kw], smb[:, 0:kw], AF.Exp,
                                bias=bias1[:, 0:1], scale=8.0,
                                accum_out=zr[:, 0:1]),
                                reads=[smb, bias1], writes=[smb, zr])
                            ctx.op("v", lambda eng: eng.reciprocal(zr.ap, zr.ap),
                                   reads=[zr], writes=[zr])
                            ctx.op("v", lambda eng, kw=kw: eng.tensor_scalar(
                                smb[:, 0:kw], smb[:, 0:kw], zr[:, 0:1], None,
                                ALU.mult),
                                reads=[smb, zr], writes=[smb])
                            for kc2 in range(kw // 128):
                                pt = sb.ps[5 + (kc2 % 2)]
                                ctx.op("t", lambda eng, kc2=kc2, pt=pt: eng.transpose(
                                    pt[0:128, 0:128],
                                    smb[:, kc2 * 128:(kc2 + 1) * 128], ident.ap),
                                    reads=[smb, ident], writes=[pt])
                                dsl = slice(kc2 * 256 + qt2 * 128,
                                            kc2 * 256 + (qt2 + 1) * 128)
                                if kc2 % 2 == 0:
                                    ctx.op("v", lambda eng, dsl=dsl, pt=pt:
                                           eng.tensor_copy(pt2[:, dsl],
                                                           pt[0:128, 0:128]),
                                           reads=[pt], writes=[pt2])
                                else:
                                    ctx.op("s", lambda eng, dsl=dsl, pt=pt:
                                           eng.activation(pt2[:, dsl],
                                                          pt[0:128, 0:128],
                                                          AF.Identity),
                                           reads=[pt], writes=[pt2])
                        nkc = kw // 128
                        psav = sb.ps[4] if w % 2 == 0 else sb.ps[2]
                        for kc2 in range(nkc):
                            kcg = (k0 // 128) + kc2
                            ctx.mm(psav[0:64, 0:256], psav,
                                   [(vh[kcg].ap, pt2[:, kc2 * 256:(kc2 + 1) * 256],
                                     [vh[kcg], pt2])],
                                   accumulate=(kc2 > 0), close=(kc2 == nkc - 1))
                        ctx.op("s", lambda eng, h=h, w=w, psav=psav: eng.activation(
                            ao[h // 2][(h % 2) * 64:(h % 2) * 64 + 64,
                                       w * 256:(w + 1) * 256],
                            psav[0:64, 0:256], AF.Identity),
                            reads=[psav], writes=[ao[h // 2]])
                ctx.fence()
                for tt in range(NT):
                    sl = slice(tt * 512, (tt + 1) * 512)
                    for oc in range(NC):
                        slab = stream_w_slab(ctx, sb, Wo[li], oc % 2, oc * 128, 128)
                        ps = sb.ps[oc % 4]
                        ctx.mm(ps.ap, ps,
                               [(slab[:, k * 128:(k + 1) * 128], ao[k][:, sl],
                                 [slab, ao[k]]) for k in range(NC)])
                        ctx.op("v", lambda eng, oc=oc, ps=ps, sl=sl: eng.tensor_add(
                            x3[oc][:, sl], x3[oc][:, sl], ps.ap),
                            reads=[ps, x3[oc]], writes=[x3[oc]])
                ctx.fence()
                cast_stream(ctx, sb, x3, sxb, shift_chunks=(4, 5, 6, 7))
                for tt in range(NT):
                    sl = slice(tt * 512, (tt + 1) * 512)
                    emit_ln_fm(ctx, sb, sxb, xnb, 1e-5, gamma_col=vc_g,
                               beta_col=vc_bb, src_bf16=True, Tl=512,
                               src_off=tt * 512, dst_off=0)
                    for j in range(L3F_NA):
                        wj = L3F_WIDTHS[j]
                        slab_a = wsl[0]
                        slab_g = wsl[1]
                        ctx.dma("qw0", slab_a.ap, W1P[li, j], writes=[slab_a])
                        ctx.dma("qw1", slab_g.ap, W1P[li, L3F_NA + j],
                                writes=[slab_g])
                        psa = sb.ps[0]
                        psg = sb.ps[1]
                        ctx.mm(psa.ap, psa,
                               [(slab_a[:, k * 128:(k + 1) * 128], xnb[k].ap,
                                 [slab_a, xnb[k]]) for k in range(NC)])
                        ctx.mm(psg.ap, psg,
                               [(slab_g[:, k * 128:(k + 1) * 128], xnb[k].ap,
                                 [slab_g, xnb[k]]) for k in range(NC)])
                        ctx.op("v", lambda eng, psa=psa, j=j, wj=wj: eng.tensor_scalar(
                            gat[0:wj, :], psa[0:wj, :], sb.vcol(vc_b1a + j)[0:wj, :],
                            None, ALU.add),
                            reads=[psa, sb.vecs], writes=[gat])
                        ctx.op("s", lambda eng, psg=psg, j=j, wj=wj: eng.activation(
                            sqs[0:wj, :], psg[0:wj, :], AF.Gelu,
                            bias=sb.vcol(vc_b1g + j)[0:wj, :]),
                            reads=[psg, sb.vecs], writes=[sqs])
                        ctx.op("v", lambda eng, j=j, wj=wj: eng.tensor_mul(
                            hb[j][0:wj, :], gat[0:wj, :], sqs[0:wj, :]),
                            reads=[gat, sqs], writes=[hb[j]])
                    for j in range(L3F_NA):
                        wj = L3F_WIDTHS[j]
                        slab = wsl[j % 2]
                        ctx.dma("qw%d" % (2 + j % 2), slab.ap,
                                W2P[li, j * 128:(j + 1) * 128, :], writes=[slab])
                        for dc in range(NC):
                            ctx.mm(sb.ps[dc].ap, sb.ps[dc],
                                   [(slab[0:wj, dc * 128:(dc + 1) * 128],
                                     hb[j][0:wj, :], [slab, hb[j]])],
                                   accumulate=(j > 0), close=(j == L3F_NA - 1))
                    for dc in range(NC):
                        ctx.op("v", lambda eng, dc=dc, sl=sl: eng.scalar_tensor_tensor(
                            x3[dc][:, sl], sb.ps[dc].ap, sb.vcol(vc_b2 + dc),
                            x3[dc][:, sl], ALU.add, ALU.add),
                            reads=[sb.ps[dc], sb.vecs, x3[dc]], writes=[x3[dc]])
                ctx.fence()
            xnew = [ctx.sbuf_at(f"xnew{i}", [128, 512], F32, hb[0], delta=i * 2048)
                    for i in range(NC)]
            xr = ctx.sbuf_at("xr", [128, 512], F32, hb[16])
            ctx.fence()
            for tt in range(NT):
                sl = slice(tt * 512, (tt + 1) * 512)
                for oc in range(NC):
                    xld(ctx, xr, oc, col0=tt * 512, ncols=512, dcol0=0,
                        q=QS_RR[oc % 2])
                    ctx.op("v", lambda eng, oc=oc, sl=sl: eng.tensor_add(
                        xnew[oc].ap, x3[oc][:, sl], xr.ap),
                        reads=[x3[oc], xr], writes=[xnew[oc]])
                emit_ln_fm(ctx, sb, xnew, xnew, 1e-5, Tl=512)
                for oc in range(NC):
                    xst(ctx, xnew[oc], oc, col0=tt * 512, ncols=512, scol0=0)


def build_l3_ffn():
    nc = bass.Bass("TRN2", target_bir_lowering=False)
    aps = declare_l3_ffn(nc)
    xin = nc.dram_tensor("xin", [D, T], F32, kind="ExternalInput").ap()
    xout = nc.dram_tensor("xout", [D, T], F32, kind="ExternalOutput").ap()
    with contextlib.ExitStack() as stack:
        ctx = Ctx(nc)
        ctx.make_sems(stack)
        with nc.Block() as block:
            emit_l3_ffn(nc, ctx, stack, aps, plain_ld(xin), plain_st(xout))
            ctx.finish()
            ctx.build(block)
    return nc


def _l3f_rotary_tables():
    DIM_HEAD = 64
    WINDOW = 256
    n = T
    half = np.arange(0, DIM_HEAD, 2, dtype=np.float64)
    inv_freq = 1.0 / 10000.0 ** (half / DIM_HEAD)
    t = np.arange(n, dtype=np.float64)
    freqs = t[:, None] * inv_freq[None, :]
    freqs = np.concatenate([freqs, freqs], -1)
    base_scale = (half + 0.4 * DIM_HEAD) / (1.4 * DIM_HEAD)
    power = (t - n // 2) / (WINDOW // 2)
    scale = base_scale[None, :] ** power[:, None]
    scale = np.concatenate([scale, scale], -1)
    cos = np.cos(freqs)
    sin = np.sin(freqs)
    sgn = np.where(np.arange(DIM_HEAD) < 32, -1.0, 1.0)[None, :]
    csq_ = (cos * scale).T
    snq_ = (sin * scale * sgn).T
    csk_ = (cos / scale).T
    snk_ = (sin / scale * sgn).T
    stack2 = lambda a: np.ascontiguousarray(
        np.concatenate([a, a], axis=0), dtype=np.float32)
    return stack2(csq_), stack2(snq_), stack2(csk_), stack2(snk_)


def _weights_l3_ffn(ins):
    csq_, snq_, csk_, snk_ = _l3f_rotary_tables()
    bho = np.zeros((128, 8 * 16), dtype=np.float32)
    sel = np.zeros((16, 8 * 128), dtype=np.float32)
    for c in range(8):
        bho[0:64, c * 16 + 2 * c] = 1.0
        bho[64:128, c * 16 + 2 * c + 1] = 1.0
        sel[2 * c, c * 128:c * 128 + 64] = 1.0
        sel[2 * c + 1, c * 128 + 64:(c + 1) * 128] = 1.0
    i_ = np.arange(256)[:, None]
    jj = np.arange(256)[None, :]
    m0 = np.where(jj <= i_, 0.0, -1e30).astype(np.float32)
    j_ = np.arange(512)[None, :]
    mwv = np.where((j_ >= i_) & (j_ <= i_ + 256), 0.0, -1e30).astype(np.float32)
    W1 = np.asarray(ins["L3_ffn_W1"], dtype=np.float32)
    W1X = np.zeros((L3F_DEPTH, 1024, 2 * L3F_NA * 128), dtype=np.float32)
    for b in range(L3F_NA):
        wj = L3F_WIDTHS[b]
        W1X[:, :, b * 128:b * 128 + wj] = W1[:, :, b * 128:b * 128 + wj]
        W1X[:, :, (L3F_NA + b) * 128:(L3F_NA + b) * 128 + wj] = \
            W1[:, :, L3F_DI + b * 128:L3F_DI + b * 128 + wj]
    W1Pv = np.stack([pack_lhsT(W1X[l]) for l in range(L3F_DEPTH)], axis=0)
    W2 = np.asarray(ins["L3_ffn_W2"], dtype=np.float32)
    W2Pv = np.zeros((L3F_DEPTH, L3F_NA * 128, D), dtype=np.float32)
    W2Pv[:, 0:L3F_DI, :] = W2
    return {
        "Wqkv": _f32c(ins["L3_ffn_Wqkv"]), "Wo": _f32c(ins["L3_ffn_Wo"]),
        "W1P": W1Pv, "W2P": W2Pv,
        "qscale": _f32c(ins["L3_ffn_q_scale"]), "kscale": _f32c(ins["L3_ffn_k_scale"]),
        "ln_g": _f32c(ins["L3_ffn_ln_g"]), "ln_b": _f32c(ins["L3_ffn_ln_b"]),
        "b1": _f32c(ins["L3_ffn_b1"]), "b2": _f32c(ins["L3_ffn_b2"]),
        "csq": csq_, "snq": snq_, "csk": csk_, "snk": snk_,
        "bho": bho, "sel": sel, "mw": mwv,
        "idin": np.eye(128, dtype=np.float32),
    }


def run_l3_ffn(x_fm_cores, ins):
    return _run_block("l3_ffn", build_l3_ffn, x_fm_cores, ins)




def declare_l2_attn(nc, p=""):
    return {
        "winT": nc.dram_tensor(p + "winT", [D, 3 * D], F32,
                               kind="ExternalInput").ap(),
        "b_in": nc.dram_tensor(p + "b_in", [3 * D], F32, kind="ExternalInput").ap(),
        "kde": nc.dram_tensor(p + "kde", [11, 64], F32, kind="ExternalInput").ap(),
        "vde": nc.dram_tensor(p + "vde", [11, 64], F32, kind="ExternalInput").ap(),
        "woT": nc.dram_tensor(p + "woT", [D, D], F32, kind="ExternalInput").ap(),
        "b_out": nc.dram_tensor(p + "b_out", [D], F32, kind="ExternalInput").ap(),
        "bho": nc.dram_tensor(p + "bho", [128, 8 * 16], F32,
                              kind="ExternalInput").ap(),
        "sel": nc.dram_tensor(p + "sel", [16, 8 * 128], F32,
                              kind="ExternalInput").ap(),
        "idin": nc.dram_tensor(p + "idin", [128, 128], F32,
                               kind="ExternalInput").ap(),
    }


def emit_l2_attn(nc, ctx, stack, aps, xld, xst):
    S8 = 8
    B256 = 256
    winT, b_in, kde, vde = aps["winT"], aps["b_in"], aps["kde"], aps["vde"]
    woT, b_out, bho, sel, idin = (aps["woT"], aps["b_out"], aps["bho"],
                                  aps["sel"], aps["idin"])
    if True:
        sb = _mk_lnbuf(ctx, stack, Tl=512, n_tmp=1)
        sb.ps = [ctx.psum(stack, f"ps{i}", [128, 512], F32) for i in range(8)]
        ident = ctx.sbuf(stack, "ident", [128, 128], F32)
        wsl = [ctx.sbuf(stack, f"wsl{i}", [128, 1024], BF16) for i in range(2)]
        sb.wslab = wsl
        xb = [ctx.sbuf(stack, f"xb{c}", [128, T], BF16) for c in range(NC)]
        qc = [ctx.sbuf(stack, f"qc{c}", [128, T], BF16) for c in range(NC)]
        kc = [ctx.sbuf(stack, f"kc{c}", [128, T], BF16) for c in range(NC)]
        vc = [ctx.sbuf(stack, f"vc{c}", [128, T], BF16) for c in range(NC)]
        kdt = ctx.sbuf(stack, "kdt", [128, 11], F32)
        vdt = ctx.sbuf(stack, "vdt", [128, 11], F32)
        bhot = ctx.sbuf(stack, "bhot", [128, 8 * 16], BF16)
        selt = ctx.sbuf(stack, "selt", [16, 8 * 128], F32)
        Lb = [ctx.sbuf(stack, f"Lb{i}", [128, 1024], F32) for i in range(2)]
        Pb = [ctx.sbuf(stack, f"Pb{i}", [128, 1024], F32) for i in range(2)]
        gm = ctx.sbuf(stack, "gm", [128, 512], F32)
        gz = ctx.sbuf(stack, "gz", [128, 128], F32)
        pt16 = ctx.sbuf(stack, "pt16", [16, 256], F32)
        vpl = ctx.sbuf(stack, "vpl", [128, 256], BF16)
        tmp1 = ctx.sbuf(stack, "tmp1", [128, 256], F32)
        tmp1b = ctx.sbuf(stack, "tmp1b", [128, 128], BF16)
        scr16 = ctx.sbuf(stack, "scr16", [128, 16], F32)
        out1 = [ctx.sbuf_at(f"out1_{c}", [128, T], F32,
                            qc[2 * c] if c < 4 else kc[2 * (c - 4)])
                for c in range(NC)]
        ob = [ctx.sbuf_at(f"ob{c}", [128, T], BF16, vc[c]) for c in range(NC)]
        xnew = [ctx.sbuf_at(f"xnew{i}", [128, 512], F32, Lb[0], delta=i * 2048)
                for i in range(NC)]
        xr = ctx.sbuf_at("xr", [128, 512], F32, gm)

        if True:
            emit_const_init(ctx, sb)
            ctx.dma("qm", ident.ap, idin, writes=[ident])
            ctx.dma("qw0", bhot.ap, bho, writes=[bhot])
            ctx.dma("qm", selt.ap, sel, writes=[selt])
            for j in range(11):
                for half in range(2):
                    ctx.dma("qm", kdt[half * 64:(half + 1) * 64, j:j + 1],
                            kde[j][:, None], writes=[kdt])
                    ctx.dma("qm", vdt[half * 64:(half + 1) * 64, j:j + 1],
                            vde[j][:, None], writes=[vdt])
            bic = load_vec(ctx, sb, b_in, 3 * D)
            boc = load_vec(ctx, sb, b_out, D)
            ctx.op("v", lambda eng: eng.tensor_scalar(
                sb.vecs[:, bic:bic + 8], sb.vecs[:, bic:bic + 8], 0.125, None,
                ALU.mult),
                reads=[sb.vecs], writes=[sb.vecs])
            with contextlib.ExitStack() as sA:
                xtmp = ctx.sbuf(sA, "xtmp0", [128, T], F32)
                for c in range(NC):
                    xld(ctx, xtmp, c, q=QS_RR[c % 2])
                    ctx.op("v", lambda eng, a=xb[c], b=xtmp: eng.tensor_copy(a.ap, b.ap),
                           reads=[xtmp], writes=[xb[c]])
            ctx.fence()
            for grp, dst in ((0, qc), (1, kc), (2, vc)):
                for c in range(NC):
                    oc = grp * 8 + c
                    slab = stream_w_slab(ctx, sb, winT, c % 2, oc * 128, 128)
                    for tt in range(NT):
                        sl = slice(tt * 512, (tt + 1) * 512)
                        ps = sb.ps[tt % 2]
                        ctx.mm(ps.ap, ps,
                               [(slab[:, k * 128:(k + 1) * 128], xb[k][:, sl],
                                 [slab, xb[k]]) for k in range(NC)])
                        sc = 0.125 if grp == 0 else 1.0
                        ctx.op("s", lambda eng, ps=ps, dst=dst, c=c, sl=sl, sc=sc,
                               oc=oc: eng.activation(
                                   dst[c][:, sl], ps.ap, AF.Identity,
                                   bias=sb.vcol(bic + oc), scale=sc),
                               reads=[ps, sb.vecs], writes=[dst[c]])
            for bh in range(2):
                for s in range(S8):
                    for t in range(S8):
                        dist = abs(s - t)
                        p16 = sb.ps[4]
                        for c in range(NC):
                            ksl = kc[c][:, t * B256 + bh * 128:
                                        t * B256 + (bh + 1) * 128]
                            qsl = qc[c][:, s * B256 + bh * 128:
                                        s * B256 + (bh + 1) * 128]
                            ctx.op("v", lambda eng, ksl=ksl, c=c, dist=dist:
                                   eng.tensor_scalar(
                                       vpl[:, 0:128], ksl,
                                       kdt[:, dist:dist + 1], None, ALU.add),
                                   reads=[kc[c], kdt], writes=[vpl])
                            ctx.op("v", lambda eng, qsl=qsl, c=c: eng.tensor_mul(
                                tmp1b.ap, qsl, vpl[:, 0:128]),
                                reads=[qc[c], vpl], writes=[tmp1b])
                            ctx.mm(p16[0:16, 0:128], p16,
                                   [(bhot[:, c * 16:(c + 1) * 16],
                                     tmp1b.ap, [bhot, tmp1b])],
                                   accumulate=(c > 0), close=(c == NC - 1))
                        ctx.op("v", lambda eng, p16=p16: eng.tensor_copy(
                            pt16[0:16, 0:128], p16[0:16, 0:128]),
                            reads=[p16], writes=[pt16])
                        pt = sb.ps[5]
                        ctx.op("t", lambda eng: eng.transpose(
                            pt[0:128, 0:16], pt16[0:16, 0:128], ident[0:16, 0:16]),
                            reads=[pt16, ident], writes=[pt])
                        ctx.op("v", lambda eng, bh=bh, s=s, t=t, pt=pt: eng.tensor_copy(
                            bass.AP(Lb[bh].h, s * 128 + t, [[1024, 128], [8, 16]]),
                            pt[0:128, 0:16]),
                            reads=[pt], writes=[Lb[bh]])
                L = Lb[bh]
                P = Pb[bh]
                ctx.op("v", lambda eng, L=L: eng.tensor_max(
                    gm[:, 0:512], bass.AP(L.h, 0, [[1024, 128], [2, 512]]),
                    bass.AP(L.h, 1, [[1024, 128], [2, 512]])),
                    reads=[L], writes=[gm])
                ctx.op("v", lambda eng: eng.tensor_max(
                    gm[:, 0:256], bass.AP(gm.h, 0, [[512, 128], [2, 256]]),
                    bass.AP(gm.h, 1, [[512, 128], [2, 256]])),
                    reads=[gm], writes=[gm])
                ctx.op("v", lambda eng: eng.tensor_max(
                    gz[:, 0:128], bass.AP(gm.h, 0, [[512, 128], [2, 128]]),
                    bass.AP(gm.h, 1, [[512, 128], [2, 128]])),
                    reads=[gm], writes=[gz])
                ctx.op("v", lambda eng, L=L, P=P: eng.tensor_sub(
                    P.ap, L.ap,
                    bass.AP(gz.h, 0, [[128, 128], [1, 128], [0, 8]])),
                    reads=[L, gz], writes=[P])
                ctx.op("s", lambda eng, P=P: eng.activation(P.ap, P.ap, AF.Exp),
                       reads=[P], writes=[P])
                ctx.op("v", lambda eng, P=P: eng.tensor_add(
                    gm[:, 0:512], bass.AP(P.h, 0, [[1024, 128], [2, 512]]),
                    bass.AP(P.h, 1, [[1024, 128], [2, 512]])),
                    reads=[P], writes=[gm])
                ctx.op("v", lambda eng: eng.tensor_add(
                    gm[:, 0:256], bass.AP(gm.h, 0, [[512, 128], [2, 256]]),
                    bass.AP(gm.h, 1, [[512, 128], [2, 256]])),
                    reads=[gm], writes=[gm])
                ctx.op("v", lambda eng: eng.tensor_add(
                    gz[:, 0:128], bass.AP(gm.h, 0, [[512, 128], [2, 128]]),
                    bass.AP(gm.h, 1, [[512, 128], [2, 128]])),
                    reads=[gm], writes=[gz])
                ctx.op("v", lambda eng: eng.reciprocal(gz[:, 0:128], gz[:, 0:128]),
                       reads=[gz], writes=[gz])
                ctx.op("v", lambda eng, P=P: eng.tensor_mul(
                    P.ap, P.ap,
                    bass.AP(gz.h, 0, [[128, 128], [1, 128], [0, 8]])),
                    reads=[P, gz], writes=[P])
            ctx.fence()
            for c in range(NC):
                ctx.op("v", lambda eng, c=c: eng.memset(out1[c].ap, 0.0),
                       writes=[out1[c]])
            for bh in range(2):
                for s in range(S8):
                    for t in range(S8):
                        dist = abs(s - t)
                        ctx.op("v", lambda eng, bh=bh, s=s, t=t: eng.tensor_copy(
                            scr16.ap,
                            bass.AP(Pb[bh].h, s * 128 + t, [[1024, 128], [8, 16]])),
                            reads=[Pb[bh]], writes=[scr16])
                        pt = sb.ps[5]
                        ctx.op("t", lambda eng: eng.transpose(
                            pt[0:16, 0:128], scr16.ap, ident.ap),
                            reads=[scr16, ident], writes=[pt])
                        ctx.op("v", lambda eng, pt=pt: eng.tensor_copy(
                            pt16[0:16, 0:128], pt[0:16, 0:128]),
                            reads=[pt], writes=[pt16])
                        for c in range(NC):
                            bc = sb.ps[6]
                            ctx.mm(bc[:, 0:128], bc,
                                   [(selt[:, c * 128:(c + 1) * 128],
                                     pt16[0:16, 0:128], [selt, pt16])])
                            vsl = vc[c][:, t * B256 + bh * 128:
                                        t * B256 + (bh + 1) * 128]
                            ctx.op("v", lambda eng, vsl=vsl, c=c, dist=dist:
                                   eng.tensor_scalar(
                                       vpl[:, 0:128], vsl,
                                       vdt[:, dist:dist + 1], None, ALU.add),
                                   reads=[vc[c], vdt], writes=[vpl])
                            ctx.op("v", lambda eng, bc=bc: eng.tensor_mul(
                                tmp1[:, 0:128], vpl[:, 0:128], bc[:, 0:128]),
                                reads=[vpl, bc], writes=[tmp1])
                            osl = slice(s * B256 + bh * 128,
                                        s * B256 + (bh + 1) * 128)
                            ctx.op("v", lambda eng, c=c, osl=osl: eng.tensor_add(
                                out1[c][:, osl], out1[c][:, osl], tmp1[:, 0:128]),
                                reads=[out1[c], tmp1], writes=[out1[c]])
            ctx.fence()
            for c in range(NC):
                ctx.op("v", lambda eng, c=c: eng.tensor_copy(ob[c].ap, out1[c].ap),
                       reads=[out1[c]], writes=[ob[c]])
            ctx.fence()
            for tt in range(NT):
                sl = slice(tt * 512, (tt + 1) * 512)
                for oc in range(NC):
                    slab = stream_w_slab(ctx, sb, woT, oc % 2, oc * 128, 128)
                    ps = sb.ps[oc % 4]
                    ctx.mm(ps.ap, ps,
                           [(slab[:, k * 128:(k + 1) * 128], ob[k][:, sl],
                             [slab, ob[k]]) for k in range(NC)])
                    xld(ctx, xr, oc, col0=tt * 512, ncols=512, dcol0=0,
                        q=QS_RR[2 + oc % 2])
                    ctx.op("v", lambda eng, oc=oc, ps=ps: eng.tensor_scalar(
                        xnew[oc].ap, ps.ap, 1.0, sb.vcol(boc + oc), ALU.mult,
                        ALU.add),
                        reads=[ps, sb.vecs], writes=[xnew[oc]])
                    ctx.op("v", lambda eng, oc=oc: eng.tensor_add(
                        xnew[oc].ap, xnew[oc].ap, xr.ap),
                        reads=[xnew[oc], xr], writes=[xnew[oc]])
                emit_ln_fm(ctx, sb, xnew, xnew, 1e-5, Tl=512)
                for oc in range(NC):
                    xst(ctx, xnew[oc], oc, col0=tt * 512, ncols=512, scol0=0)


def build_l2_attn():
    nc = bass.Bass("TRN2", target_bir_lowering=False)
    aps = declare_l2_attn(nc)
    xin = nc.dram_tensor("xin", [D, T], F32, kind="ExternalInput").ap()
    xout = nc.dram_tensor("xout", [D, T], F32, kind="ExternalOutput").ap()
    with contextlib.ExitStack() as stack:
        ctx = Ctx(nc)
        ctx.make_sems(stack)
        with nc.Block() as block:
            emit_l2_attn(nc, ctx, stack, aps, plain_ld(xin), plain_st(xout))
            ctx.finish()
            ctx.build(block)
    return nc


def _weights_l2_attn(ins):
    bho = np.zeros((128, 8 * 16), dtype=np.float32)
    sel = np.zeros((16, 8 * 128), dtype=np.float32)
    for c in range(8):
        bho[0:64, c * 16 + 2 * c] = 1.0
        bho[64:128, c * 16 + 2 * c + 1] = 1.0
        sel[2 * c, c * 128:c * 128 + 64] = 1.0
        sel[2 * c + 1, c * 128 + 64:(c + 1) * 128] = 1.0
    return {
        "winT": _f32c(np.asarray(ins["L2_attn_W_in"]).T),
        "b_in": _f32c(ins["L2_attn_b_in"]),
        "kde": _f32c(ins["L2_attn_k_dist_emb"]),
        "vde": _f32c(ins["L2_attn_v_dist_emb"]),
        "woT": _f32c(np.asarray(ins["L2_attn_W_out"]).T),
        "b_out": _f32c(ins["L2_attn_b_out"]),
        "bho": bho, "sel": sel, "idin": np.eye(128, dtype=np.float32),
    }


def run_l2_attn(x_fm_cores, ins):
    return _run_block("l2_attn", build_l2_attn, x_fm_cores, ins)


_BUILT = {}


def _get(name, builder):
    if name not in _BUILT:
        _BUILT[name] = builder()
    return _BUILT[name]


def _f32c(a):
    return np.ascontiguousarray(a, dtype=np.float32)


def pack_lhsT(Wt):
    K, M = Wt.shape
    nk, nm = K // 128, M // 128
    out = Wt.reshape(nk, 128, nm, 128).transpose(2, 1, 0, 3).reshape(nm, 128, nk * 128)
    return np.ascontiguousarray(out, dtype=np.float32)


def _weights_l0_attn(ins):
    base = {}
    for short, full in [("Wg", "Wg"), ("Wi", "Wi"), ("Wfg", "Wfg"), ("Wfu", "Wfu"),
                        ("Wfd", "Wfd"), ("Wo", "Wo"), ("bg", "bg"), ("bi", "bi"),
                        ("A", "A"), ("Bp", "Bp"), ("Cp", "Cp"), ("Dp", "Dp"),
                        ("gamma", "gamma"), ("beta", "beta"), ("bfg", "bfg"),
                        ("bfu", "bfu"), ("bfd", "bfd"), ("bo", "bo")]:
        base[short] = _f32c(ins["L0_attn_" + full])
    return base


def _weights_l0_ffn(ins):
    return {
        "W1T": _f32c(np.asarray(ins["L0_ffn_W1"]).T),
        "W2T": _f32c(np.asarray(ins["L0_ffn_W2"]).T),
        "b1": _f32c(ins["L0_ffn_b1"]), "b2": _f32c(ins["L0_ffn_b2"]),
    }


IN_MAPS = {"l0_attn": _weights_l0_attn, "l0_ffn": _weights_l0_ffn,
           "l1_attn": _weights_l1_attn, "l1_ffn": _weights_l1_ffn,
           "l2_attn": _weights_l2_attn,
           "l2_ffn": _weights_l2_ffn, "l3_attn": _weights_l3_attn,
           "l3_ffn": _weights_l3_ffn}


def _run_block(name, builder, x_fm_cores, ins, wkey=None):
    nc = _get(name, builder)
    base = IN_MAPS[wkey or name](ins)
    in_maps = [dict(base, xin=_f32c(x_fm_cores[c])) for c in range(NCORES)]
    res = run_bass_kernel_spmd(nc, in_maps, core_ids=list(range(NCORES)))
    return [r["xout"] for r in res.results]


def run_l0_attn(x_fm_cores, ins):
    return _run_block("l0_attn", build_l0_attn, x_fm_cores, ins)


def _to_l2_layout(x_fm):
    F = np.stack([np.asarray(f).T for f in x_fm], axis=0)
    out = []
    for c in range(NCORES):
        sl = F[:, c * 256:(c + 1) * 256, :]
        out.append(np.ascontiguousarray(sl.reshape(8 * 256, D).T))
    return out


def _from_l2_layout(cores):
    F = np.zeros((NCORES, T, D), dtype=np.float32)
    for c in range(NCORES):
        sl = np.asarray(cores[c]).T.reshape(8, 256, D)
        F[:, c * 256:(c + 1) * 256, :] = sl
    return [np.ascontiguousarray(F[b].T) for b in range(NCORES)]



FUSE_SPECS = [
    ("l0a_", declare_l0_attn, emit_l0_attn, "_weights_l0_attn"),
    ("l0f_", declare_l0_ffn, emit_l0_ffn, "_weights_l0_ffn"),
    ("l1a_", declare_l1_attn, emit_l1_attn, "_weights_l1_attn"),
    ("l1f_", declare_l1_ffn, emit_l1_ffn, "_weights_l1_ffn"),
    ("l2a_", declare_l2_attn, emit_l2_attn, "_weights_l2_attn"),
    ("l2f_", declare_l2_ffn, emit_l2_ffn, "_weights_l2_ffn"),
    ("l3a_", declare_l3_attn, emit_l3_attn, "_weights_l3_attn"),
    ("l3f_", declare_l3_ffn, emit_l3_ffn, "_weights_l3_ffn"),
]


def build_fused(nb=8):
    nc = bass.Bass("TRN2", target_bir_lowering=False)
    xin0 = nc.dram_tensor("xin", [D, T], F32, kind="ExternalInput").ap()
    bnd = []
    for i in range(nb):
        last = (i == nb - 1)
        kind = "ExternalOutput" if last else "Internal"
        name = "xout" if last else f"hb{i}"
        shape = [8 * D, 256] if i in (3, 5) else [D, T]
        bnd.append(nc.dram_tensor(name, shape, F32, kind=kind).ap())
    a2a1_out = (nc.dram_tensor("a2a1_out", [8 * D, 256], F32, kind="Internal").ap()
                if nb > 4 else None)
    a2a2_out = (nc.dram_tensor("a2a2_out", [8 * D, 256], F32, kind="Internal").ap()
                if nb > 6 else None)
    with contextlib.ExitStack() as stack:
        ctx = Ctx(nc)
        ctx.make_sems(stack)
        cc1 = stack.enter_context(nc.semaphore("cc1"))
        cc2 = stack.enter_context(nc.semaphore("cc2"))
        tiny = ctx.sbuf(stack, "cc_tiny", [1, 8], F32)
        with nc.Block() as block:
            for i in range(nb):
                p, dec, emit, _w = FUSE_SPECS[i]
                aps = dec(nc, p)
                if i == 0:
                    ld = plain_ld(xin0)
                elif i == 4:
                    ld = a2a_ld(a2a1_out, "a2a1_out")
                elif i == 6:
                    ld = a2a_ld(a2a2_out, "a2a2_out")
                else:
                    ld = plain_ld(bnd[i - 1])
                mk_st = a2a_st if i in (3, 5) else plain_st
                st = mk_st(bnd[i], f"st{i}")
                ctx.prefix = p
                with contextlib.ExitStack() as bs:
                    emit(nc, ctx, bs, aps, ld, st)
                ctx.prefix = ""
                ctx.fence()
                if i == 3 and nb > 4:
                    emit_a2a(ctx, cc1, bnd[3], a2a1_out, "st3", "a2a1_out", tiny)
                if i == 5 and nb > 6:
                    emit_a2a(ctx, cc2, bnd[5], a2a2_out, "st5", "a2a2_out", tiny)
            ctx.finish()
            ctx.build(block)
    return nc


def fused_in_maps(inputs, x):
    base = {}
    for p, _dec, _emit, wname in FUSE_SPECS:
        for k, v in globals()[wname](inputs).items():
            base[p + k] = v
    return [dict(base, xin=np.ascontiguousarray(x[c].T)) for c in range(NCORES)]


def kernel(**inputs):
    x = np.asarray(inputs["L0_attn_x"], dtype=np.float32)
    nc = _get("fused", build_fused)
    in_maps = fused_in_maps(inputs, x)
    res = run_bass_kernel_spmd(nc, in_maps, core_ids=list(range(NCORES)))
    return np.stack([np.ascontiguousarray(np.asarray(r["xout"]).T)
                     for r in res.results], axis=0)
